# version 14
# baseline (speedup 1.0000x reference)
import os
import contextlib
import numpy as np
import concourse.bass as bass
import concourse.mybir as mybir
from concourse.bass_utils import run_bass_kernel_spmd

F32 = mybir.dt.float32
F32R = mybir.dt.float32r
BF16 = mybir.dt.bfloat16
AF = mybir.ActivationFunctionType
OP = mybir.AluOpType
AX = mybir.AxisListType.X

D = 1024
DC = 8
S = 4096
OWN = 2048
PAIRS = 8
DFF = 4096
MEG = 1024
NMEG = 4
EPS = 1e-5
EPS2 = 1e-3
EPSQ = 1e-5

ENGS = ("sy", "sc", "ve", "te", "gp")
BLKA = {"sy": "sync", "sc": "scalar", "ve": "vector", "te": "tensor", "gp": "gpsimd"}


class Buf:
    __slots__ = ("w", "r")

    def __init__(self):
        self.w = None
        self.r = {}


class Prog:
    def __init__(self, nc):
        self.nc = nc
        self.ops = {e: [] for e in ENGS}
        self.cnt = {e: 0 for e in ENGS}
        self.pend = {e: [] for e in ENGS}
        self.waited = {e: {} for e in ENGS}

    def _wait(self, eng, sem, val):
        if val <= 0:
            return
        w = self.waited[eng]
        if w.get(sem, 0) >= val:
            return
        w[sem] = val
        self.pend[eng].append((sem, val))

    def op(self, eng, fn, reads=(), writes=(), prewrites=(), dma=False, quiet=False):
        for b in reads:
            if b.w is not None and b.w[0] != eng:
                self._wait(eng, b.w[0], b.w[1])
            elif b.w is not None:
                self._wait(eng, b.w[0], b.w[1])
        for b in prewrites:
            if b.w is not None:
                self._wait(eng, b.w[0], b.w[1])
            for re_, rv in b.r.items():
                self._wait(eng, re_, rv)
        for b in writes:
            if b.w is not None:
                self._wait(eng, b.w[0], b.w[1])
            for re_, rv in b.r.items():
                if re_ != eng or True:
                    self._wait(eng, re_, rv)
        amount = 0 if quiet else (16 if dma else 1)
        waits, self.pend[eng] = self.pend[eng], []
        self.ops[eng].append((fn, waits, amount))
        if amount:
            self.cnt[eng] += amount
        if not quiet:
            c = self.cnt[eng]
            for b in reads:
                b.r[eng] = c
            for b in writes:
                b.w = (eng, c)
                b.r = {}

    def group(self, eng, fns, reads=(), writes=()):
        self.op(eng, fns[0], reads=reads, prewrites=writes, quiet=True)
        for f in fns[1:-1]:
            self.op(eng, f, quiet=True)
        self.op(eng, fns[-1], reads=reads, writes=writes)

    def barrier(self):
        for e in ENGS:
            for o in ENGS:
                if o != e:
                    self._wait(e, o, self.cnt[o])

    def emit(self, es):
        nc = self.nc
        sems = {e: es.enter_context(nc.semaphore("sem_" + e)) for e in ENGS}
        with nc.Block() as blk:
            for e in ENGS:
                if not self.ops[e]:
                    continue

                def body(h, e=e):
                    for fn, waits, amount in self.ops[e]:
                        for s, v in waits:
                            h.wait_ge(sems[s], v)
                        inst = fn(h)
                        if amount:
                            inst.then_inc(sems[e], amount)

                getattr(blk, BLKA[e])(body)


def build():
    nc = bass.Bass("TRN2", target_bir_lowering=False)
    x = nc.dram_tensor("x", [S, D], F32, kind="ExternalInput").ap()
    wq = nc.dram_tensor("wq", [D, D], F32, kind="ExternalInput").ap()
    wk = nc.dram_tensor("wk", [D, D], F32, kind="ExternalInput").ap()
    wv = nc.dram_tensor("wv", [D, D], F32, kind="ExternalInput").ap()
    wo = nc.dram_tensor("wo", [D, D], F32, kind="ExternalInput").ap()
    bqt = nc.dram_tensor("bqt", [128, 8], F32, kind="ExternalInput").ap()
    bkt = nc.dram_tensor("bkt", [128, 8], F32, kind="ExternalInput").ap()
    bv_v = nc.dram_tensor("bv_v", [D], F32, kind="ExternalInput").ap()
    bo_v = nc.dram_tensor("bo_v", [D], F32, kind="ExternalInput").ap()
    g2_v = nc.dram_tensor("g2_v", [D], F32, kind="ExternalInput").ap()
    be2_v = nc.dram_tensor("be2_v", [D], F32, kind="ExternalInput").ap()
    b2_v = nc.dram_tensor("b2_v", [D], F32, kind="ExternalInput").ap()
    b1t = nc.dram_tensor("b1t", [128, 32], F32, kind="ExternalInput").ap()
    w1 = nc.dram_tensor("w1", [DFF, D], F32, kind="ExternalInput").ap()
    w2 = nc.dram_tensor("w2", [D, DFF], F32, kind="ExternalInput").ap()
    y = nc.dram_tensor("y", [OWN, D], F32, kind="ExternalOutput").ap()
    DBG = "ExternalOutput" if int(os.environ.get("KDBG", "0")) else "Internal"
    q_h = nc.dram_tensor("q_h", [D, OWN], F32, kind=DBG).ap()
    k_h = nc.dram_tensor("k_h", [D, S], F32, kind=DBG).ap()
    v_h = nc.dram_tensor("v_h", [S, D], F32, kind=DBG).ap()
    ctx_h = nc.dram_tensor("ctx_h", [D, OWN], F32, kind=DBG).ap()
    x1_h = nc.dram_tensor("x1_h", [OWN, D], F32, kind=DBG).ap()
    w1qt_h = nc.dram_tensor("w1qt_h", [D, DFF], BF16, kind="Internal").ap()
    w2qt_h = nc.dram_tensor("w2qt_h", [DFF, D], BF16, kind="Internal").ap()
    bc_h = nc.dram_tensor("bc_h", [8, 512], F32, kind="Internal").ap()

    es = contextlib.ExitStack()
    pr = Prog(nc)
    PH = int(os.environ.get("KPHASES", "5"))
    P3L = int(os.environ.get("KP3LVL", "8"))
    P3P = int(os.environ.get("KP3PAIRS", "8"))
    P3I = int(os.environ.get("KP3ICS", "4"))
    P3J = int(os.environ.get("KP3JS", "32"))
    NOCB = int(os.environ.get("KP3NOCB", "0"))
    NOVP = int(os.environ.get("KP3NOVP", "0"))

    def sb(n, shape, dt, stack=None):
        return (stack or es).enter_context(nc.sbuf_tensor(n, shape, dt))

    def ps(n, shape, dt, stack=None):
        return (stack or es).enter_context(nc.psum_tensor(n, shape, dt))

    B = {}

    def nb(*names):
        for n in names:
            B[n] = Buf()

    iot = sb("iot", [128, 128], F32)
    pidx = sb("pidx", [128, 1], F32)
    identf = sb("identf", [128, 128], F32)
    identr = sb("identr", [128, 128], BF16)
    ones1 = sb("ones1", [128, 1], F32)
    onesr = sb("onesr", [1, 128], F32)
    bqs = sb("bqs", [128, 8], F32)
    bks = sb("bks", [128, 8], F32)
    b1s = sb("b1s", [128, 32], F32)
    mu1a = sb("mu1a", [128, 16], F32)
    r1a = sb("r1a", [128, 16], F32)
    sqdump = sb("sqdump", [128, 1024], F32)
    thrb = sb("thrb", [128, 4], F32)
    asum = sb("asum", [128, 64], F32)
    gsc = sb("gsc", [1, 8], F32)
    st0 = [sb(f"st0_{i}", [128, 1], F32) for i in range(5)]
    st1 = [sb(f"st1_{i}", [128, 1], F32) for i in range(5)]
    st2 = [sb(f"st2_{i}", [128, 1], F32) for i in range(5)]
    st3 = [sb(f"st3_{i}", [128, 1], F32) for i in range(5)]
    epsb = sb("epsb", [128, 2], F32)
    nb("iot", "pidx", "identf", "identr", "ones1", "bvb", "bob", "g2b", "be2b",
       "b2b", "bqs", "bks", "b1s", "mu1a", "r1a", "sqdump", "thrb", "asum",
       "gsc", "st0", "st1", "st2", "st3", "epsb", "onesr", "q_h", "k_h", "v_h", "ctx_h", "x1_h", "w1qt_h",
       "w2qt_h", "y")
    for i in range(8):
        nb(f"bch{i}")

    pr.op("gp", lambda g: g.iota(iot[:, :], [[1, 128]], base=0, channel_multiplier=0,
                                 allow_small_or_imprecise_dtypes=True), writes=[B["iot"]])
    pr.op("gp", lambda g: g.iota(pidx[:, :], [[0, 1]], base=0, channel_multiplier=1,
                                 allow_small_or_imprecise_dtypes=True), writes=[B["pidx"]])
    pr.op("ve", lambda v: v.tensor_scalar(identf[:, :], iot[:, :], pidx[:, 0:1], None, OP.is_equal),
          reads=[B["iot"], B["pidx"]], writes=[B["identf"]])
    pr.op("ve", lambda v: v.tensor_scalar(identr[:, :], iot[:, :], pidx[:, 0:1], None, OP.is_equal),
          reads=[B["iot"], B["pidx"]], writes=[B["identr"]])
    pr.op("ve", lambda v: v.memset(ones1[:, :], 1.0), writes=[B["ones1"]])
    pr.op("ve", lambda v: v.memset(onesr[0:1, :], 1.0), writes=[B["onesr"]])
    pr.op("ve", lambda v: v.memset(epsb[:, 0:1], EPS), writes=[B["epsb"]])
    pr.op("ve", lambda v: v.memset(epsb[:, 1:2], EPS2), writes=[B["epsb"]])
    def bcast_load(nm, buf, vec):
        pr.op("sy", lambda s, buf=buf, vec=vec: s.dma_start(
            out=buf[:, :], in_=bass.AP(vec.tensor, 0, [[0, 128], [1, D]])),
            writes=[B[nm]], dma=True)

    pr.op("sy", lambda s: s.dma_start(out=bqs[:, :], in_=bqt[:, :]), writes=[B["bqs"]], dma=True)
    pr.op("sy", lambda s: s.dma_start(out=bks[:, :], in_=bkt[:, :]), writes=[B["bks"]], dma=True)
    pr.op("sy", lambda s: s.dma_start(out=b1s[:, :], in_=b1t[:, :]), writes=[B["b1s"]], dma=True)

    w1b = [sb(f"p1w{i}", [128, 1024], F32) for i in range(2)]
    nb("p1w0", "p1w1")

    def p1_batch(idx):
        for t in range(4 * idx, 4 * idx + 4):
            par = t % 2
            wb, wbB = w1b[par], B[f"p1w{par}"]
            if t < 32:
                src = w1[t * 128:(t + 1) * 128, :]
            else:
                t2 = t - 32
                r, cc = t2 // 4, t2 % 4
                src = w2[r * 128:(r + 1) * 128, cc * 1024:(cc + 1) * 1024]
            pr.op("gp", lambda g, wb=wb, src=src: g.dma_start(out=wb[:, :], in_=src),
                  writes=[wbB], dma=True)
            pr.op("sc", lambda s, wb=wb, t=t: s.activation(
                sqdump[:, :], wb[:, :], AF.Abs, accum_out=asum[:, t:t + 1]),
                reads=[wbB], writes=[B["sqdump"], B["asum"]])

    def ln_stats(xb, xbB, stt, sttB, eps, save=None):
        ssq, ssum, mu, var, rstd = stt
        pr.op("sc", lambda s: s.activation(sqdump[:, :], xb[:, :], AF.Square,
                                           accum_out=ssq[:, 0:1]),
              reads=[xbB], writes=[B["sqdump"], sttB])
        pr.op("ve", lambda v: v.reduce_sum(ssum[:, 0:1], xb[:, :], axis=AX),
              reads=[xbB], writes=[sttB])
        pr.op("ve", lambda v: v.tensor_scalar_mul(mu[:, 0:1], ssum[:, 0:1], 1.0 / D),
              reads=[sttB], writes=[sttB])
        pr.op("ve", lambda v: v.tensor_scalar_mul(var[:, 0:1], ssq[:, 0:1], 1.0 / D),
              reads=[sttB], writes=[sttB])
        pr.op("ve", lambda v: v.tensor_mul(ssum[:, 0:1], mu[:, 0:1], mu[:, 0:1]),
              reads=[sttB], writes=[sttB])
        pr.op("ve", lambda v: v.tensor_sub(var[:, 0:1], var[:, 0:1], ssum[:, 0:1]),
              reads=[sttB], writes=[sttB])
        ecol = 0 if eps == EPS else 1
        pr.op("sc", lambda s, ecol=ecol: s.activation(ssum[:, 0:1], var[:, 0:1], AF.Sqrt,
                                                      bias=epsb[:, ecol:ecol + 1]),
              reads=[sttB, B["epsb"]], writes=[sttB])
        pr.op("ve", lambda v: v.reciprocal(rstd[:, 0:1], ssum[:, 0:1]),
              reads=[sttB], writes=[sttB])
        if save is not None:
            sv, svB, col = save
            pr.op("ve", lambda v: v.tensor_copy(sv[0][:, col:col + 1], mu[:, 0:1]),
                  reads=[sttB], writes=[svB])
            pr.op("ve", lambda v: v.tensor_copy(sv[1][:, col:col + 1], rstd[:, 0:1]),
                  reads=[sttB], writes=[svB])

    p2 = contextlib.ExitStack()
    bvb = sb("bvb", [128, D], F32, p2)
    bcast_load("bvb", bvb, bv_v)
    xt2 = [sb(f"p2x{i}", [128, 1024], F32, p2) for i in range(2)]
    ut2l = [sb(f"p2u{i}", [128, 1024], F32, p2) for i in range(2)]
    uTh = [sb(f"p2uth{c}", [128, MEG], BF16, p2) for c in range(DC)]
    uTl = [sb(f"p2utl{c}", [128, MEG], BF16, p2) for c in range(DC)]
    wraw = [sb(f"p2wr{i}", [128, 1024], F32, p2) for i in range(2)]
    wh = [sb(f"p2wh{c}", [128, 1024], BF16, p2) for c in range(DC)]
    wl = [sb(f"p2wl{c}", [128, 1024], BF16, p2) for c in range(DC)]
    qev = [sb(f"p2qe{i}", [128, 512], F32, p2) for i in range(2)]
    ps_t = [ps(f"p2pt{i}", [128, 128], F32, p2) for i in range(4)]
    ps_qk = [ps(f"p2pq{i}", [128, 512], F32, p2) for i in range(2)]
    ps_v = [ps(f"p2pv{i}", [128, 512], F32, p2) for i in range(2)]
    nb("p2x0", "p2x1", "p2u0", "p2u1", "p2qe0", "p2qe1", "p2pt0", "p2pt1", "p2pt2", "p2pt3",
       "p2pq0", "p2pq1", "p2pv0", "p2pv1", "p2wr0", "p2wr1")
    for c in range(DC):
        nb(f"p2uth{c}", f"p2utl{c}", f"p2wh{c}", f"p2wl{c}")

    for M in range(NMEG):
        p1_batch(M * 4 + 0)
        for ts in range(8):
            t = M * 8 + ts
            par = ts % 2
            xb, xbB = xt2[par], B[f"p2x{par}"]
            pr.op("sy", lambda s, xb=xb, t=t: s.dma_start(
                out=xb[:, :], in_=x[t * 128:(t + 1) * 128, :]), writes=[xbB], dma=True)
            stt, sttB = [(st0, B["st0"]), (st1, B["st1"]), (st2, B["st2"]),
                         (st3, B["st3"])][ts % 4]
            save = ((mu1a, r1a), B["mu1a"], t) if t < 16 else None
            ln_stats(xb, xbB, stt, sttB, EPS, save=save)
            ut2, utB = ut2l[ts % 2], B[f"p2u{ts % 2}"]
            pr.op("ve", lambda v, ut2=ut2, xb=xb, stt=stt: v.tensor_scalar(
                ut2[:, :], xb[:, :], stt[2][:, 0:1], stt[4][:, 0:1], OP.subtract, OP.mult),
                reads=[xbB, sttB], writes=[utB])
            for c in range(DC):
                pp = (ts * DC + c) % 4
                pr.op("te", lambda te, ut2=ut2, c=c, pp=pp: te.transpose(
                    ps_t[pp][:, :], ut2[:, c * 128:(c + 1) * 128], identf[:, :]),
                    reads=[utB, B["identf"]], writes=[B[f"p2pt{pp}"]])
                pr.op("sc", lambda s, c=c, ts=ts, pp=pp: s.activation(
                    uTh[c][:, ts * 128:(ts + 1) * 128], ps_t[pp][:, :], AF.Copy),
                    reads=[B[f"p2pt{pp}"]], writes=[B[f"p2uth{c}"]])
                pr.op("ve", lambda v, c=c, ts=ts, pp=pp: v.tensor_tensor(
                    uTl[c][:, ts * 128:(ts + 1) * 128], ps_t[pp][:, :],
                    uTh[c][:, ts * 128:(ts + 1) * 128], OP.subtract),
                    reads=[B[f"p2pt{pp}"], B[f"p2uth{c}"]], writes=[B[f"p2utl{c}"]])
        p1_batch(M * 4 + 1)
        wlist = [("q", wq, q_h, bqs, B["bqs"]), ("k", wk, k_h, bks, B["bks"]),
                 ("v", wv, None, None, None)]
        if M >= 2:
            wlist = wlist[1:]
        for wi, (kind, wsrc, dsth, bias, biasB) in enumerate(wlist):
            for c in range(DC):
                par = c % 2
                wb, wbB = wraw[par], B[f"p2wr{par}"]
                pr.op("sy", lambda s, wb=wb, wsrc=wsrc, c=c: s.dma_start(
                    out=wb[:, :], in_=wsrc[c * 128:(c + 1) * 128, :]), writes=[wbB], dma=True)
                pr.op("ve", lambda v, c=c, wb=wb: v.tensor_copy(wh[c][:, :], wb[:, :]),
                      reads=[wbB], writes=[B[f"p2wh{c}"]])
                pr.op("ve", lambda v, c=c, wb=wb: v.tensor_tensor(
                    wl[c][:, :], wb[:, :], wh[c][:, :], OP.subtract),
                    reads=[wbB, B[f"p2wh{c}"]], writes=[B[f"p2wl{c}"]])
            wrb = [B[f"p2wh{c}"] for c in range(DC)] + [B[f"p2wl{c}"] for c in range(DC)]
            uTb = [B[f"p2uth{c}"] for c in range(DC)] + [B[f"p2utl{c}"] for c in range(DC)]
            if kind in ("q", "k"):
                for m in range(8):
                    for n in range(2):
                        par = (m * 2 + n) % 2
                        pb = B[f"p2pq{par}"]
                        fns = []
                        for c in range(DC):
                            for (a_, b_) in ((wh, uTh), (wh, uTl), (wl, uTh)):
                                st_ = (c == 0 and a_ is wh and b_ is uTh)
                                sp_ = (c == DC - 1 and a_ is wl)
                                fns.append(lambda te, c=c, a_=a_, b_=b_, m=m, n=n, par=par, st_=st_, sp_=sp_:
                                           te.matmul(ps_qk[par][:, :],
                                                     a_[c][:, m * 128:(m + 1) * 128],
                                                     b_[c][:, n * 512:(n + 1) * 512],
                                                     start=st_, stop=sp_))
                        pr.group("te", fns, reads=wrb + uTb, writes=[pb])
                        qpar = (m * 2 + n) % 2
                        qb, qbB = qev[qpar], B[f"p2qe{qpar}"]
                        pr.op("ve", lambda v, par=par, qb=qb, bias=bias, m=m: v.tensor_scalar(
                            qb[:, :], ps_qk[par][:, :], bias[:, m:m + 1], None, OP.add),
                            reads=[pb, biasB], writes=[qbB])
                        tok0 = M * MEG + n * 512
                        pr.op("gp", lambda g, qb=qb, dsth=dsth, m=m, tok0=tok0: g.dma_start(
                            out=dsth[m * 128:(m + 1) * 128, tok0:tok0 + 512], in_=qb[:, :]),
                            reads=[qbB], writes=[B["q_h" if dsth is q_h else "k_h"]], dma=True)
            else:
                for ts in range(8):
                    for n in range(2):
                        par = (ts * 2 + n) % 2
                        pb = B[f"p2pv{par}"]
                        fns = []
                        for c in range(DC):
                            for (a_, b_) in ((uTh, wh), (uTh, wl), (uTl, wh)):
                                st_ = (c == 0 and a_ is uTh and b_ is wh)
                                sp_ = (c == DC - 1 and a_ is uTl)
                                fns.append(lambda te, c=c, a_=a_, b_=b_, ts=ts, n=n, par=par, st_=st_, sp_=sp_:
                                           te.matmul(ps_v[par][:, :],
                                                     a_[c][:, ts * 128:(ts + 1) * 128],
                                                     b_[c][:, n * 512:(n + 1) * 512],
                                                     start=st_, stop=sp_))
                        pr.group("te", fns, reads=wrb + uTb, writes=[pb])
                        qpar = (ts * 2 + n) % 2
                        qb, qbB = qev[qpar], B[f"p2qe{qpar}"]
                        pr.op("ve", lambda v, par=par, qb=qb, n=n: v.tensor_tensor(
                            qb[:, :], ps_v[par][:, :], bvb[:, n * 512:(n + 1) * 512], OP.add),
                            reads=[pb, B["bvb"]], writes=[qbB])
                        tok0 = M * MEG + ts * 128
                        pr.op("gp", lambda g, qb=qb, ts=ts, n=n, tok0=tok0: g.dma_start(
                            out=v_h[tok0:tok0 + 128, n * 512:(n + 1) * 512], in_=qb[:, :]),
                            reads=[qbB], writes=[B["v_h"]], dma=True)
        p1_batch(M * 4 + 2)
        p1_batch(M * 4 + 3)

    pr.op("ve", lambda v: v.reduce_sum(st0[0][:, 0:1], asum[:, 0:32], axis=AX),
          reads=[B["asum"]], writes=[B["st0"]])
    pr.op("ve", lambda v: v.reduce_sum(st0[1][:, 0:1], asum[:, 32:64], axis=AX),
          reads=[B["asum"]], writes=[B["st0"]])
    pr.op("te", lambda te: te.matmul(ps_t[0][0:1, 0:1], ones1[:, 0:1], st0[0][:, 0:1],
                                     start=True, stop=True),
          reads=[B["st0"], B["ones1"]], writes=[B["p2pt0"]])
    pr.op("te", lambda te: te.matmul(ps_t[1][0:1, 0:1], ones1[:, 0:1], st0[1][:, 0:1],
                                     start=True, stop=True),
          reads=[B["st0"], B["ones1"]], writes=[B["p2pt1"]])
    pr.op("ve", lambda v: v.tensor_scalar(gsc[0:1, 0:1], ps_t[0][0:1, 0:1],
                                          1.0 / (DFF * D), EPSQ, OP.mult, OP.add),
          reads=[B["p2pt0"]], writes=[B["gsc"]])
    pr.op("ve", lambda v: v.tensor_scalar(gsc[0:1, 2:3], ps_t[1][0:1, 0:1],
                                          1.0 / (DFF * D), EPSQ, OP.mult, OP.add),
          reads=[B["p2pt1"]], writes=[B["gsc"]])
    pr.op("ve", lambda v: v.tensor_scalar_mul(gsc[0:1, 0:1], gsc[0:1, 0:1], 0.5),
          reads=[B["gsc"]], writes=[B["gsc"]])
    pr.op("ve", lambda v: v.tensor_scalar_mul(gsc[0:1, 2:3], gsc[0:1, 2:3], 0.5),
          reads=[B["gsc"]], writes=[B["gsc"]])
    pr.op("ve", lambda v: v.tensor_scalar_mul(gsc[0:1, 1:2], gsc[0:1, 0:1], -1.0),
          reads=[B["gsc"]], writes=[B["gsc"]])
    pr.op("ve", lambda v: v.tensor_scalar_mul(gsc[0:1, 3:4], gsc[0:1, 2:3], -1.0),
          reads=[B["gsc"]], writes=[B["gsc"]])
    pr.op("te", lambda te: te.matmul(ps_t[0][:, 0:4], onesr[0:1, :], gsc[0:1, 0:4],
                                     start=True, stop=True),
          reads=[B["gsc"], B["onesr"]], writes=[B["p2pt0"]])
    pr.op("ve", lambda v: v.tensor_copy(thrb[:, :], ps_t[0][:, 0:4]),
          reads=[B["p2pt0"]], writes=[B["thrb"]])

    pr.barrier()
    p2.close()

    if PH < 3:
        pr._wait("gp", "gp", pr.cnt["gp"])
        pr._wait("gp", "sy", pr.cnt["sy"])
        pr.op("gp", lambda g: g.memset(gsc[0:1, 4:5], 0.0), writes=[B["gsc"]])
        pr.emit(es)
        return nc, es
    p3 = contextlib.ExitStack()
    kraw = sb("p3kr", [128, S], F32, p3)
    khh = sb("p3kh", [128, S], BF16, p3)
    kll = sb("p3kl", [128, S], BF16, p3)
    qraw = sb("p3qr", [128, OWN], F32, p3)
    qhh = sb("p3qh", [128, OWN], BF16, p3)
    qll = sb("p3ql", [128, OWN], BF16, p3)
    vr3 = sb("p3vr", [128, 32, 128], F32, p3)
    vph = sb("p3vph", [128, 32, 130], BF16, p3)
    vpl = sb("p3vpl", [128, 32, 130], BF16, p3)
    pfb = [sb(f"p3pf{i}", [128, 1024], F32, p3) for i in range(3)]
    phb = [sb(f"p3ph{i}", [128, 1024], BF16, p3) for i in range(3)]
    plb = [sb(f"p3pl{i}", [128, 1024], BF16, p3) for i in range(3)]
    maxk = sb("p3mk", [128, 1], F32, p3)
    cst = sb("p3cs", [1, 16], F32, p3)
    biasA = sb("p3ba", [128, 1], F32, p3)
    biasB = sb("p3bb", [128, 1], F32, p3)
    rs_s = sb("p3rs", [1, 512], F32, p3)
    rs_t = sb("p3rt", [1, 512], F32, p3)
    rsb = [sb(f"p3rb{i}", [64, 512], F32, p3) for i in range(4)]
    cxe = [sb(f"p3cx{i}", [64, 512], F32, p3) for i in range(4)]
    psSA = [ps(f"p3ssa{i}", [128, 512], F32, p3) for i in range(2)]
    psSB = [ps(f"p3ssb{i}", [128, 512], F32, p3) for i in range(2)]
    psCA = [ps(f"p3ca{i}", [65, 512], F32, p3) for i in range(2)]
    psCB = [ps(f"p3cb{i}", [65, 512], F32, p3) for i in range(2)]
    nb("p3kr", "p3kh", "p3kl", "p3qr", "p3qh", "p3ql", "p3vr", "p3vph", "p3vpl",
       "p3mk", "p3cs", "p3ba", "p3bb", "p3rs", "p3rt",
       "p3pf0", "p3pf1", "p3pf2", "p3ph0", "p3ph1", "p3ph2", "p3pl0", "p3pl1", "p3pl2",
       "p3ssa0", "p3ssa1", "p3ssb0", "p3ssb1", "p3ca0", "p3ca1", "p3cb0", "p3cb1",
       "p3rb0", "p3rb1", "p3rb2", "p3rb3", "p3cx0", "p3cx1", "p3cx2", "p3cx3")

    for p in range(P3P):
        pr.op("sy", lambda s, p=p: s.dma_start(out=kraw[:, :],
                                               in_=k_h[p * 128:(p + 1) * 128, :]),
              reads=[B["k_h"]], writes=[B["p3kr"]], dma=True)
        pr.op("sy", lambda s, p=p: s.dma_start(out=qraw[:, :],
                                               in_=q_h[p * 128:(p + 1) * 128, :]),
              reads=[B["q_h"]], writes=[B["p3qr"]], dma=True)
        pr.op("sy", lambda s, p=p: s.dma_start(
            out=vr3[:, :, :], in_=bass.AP(v_h.tensor, p * 128,
                                          [[1024, 128], [131072, 32], [1, 128]])),
            reads=[B["v_h"]], writes=[B["p3vr"]], dma=True)
        pr.op("ve", lambda v: v.tensor_copy(khh[:, :], kraw[:, :]),
              reads=[B["p3kr"]], writes=[B["p3kh"]])
        pr.op("ve", lambda v: v.tensor_tensor(kll[:, :], kraw[:, :], khh[:, :],
                                              OP.subtract),
              reads=[B["p3kr"], B["p3kh"]], writes=[B["p3kl"]])
        pr.op("ve", lambda v: v.tensor_copy(qhh[:, :], qraw[:, :]),
              reads=[B["p3qr"]], writes=[B["p3qh"]])
        pr.op("ve", lambda v: v.tensor_tensor(qll[:, :], qraw[:, :], qhh[:, :],
                                              OP.subtract),
              reads=[B["p3qr"], B["p3qh"]], writes=[B["p3ql"]])
        pr.op("ve", lambda v: v.reduce_max(maxk[:, 0:1], kraw[:, :], axis=AX,
                                           apply_absolute_value=True),
              reads=[B["p3kr"]], writes=[B["p3mk"]])
        pr.op("sc", lambda s: s.activation(pfb[0][:, :], qraw[:, 0:1024], AF.Abs),
              reads=[B["p3qr"]], writes=[B["p3pf0"]])
        pr.op("sc", lambda s: s.activation(pfb[1][:, :], qraw[:, 1024:2048], AF.Abs),
              reads=[B["p3qr"]], writes=[B["p3pf1"]])
        if P3L < 2:
            continue
        if p == 0 and not NOVP:
            pr.op("ve", lambda v: v.memset(vph[:, :, :], 1.0), writes=[B["p3vph"]])
            pr.op("ve", lambda v: v.memset(vpl[:, :, :], 0.0), writes=[B["p3vpl"]])
        if NOVP:
            pr.op("ve", lambda v: v.tensor_copy(vph[:, 0, 0:64], vr3[:, 0, 0:64]),
                  reads=[B["p3vr"]], writes=[B["p3vph"]])
        if not NOVP:
            pr.op("ve", lambda v: v.tensor_copy(vph[:, :, 0:64], vr3[:, :, 0:64]),
              reads=[B["p3vr"]], writes=[B["p3vph"]])
            pr.op("ve", lambda v: v.tensor_copy(vph[:, :, 65:129], vr3[:, :, 64:128]),
                  reads=[B["p3vr"]], writes=[B["p3vph"]])
            pr.op("ve", lambda v: v.tensor_tensor(vpl[:, :, 0:64], vr3[:, :, 0:64],
                                                  vph[:, :, 0:64], OP.subtract),
                  reads=[B["p3vr"], B["p3vph"]], writes=[B["p3vpl"]])
            pr.op("ve", lambda v: v.tensor_tensor(vpl[:, :, 65:129], vr3[:, :, 64:128],
                                                  vph[:, :, 65:129], OP.subtract),
                  reads=[B["p3vr"], B["p3vph"]], writes=[B["p3vpl"]])
        if P3L < 3:
            continue
        if NOCB:
            pr.op("ve", lambda v: v.memset(biasA[:, 0:1], -3.0), writes=[B["p3ba"]])
            pr.op("ve", lambda v: v.memset(biasB[:, 0:1], -3.0), writes=[B["p3bb"]])
        for hh_ in range(0 if NOCB else 2):
            for ic in range(4):
                pf, pfB = (pfb[0], B["p3pf0"]) if ic < 2 else (pfb[1], B["p3pf1"])
                cslice = pf[hh_ * 64:hh_ * 64 + 64, (ic % 2) * 512:(ic % 2) * 512 + 512]
                pr.op("te", lambda te, hh_=hh_, cs_=cslice: te.matmul(
                    psSA[0][0:1, :], maxk[hh_ * 64:hh_ * 64 + 64, 0:1], cs_,
                    start=True, stop=True),
                    reads=[B["p3mk"], pfB], writes=[B["p3ssa0"]])
                pr.op("ve", lambda v, hh_=hh_, ic=ic: v.reduce_max(
                    cst[0:1, hh_ * 4 + ic:hh_ * 4 + ic + 1], psSA[0][0:1, :], axis=AX),
                    reads=[B["p3ssa0"]], writes=[B["p3cs"]])
        if not NOCB:
            pr.op("ve", lambda v: v.reduce_max(cst[0:1, 8:9], cst[0:1, 0:4], axis=AX),
                  reads=[B["p3cs"]], writes=[B["p3cs"]])
            pr.op("ve", lambda v: v.reduce_max(cst[0:1, 9:10], cst[0:1, 4:8], axis=AX),
                  reads=[B["p3cs"]], writes=[B["p3cs"]])
            pr.op("ve", lambda v: v.tensor_scalar(cst[0:1, 10:12], cst[0:1, 8:10],
                                                  -0.125 * 1.001, -0.01, OP.mult, OP.add),
                  reads=[B["p3cs"]], writes=[B["p3cs"]])
            pr.op("te", lambda te: te.matmul(psSA[0][:, 0:2], onesr[0:1, :], cst[0:1, 10:12],
                                             start=True, stop=True),
                  reads=[B["p3cs"], B["onesr"]], writes=[B["p3ssa0"]])
            pr.op("ve", lambda v: v.tensor_copy(biasA[:, 0:1], psSA[0][:, 0:1]),
                  reads=[B["p3ssa0"]], writes=[B["p3ba"]])
            pr.op("ve", lambda v: v.tensor_copy(biasB[:, 0:1], psSA[0][:, 1:2]),
                  reads=[B["p3ssa0"]], writes=[B["p3bb"]])

        if P3L < 4:
            continue
        kb = [B["p3kh"], B["p3kl"]]
        qb = [B["p3qh"], B["p3ql"]]
        vb = [B["p3vph"], B["p3vpl"]]
        evac_q = []
        for ic in range(P3I):
            cp = ic % 2
            pCA, pCB = B[f"p3ca{cp}"], B[f"p3cb{cp}"]
            def emit_pv(j):
                sp = j % 3
                pvb = [B[f"p3ph{sp}"], B[f"p3pl{sp}"]]
                fns = []
                for hh_, pC in ((0, psCA[cp]), (1, psCB[cp])):
                    c0 = hh_ * 65
                    for pi, (va_, pa_) in enumerate(((vph, phb[sp]), (vpl, phb[sp]), (vph, plb[sp]))):
                        st_ = (j == 0 and pi == 0)
                        sp_ = (j == P3J - 1 and pi == 2)
                        fns.append(lambda te, va_=va_, pa_=pa_, hh_=hh_, j=j, c0=c0, pC=pC, st_=st_, sp_=sp_:
                                   te.matmul(pC[:, :], va_[:, j, c0:c0 + 65],
                                             pa_[:, hh_ * 512:hh_ * 512 + 512],
                                             start=st_, stop=sp_))
                pr.group("te", fns, reads=vb + pvb, writes=[pCA, pCB])

            for j in range(P3J + 1):
                if j < P3J:
                    sp = j % 2
                    pSA, pSB = B[f"p3ssa{sp}"], B[f"p3ssb{sp}"]
                    i0 = ic * 512
                    fns = []
                    for (ka_, qa_) in ((khh, qhh), (khh, qll), (kll, qhh)):
                        for hh_ in range(2):
                            st_ = (ka_ is khh and qa_ is qhh)
                            sp_ = (ka_ is kll)
                            pst = (psSA if hh_ == 0 else psSB)[sp]
                            fns.append(lambda te, ka_=ka_, qa_=qa_, hh_=hh_, j=j, i0=i0, pst=pst, st_=st_, sp_=sp_:
                                       te.matmul(pst[:, :],
                                                 ka_[hh_ * 64:hh_ * 64 + 64, j * 128:(j + 1) * 128],
                                                 qa_[hh_ * 64:hh_ * 64 + 64, i0:i0 + 512],
                                                 start=st_, stop=sp_,
                                                 tile_position=(hh_ * 64, 0)))
                    pr.group("te", fns, reads=kb + qb, writes=[pSA, pSB])
                    pp3 = j % 3
                    if P3L >= 5:
                        pr.op("sc", lambda s, pp3=pp3, sp=sp: s.activation(
                            pfb[pp3][:, 0:512], psSA[sp][:, :], AF.Exp, scale=0.125,
                            bias=biasA[:, 0:1]),
                            reads=[pSA, B["p3ba"]], writes=[B[f"p3pf{pp3}"]])
                        pr.op("sc", lambda s, pp3=pp3, sp=sp: s.activation(
                            pfb[pp3][:, 512:1024], psSB[sp][:, :], AF.Exp, scale=0.125,
                            bias=biasB[:, 0:1]),
                            reads=[pSB, B["p3bb"]], writes=[B[f"p3pf{pp3}"]])
                    if P3L >= 6:
                        pr.op("ve", lambda v, pp3=pp3: v.tensor_copy(
                            phb[pp3][:, :], pfb[pp3][:, :]),
                            reads=[B[f"p3pf{pp3}"]], writes=[B[f"p3ph{pp3}"]])
                        pr.op("ve", lambda v, pp3=pp3: v.tensor_tensor(
                            plb[pp3][:, :], pfb[pp3][:, :], phb[pp3][:, :], OP.subtract),
                            reads=[B[f"p3pf{pp3}"], B[f"p3ph{pp3}"]], writes=[B[f"p3pl{pp3}"]])
                if P3L >= 7 and j >= 1:
                    emit_pv(j - 1)
                if j == 6 and evac_q:
                    evac_q.pop(0)()
            if P3L < 8:
                continue
            def emit_evac(cp=cp, pCA=pCA, pCB=pCB, ic=ic):
              for hh_, pC, pCB_ in ((0, psCA[cp], pCA), (1, psCB[cp], pCB)):
                  rp = cp * 2 + hh_
                  rb, rbB = rsb[rp], B[f"p3rb{rp}"]
                  cx, cxB = cxe[rp], B[f"p3cx{rp}"]
                  pr.op("ve", lambda v, pC=pC: v.tensor_scalar_add(rs_s[0:1, :], pC[64:65, :], 1e-30),
                        reads=[pCB_], writes=[B["p3rs"]])
                  pr.op("ve", lambda v: v.reciprocal(rs_t[0:1, :], rs_s[0:1, :]),
                        reads=[B["p3rs"]], writes=[B["p3rt"]])
                  pr.op("ve", lambda v: v.tensor_mul(rs_s[0:1, :], rs_s[0:1, :], rs_t[0:1, :]),
                        reads=[B["p3rs"], B["p3rt"]], writes=[B["p3rs"]])
                  pr.op("ve", lambda v: v.tensor_scalar(rs_s[0:1, :], rs_s[0:1, :], -1.0, 2.0,
                                                        OP.mult, OP.add),
                        reads=[B["p3rs"]], writes=[B["p3rs"]])
                  pr.op("ve", lambda v: v.tensor_mul(rs_t[0:1, :], rs_t[0:1, :], rs_s[0:1, :]),
                        reads=[B["p3rs"], B["p3rt"]], writes=[B["p3rt"]])
                  psr, psrB = (psSA, "p3ssa") if hh_ == 0 else (psSB, "p3ssb")
                  pr.op("te", lambda te, psr=psr: te.matmul(
                      psr[0][0:64, :], onesr[0:1, 0:64], rs_t[0:1, :], start=True, stop=True),
                      reads=[B["p3rt"], B["onesr"]], writes=[B[psrB + "0"]])
                  pr.op("ve", lambda v, rb=rb, psr=psr: v.tensor_copy(rb[:, :], psr[0][0:64, :]),
                        reads=[B[psrB + "0"]], writes=[rbB])
                  pr.op("ve", lambda v, cx=cx, pC=pC, rb=rb: v.tensor_tensor(
                      cx[:, :], pC[0:64, :], rb[:, :], OP.mult),
                      reads=[pCB_, rbB], writes=[cxB])
                  r0 = p * 128 + hh_ * 64
                  pr.op("gp", lambda g, cx=cx, r0=r0, ic=ic: g.dma_start(
                      out=ctx_h[r0:r0 + 64, ic * 512:(ic + 1) * 512], in_=cx[:, :]),
                      reads=[cxB], writes=[B["ctx_h"]], dma=True)
            evac_q.append(emit_evac)

        for fn_ in evac_q:
            fn_()
        evac_q = []
    pr.barrier()
    p3.close()

    if PH < 4:
        pr._wait("gp", "gp", pr.cnt["gp"])
        pr._wait("gp", "sy", pr.cnt["sy"])
        pr.op("gp", lambda g: g.memset(gsc[0:1, 4:5], 0.0), writes=[B["gsc"]])
        pr.emit(es)
        return nc, es
    p4 = contextlib.ExitStack()
    bob = sb("bob", [128, D], F32, p4)
    g2b = sb("g2b", [128, D], F32, p4)
    be2b = sb("be2b", [128, D], F32, p4)
    bcast_load("bob", bob, bo_v)
    bcast_load("g2b", g2b, g2_v)
    bcast_load("be2b", be2b, be2_v)
    ctxs = [sb(f"p4c{pp}", [128, OWN], F32, p4) for pp in range(PAIRS)]
    woh = [sb(f"p4wh{c}", [128, 1024], BF16, p4) for c in range(DC)]
    wol = [sb(f"p4wl{c}", [128, 1024], BF16, p4) for c in range(DC)]
    wraw4 = [sb(f"p4wr{i}", [128, 1024], F32, p4) for i in range(2)]
    chs = [sb(f"p4ch{i}", [128, 128], BF16, p4) for i in range(2)]
    cls = [sb(f"p4cl{i}", [128, 128], BF16, p4) for i in range(2)]
    xb4 = [sb(f"p4x{i}", [128, 1024], F32, p4) for i in range(2)]
    yb4 = sb("p4y", [128, 1024], F32, p4)
    zb4 = sb("p4z", [128, 1024], F32, p4)
    ps_ao = [ps(f"p4ao{i}", [128, 512], F32, p4) for i in range(4)]
    nb("p4y", "p4z", "p4wr0", "p4wr1", "p4ao0", "p4ao1", "p4ao2", "p4ao3", "p4x0", "p4x1",
       "p4ch0", "p4ch1", "p4cl0", "p4cl1")
    for pp in range(PAIRS):
        nb(f"p4c{pp}", f"p4wh{pp}", f"p4wl{pp}")

    p5a = contextlib.ExitStack()
    wr5 = [sb(f"p5wr{i}", [128, 1024], F32, p5a) for i in range(2)]
    tq5 = [sb(f"p5tq{i}", [128, 1024], BF16, p5a) for i in range(2)]
    sc5 = sb("p5sc", [128, 1024], F32, p5a)
    tevac = [sb(f"p5te{i}", [128, 128], BF16, p5a) for i in range(2)]
    ps_tr = [ps(f"p5tr{i}", [128, 128], BF16, p5a) for i in range(2)]
    nb("p5wr0", "p5wr1", "p5tq0", "p5tq1", "p5sc", "p5te0", "p5te1", "p5tr0", "p5tr1")

    def terntile(widx, fc):
        par = fc % 2
        wb, wbB = wr5[par], B[f"p5wr{par}"]
        tq, tqB = tq5[par], B[f"p5tq{par}"]
        tcol = 0 if widx == 0 else 2
        if widx == 0:
            pr.op("sy", lambda s, fc=fc: s.dma_start(out=wb[:, :],
                                                     in_=w1[fc * 128:(fc + 1) * 128, :]),
                  writes=[wbB], dma=True)
        else:
            for c in range(DC):
                pr.op("sy", lambda s, fc=fc, c=c: s.dma_start(
                    out=wb[:, c * 128:(c + 1) * 128],
                    in_=w2[c * 128:(c + 1) * 128, fc * 128:(fc + 1) * 128]),
                    writes=[wbB], dma=True)
        pr.op("ve", lambda v, wb=wb, tcol=tcol: v.tensor_scalar(
            sc5[:, :], wb[:, :], thrb[:, tcol:tcol + 1], None, OP.is_gt),
            reads=[wbB, B["thrb"]], writes=[B["p5sc"]])
        pr.op("ve", lambda v, wb=wb, tcol=tcol: v.tensor_scalar(
            wb[:, :], wb[:, :], thrb[:, tcol + 1:tcol + 2], None, OP.is_lt),
            reads=[wbB, B["thrb"]], writes=[wbB])
        pr.op("ve", lambda v, wb=wb, tq=tq: v.tensor_tensor(tq[:, :], sc5[:, :], wb[:, :],
                                                            OP.subtract),
            reads=[wbB, B["p5sc"]], writes=[tqB])
        for c in range(DC):
            tp = c % 2
            pr.op("te", lambda te, tq=tq, c=c, tp=tp: te.transpose(
                ps_tr[tp][:, :], tq[:, c * 128:(c + 1) * 128], identr[:, :]),
                reads=[tqB, B["identr"]], writes=[B[f"p5tr{tp}"]])
            pr.op("ve", lambda v, tp=tp: v.tensor_copy(tevac[tp][:, :], ps_tr[tp][:, :]),
                  reads=[B[f"p5tr{tp}"]], writes=[B[f"p5te{tp}"]])
            if widx == 0:
                dst = w1qt_h[c * 128:(c + 1) * 128, fc * 128:(fc + 1) * 128]
                dB = "w1qt_h"
            else:
                dst = w2qt_h[fc * 128:(fc + 1) * 128, c * 128:(c + 1) * 128]
                dB = "w2qt_h"
            pr.op("gp", lambda g, dst=dst, tp=tp: g.dma_start(out=dst, in_=tevac[tp][:, :]),
                  reads=[B[f"p5te{tp}"]], writes=[B[dB]], dma=True)


    for pp in range(PAIRS):
        pr.op("sy", lambda s, pp=pp: s.dma_start(out=ctxs[pp][:, :],
                                                 in_=ctx_h[pp * 128:(pp + 1) * 128, :]),
              reads=[B["ctx_h"]], writes=[B[f"p4c{pp}"]], dma=True)
        par = pp % 2
        wb, wbB = wraw4[par], B[f"p4wr{par}"]
        pr.op("sy", lambda s, wb=wb, pp=pp: s.dma_start(
            out=wb[:, :], in_=wo[pp * 128:(pp + 1) * 128, :]), writes=[wbB], dma=True)
        pr.op("ve", lambda v, pp=pp, wb=wb: v.tensor_copy(woh[pp][:, :], wb[:, :]),
              reads=[wbB], writes=[B[f"p4wh{pp}"]])
        pr.op("ve", lambda v, pp=pp, wb=wb: v.tensor_tensor(
            wol[pp][:, :], wb[:, :], woh[pp][:, :], OP.subtract),
            reads=[wbB, B[f"p4wh{pp}"]], writes=[B[f"p4wl{pp}"]])

    for t in range(16):
        ap_ = t % 2
        pA0, pA1 = B[f"p4ao{ap_ * 2}"], B[f"p4ao{ap_ * 2 + 1}"]
        for pp in range(PAIRS):
            cpar = pp % 2
            ch, chB = chs[cpar], B[f"p4ch{cpar}"]
            cl, clB = cls[cpar], B[f"p4cl{cpar}"]
            pr.op("ve", lambda v, ch=ch, pp=pp, t=t: v.tensor_copy(
                ch[:, :], ctxs[pp][:, t * 128:(t + 1) * 128]),
                reads=[B[f"p4c{pp}"]], writes=[chB])
            pr.op("ve", lambda v, cl=cl, ch=ch, pp=pp, t=t: v.tensor_tensor(
                cl[:, :], ctxs[pp][:, t * 128:(t + 1) * 128], ch[:, :],
                OP.subtract),
                reads=[B[f"p4c{pp}"], chB], writes=[clB])
            fns = []
            for n in range(2):
                for (ca_, wa_, wB_) in ((ch, woh[pp], B[f"p4wh{pp}"]),
                                        (ch, wol[pp], B[f"p4wl{pp}"]),
                                        (cl, woh[pp], B[f"p4wh{pp}"])):
                    st_ = (pp == 0 and ca_ is ch and wa_ is woh[pp])
                    sp_ = (pp == PAIRS - 1 and ca_ is cl)
                    fns.append(lambda te, ca_=ca_, wa_=wa_, n=n, ap_=ap_, st_=st_, sp_=sp_:
                               te.matmul(ps_ao[ap_ * 2 + n][:, :], ca_[:, :],
                                         wa_[:, n * 512:(n + 1) * 512], start=st_, stop=sp_))
            rds = [chB, clB, B[f"p4wh{pp}"], B[f"p4wl{pp}"]]
            pr.group("te", fns, reads=rds, writes=[pA0, pA1])
        xb, xbB = xb4[ap_], B[f"p4x{ap_}"]
        pr.op("sy", lambda s, xb=xb, t=t: s.dma_start(out=xb[:, :],
                                                      in_=x[t * 128:(t + 1) * 128, :]),
              writes=[xbB], dma=True)
        pr.op("ve", lambda v, ap_=ap_: v.tensor_tensor(
            yb4[:, 0:512], ps_ao[ap_ * 2][:, :], bob[:, 0:512], OP.add),
            reads=[pA0, B["bob"]], writes=[B["p4y"]])
        pr.op("ve", lambda v, ap_=ap_: v.tensor_tensor(
            yb4[:, 512:1024], ps_ao[ap_ * 2 + 1][:, :], bob[:, 512:1024], OP.add),
            reads=[pA1, B["bob"]], writes=[B["p4y"]])
        pr.op("ve", lambda v, xb=xb, t=t: v.tensor_scalar(
            zb4[:, :], xb[:, :], mu1a[:, t:t + 1], r1a[:, t:t + 1], OP.subtract, OP.mult),
            reads=[xbB, B["mu1a"]], writes=[B["p4z"]])
        pr.op("ve", lambda v: v.tensor_tensor(yb4[:, :], yb4[:, :], zb4[:, :], OP.add),
              reads=[B["p4z"], B["p4y"]], writes=[B["p4y"]])
        stt, sttB = (st0, B["st0"]) if ap_ == 0 else (st1, B["st1"])
        ln_stats(yb4, B["p4y"], stt, sttB, EPS2)
        pr.op("ve", lambda v, stt=stt: v.tensor_scalar(
            zb4[:, :], yb4[:, :], stt[2][:, 0:1], stt[4][:, 0:1], OP.subtract, OP.mult),
            reads=[B["p4y"], sttB], writes=[B["p4z"]])
        pr.op("ve", lambda v: v.tensor_mul(zb4[:, :], zb4[:, :], g2b[:, :]),
              reads=[B["g2b"], B["p4z"]], writes=[B["p4z"]])
        pr.op("ve", lambda v: v.tensor_tensor(zb4[:, :], zb4[:, :], be2b[:, :], OP.add),
              reads=[B["be2b"], B["p4z"]], writes=[B["p4z"]])
        ln_stats(zb4, B["p4z"], stt, sttB, EPS)
        pr.op("ve", lambda v, stt=stt: v.tensor_scalar(
            yb4[:, :], zb4[:, :], stt[2][:, 0:1], stt[4][:, 0:1], OP.subtract, OP.mult),
            reads=[B["p4z"], sttB], writes=[B["p4y"]])
        pr.op("ve", lambda v, xb=xb: v.tensor_tensor(yb4[:, :], yb4[:, :], xb[:, :], OP.add),
              reads=[xbB, B["p4y"]], writes=[B["p4y"]])
        pr.op("gp", lambda g, t=t: g.dma_start(out=x1_h[t * 128:(t + 1) * 128, :],
                                               in_=yb4[:, :]),
              reads=[B["p4y"]], writes=[B["x1_h"]], dma=True)
        for _fc in range(4 * t, 4 * t + 4):
            if _fc < 32:
                terntile(0, _fc)
            else:
                terntile(1, _fc - 32)

    pr.barrier()
    p5a.close()
    p4.close()



    p5 = contextlib.ExitStack()
    b2b = sb("b2b", [128, D], F32, p5)
    bcast_load("b2b", b2b, b2_v)
    x5 = [sb(f"p5x{i}", [128, 1024], F32, p5) for i in range(2)]
    u5l = [sb(f"p5u{i}", [128, 1024], F32, p5) for i in range(2)]
    u2h = [sb(f"p5uh{c}", [128, MEG], BF16, p5) for c in range(DC)]
    u2l = [sb(f"p5ul{c}", [128, MEG], BF16, p5) for c in range(DC)]
    w1qS = [sb(f"p5w1{c}", [128, 1024], BF16, p5) for c in range(DC)]
    w2qS = [sb(f"p5w2{c}", [128, 1024], BF16, p5) for c in range(DC)]
    hf5 = [sb(f"p5hf{i}", [128, 256], F32, p5) for i in range(2)]
    hh5 = [sb(f"p5hh{i}", [128, 256], BF16, p5) for i in range(2)]
    hl5 = [sb(f"p5hl{i}", [128, 256], BF16, p5) for i in range(2)]
    o2sb = [sb(f"p5o{i}", [128, 1024], F32, p5) for i in range(8)]
    ps_t5 = [ps(f"p5pt{i}", [128, 128], F32, p5) for i in range(2)]
    psH = [ps(f"p5ph{i}", [128, 256], F32, p5) for i in range(2)]
    psO = [ps(f"p5po{i}", [128, 512], F32, p5) for i in range(4)]
    nb("p5x0", "p5x1", "p5u0", "p5u1", "p5pt0", "p5pt1", "p5ph0", "p5ph1",
       "p5po0", "p5po1", "p5po2", "p5po3",
       "p5hf0", "p5hf1", "p5hh0", "p5hh1", "p5hl0", "p5hl1")
    for c in range(DC):
        nb(f"p5uh{c}", f"p5ul{c}", f"p5w1{c}", f"p5w2{c}")
    for i in range(8):
        nb(f"p5o{i}")

    u2b = [B[f"p5uh{c}"] for c in range(DC)] + [B[f"p5ul{c}"] for c in range(DC)]
    w1b_ = [B[f"p5w1{c}"] for c in range(DC)]
    w2b_ = [B[f"p5w2{c}"] for c in range(DC)]

    for M in range(2):
        for ts in range(8):
            t = M * 8 + ts
            par = ts % 2
            xb, xbB = x5[par], B[f"p5x{par}"]
            pr.op("sy", lambda s, xb=xb, t=t: s.dma_start(
                out=xb[:, :], in_=x1_h[t * 128:(t + 1) * 128, :]),
                reads=[B["x1_h"]], writes=[xbB], dma=True)
            stt, sttB = [(st0, B["st0"]), (st1, B["st1"]), (st2, B["st2"]),
                         (st3, B["st3"])][ts % 4]
            ln_stats(xb, xbB, stt, sttB, EPS)
            u5, u5B = u5l[ts % 2], B[f"p5u{ts % 2}"]
            pr.op("ve", lambda v, u5=u5, xb=xb, stt=stt: v.tensor_scalar(
                u5[:, :], xb[:, :], stt[2][:, 0:1], stt[4][:, 0:1], OP.subtract, OP.mult),
                reads=[xbB, sttB], writes=[u5B])
            for c in range(DC):
                tp = (ts * DC + c) % 2
                pr.op("te", lambda te, u5=u5, c=c, tp=tp: te.transpose(
                    ps_t5[tp][:, :], u5[:, c * 128:(c + 1) * 128], identf[:, :]),
                    reads=[u5B, B["identf"]], writes=[B[f"p5pt{tp}"]])
                pr.op("sc", lambda s, c=c, ts=ts, tp=tp: s.activation(
                    u2h[c][:, ts * 128:(ts + 1) * 128], ps_t5[tp][:, :], AF.Copy),
                    reads=[B[f"p5pt{tp}"]], writes=[B[f"p5uh{c}"]])
                pr.op("ve", lambda v, c=c, ts=ts, tp=tp: v.tensor_tensor(
                    u2l[c][:, ts * 128:(ts + 1) * 128], ps_t5[tp][:, :],
                    u2h[c][:, ts * 128:(ts + 1) * 128], OP.subtract),
                    reads=[B[f"p5pt{tp}"], B[f"p5uh{c}"]], writes=[B[f"p5ul{c}"]])
        for sup in range(4):
            for c in range(DC):
                pr.op("sy", lambda s, c=c, sup=sup: s.dma_start(
                    out=w1qS[c][:, :],
                    in_=w1qt_h[c * 128:(c + 1) * 128, sup * 1024:(sup + 1) * 1024]),
                    reads=[B["w1qt_h"]], writes=[B[f"p5w1{c}"]], dma=True)
                pr.op("sy", lambda s, c=c, sup=sup: s.dma_start(
                    out=w2qS[c][:, :],
                    in_=w2qt_h[sup * 1024 + c * 128:sup * 1024 + (c + 1) * 128, :]),
                    reads=[B["w2qt_h"]], writes=[B[f"p5w2{c}"]], dma=True)
            for ts in range(4):
                def emit_ffn2(fc, ts=ts):
                    hp = fc % 2
                    hb = [B[f"p5hh{hp}"], B[f"p5hl{hp}"]]
                    fns2 = []
                    for tsl in range(2):
                        for ha_ in (hh5[hp], hl5[hp]):
                            for n in range(2):
                                st_ = (fc == 0 and ha_ is hh5[hp])
                                sp_ = (fc == 7 and ha_ is hl5[hp])
                                fns2.append(lambda te, tsl=tsl, ha_=ha_, n=n, hp=hp, fc=fc, st_=st_, sp_=sp_:
                                            te.matmul(psO[tsl * 2 + n][:, :],
                                                      ha_[:, tsl * 128:(tsl + 1) * 128],
                                                      w2qS[fc][:, n * 512:(n + 1) * 512],
                                                      start=st_, stop=sp_))
                    pr.group("te", fns2, reads=hb + w2b_,
                             writes=[B["p5po0"], B["p5po1"], B["p5po2"], B["p5po3"]])

                for fc in range(9):
                    if fc < 8:
                        hp = fc % 2
                        pH = B[f"p5ph{hp}"]
                        fns = []
                        for c in range(DC):
                            for ua_ in (u2h, u2l):
                                st_ = (c == 0 and ua_ is u2h)
                                sp_ = (c == DC - 1 and ua_ is u2l)
                                fns.append(lambda te, c=c, ua_=ua_, fc=fc, ts=ts, hp=hp, st_=st_, sp_=sp_:
                                           te.matmul(psH[hp][:, :],
                                                     w1qS[c][:, fc * 128:(fc + 1) * 128],
                                                     ua_[c][:, ts * 256:(ts + 1) * 256],
                                                     start=st_, stop=sp_))
                        pr.group("te", fns, reads=w1b_ + u2b, writes=[pH])
                        fc_g = sup * 8 + fc
                        pr.op("sc", lambda s, hp=hp, fc_g=fc_g: s.activation(
                            hf5[hp][:, :], psH[hp][:, :], AF.Relu, bias=b1s[:, fc_g:fc_g + 1]),
                            reads=[pH, B["b1s"]], writes=[B[f"p5hf{hp}"]])
                        pr.op("sc", lambda s, hp=hp: s.activation(
                            hh5[hp][:, :], hf5[hp][:, :], AF.Copy),
                            reads=[B[f"p5hf{hp}"]], writes=[B[f"p5hh{hp}"]])
                        pr.op("ve", lambda v, hp=hp: v.tensor_tensor(
                            hl5[hp][:, :], hf5[hp][:, :], hh5[hp][:, :], OP.subtract),
                            reads=[B[f"p5hf{hp}"], B[f"p5hh{hp}"]], writes=[B[f"p5hl{hp}"]])
                    if fc >= 1:
                        emit_ffn2(fc - 1)
                for tsl in range(2):
                    oi = ts * 2 + tsl
                    for n in range(2):
                        if sup == 0:
                            pr.op("ve", lambda v, oi=oi, tsl=tsl, n=n: v.tensor_copy(
                                o2sb[oi][:, n * 512:(n + 1) * 512], psO[tsl * 2 + n][:, :]),
                                reads=[B[f"p5po{tsl * 2 + n}"]], writes=[B[f"p5o{oi}"]])
                        else:
                            pr.op("ve", lambda v, oi=oi, tsl=tsl, n=n: v.tensor_tensor(
                                o2sb[oi][:, n * 512:(n + 1) * 512],
                                o2sb[oi][:, n * 512:(n + 1) * 512],
                                psO[tsl * 2 + n][:, :], OP.add),
                                reads=[B[f"p5po{tsl * 2 + n}"]], writes=[B[f"p5o{oi}"]])
        for ti in range(8):
            t = M * 8 + ti
            ob, obB = o2sb[ti], B[f"p5o{ti}"]
            pr.op("ve", lambda v, ob=ob: v.tensor_tensor(ob[:, :], ob[:, :], b2b[:, :], OP.add),
                  reads=[B["b2b"]], writes=[obB])
            stt, sttB = (st0, B["st0"]) if ti % 2 == 0 else (st1, B["st1"])
            ln_stats(ob, obB, stt, sttB, EPS)
            par = ti % 2
            xb, xbB = x5[par], B[f"p5x{par}"]
            pr.op("sy", lambda s, xb=xb, t=t: s.dma_start(
                out=xb[:, :], in_=x1_h[t * 128:(t + 1) * 128, :]),
                reads=[B["x1_h"]], writes=[xbB], dma=True)
            u5, u5B = u5l[ti % 2], B[f"p5u{ti % 2}"]
            pr.op("ve", lambda v, u5=u5, ob=ob, stt=stt: v.tensor_scalar(
                u5[:, :], ob[:, :], stt[2][:, 0:1], stt[4][:, 0:1], OP.subtract, OP.mult),
                reads=[obB, sttB], writes=[u5B])
            pr.op("ve", lambda v, u5=u5, xb=xb: v.tensor_tensor(u5[:, :], u5[:, :], xb[:, :], OP.add),
                  reads=[xbB, u5B], writes=[u5B])
            pr.op("gp", lambda g, u5=u5, t=t: g.dma_start(out=y[t * 128:(t + 1) * 128, :],
                                                          in_=u5[:, :]),
                  reads=[u5B], writes=[B["y"]], dma=True)

    pr._wait("gp", "gp", pr.cnt["gp"])
    pr.op("gp", lambda g: g.memset(gsc[0:1, 4:5], 0.0), writes=[B["gsc"]])

    pr.emit(es)
    return nc, es


_CACHE = {}


def kernel(**inputs):
    x_full = np.asarray(inputs["L0_attn_x"], np.float32)
    wq = np.ascontiguousarray(np.asarray(inputs["L0_attn_Wq"], np.float32).reshape(D, D))
    wk = np.ascontiguousarray(np.asarray(inputs["L0_attn_Wk"], np.float32).reshape(D, D))
    wv = np.ascontiguousarray(np.asarray(inputs["L0_attn_Wv"], np.float32).reshape(D, D))
    wo = np.ascontiguousarray(np.asarray(inputs["L0_attn_Wo"], np.float32).reshape(D, D))
    bqt = np.ascontiguousarray(np.asarray(inputs["L0_attn_bq"], np.float32).reshape(8, 128).T)
    bkt = np.ascontiguousarray(np.asarray(inputs["L0_attn_bk"], np.float32).reshape(8, 128).T)
    bv_flat = np.asarray(inputs["L0_attn_bv"], np.float32).reshape(D)
    bo = np.asarray(inputs["L0_attn_bo"], np.float32).reshape(D)
    g2 = np.asarray(inputs["L0_attn_gamma"], np.float32).reshape(D)
    be2 = np.asarray(inputs["L0_attn_beta"], np.float32).reshape(D)
    w1 = np.asarray(inputs["L0_ffn_W1"], np.float32)
    b1t = np.ascontiguousarray(np.asarray(inputs["L0_ffn_b1"], np.float32).reshape(32, 128).T)
    w2 = np.asarray(inputs["L0_ffn_W2"], np.float32)
    b2 = np.asarray(inputs["L0_ffn_b2"], np.float32).reshape(D)

    if "nc" not in _CACHE:
        _CACHE["nc"] = build()
    nc, _ = _CACHE["nc"]

    in_maps = []
    for c in range(8):
        b, h = c // 2, c % 2
        own = x_full[b, h * 2048:(h + 1) * 2048]
        oth = x_full[b, (1 - h) * 2048:(2 - h) * 2048]
        xr = np.ascontiguousarray(np.concatenate([own, oth], axis=0))
        in_maps.append({
            "x": xr, "wq": wq, "wk": wk, "wv": wv, "wo": wo,
            "bqt": bqt, "bkt": bkt, "bv_v": bv_flat, "bo_v": bo,
            "g2_v": g2, "be2_v": be2, "b2_v": b2, "b1t": b1t,
            "w1": w1, "w2": w2,
        })
    res = run_bass_kernel_spmd(nc, in_maps, core_ids=list(range(8)))
    out = np.empty((4, 4096, 1024), np.float32)
    for c in range(8):
        b, h = c // 2, c % 2
        out[b, h * 2048:(h + 1) * 2048] = res.results[c]["y"]
    return out


# version 16
# speedup vs baseline: 1.0913x; 1.0913x over previous
import os
import contextlib
import numpy as np
import concourse.bass as bass
import concourse.mybir as mybir
from concourse.bass_utils import run_bass_kernel_spmd

F32 = mybir.dt.float32
F32R = mybir.dt.float32r
BF16 = mybir.dt.bfloat16
AF = mybir.ActivationFunctionType
OP = mybir.AluOpType
AX = mybir.AxisListType.X

D = 1024
DC = 8
S = 4096
OWN = 2048
PAIRS = 8
DFF = 4096
MEG = 1024
NMEG = 4
EPS = 1e-5
EPS2 = 1e-3
EPSQ = 1e-5

ENGS = ("sy", "sc", "ve", "te", "gp")
BLKA = {"sy": "sync", "sc": "scalar", "ve": "vector", "te": "tensor", "gp": "gpsimd"}


class Buf:
    __slots__ = ("w", "r")

    def __init__(self):
        self.w = None
        self.r = {}


class Prog:
    def __init__(self, nc):
        self.nc = nc
        self.ops = {e: [] for e in ENGS}
        self.cnt = {e: 0 for e in ENGS}
        self.pend = {e: [] for e in ENGS}
        self.waited = {e: {} for e in ENGS}

    def _wait(self, eng, sem, val):
        if val <= 0:
            return
        w = self.waited[eng]
        if w.get(sem, 0) >= val:
            return
        w[sem] = val
        self.pend[eng].append((sem, val))

    def op(self, eng, fn, reads=(), writes=(), prewrites=(), dma=False, quiet=False):
        for b in reads:
            if b.w is not None and b.w[0] != eng:
                self._wait(eng, b.w[0], b.w[1])
            elif b.w is not None:
                self._wait(eng, b.w[0], b.w[1])
        for b in prewrites:
            if b.w is not None:
                self._wait(eng, b.w[0], b.w[1])
            for re_, rv in b.r.items():
                self._wait(eng, re_, rv)
        for b in writes:
            if b.w is not None:
                self._wait(eng, b.w[0], b.w[1])
            for re_, rv in b.r.items():
                if re_ != eng or True:
                    self._wait(eng, re_, rv)
        amount = 0 if quiet else (16 if dma else 1)
        waits, self.pend[eng] = self.pend[eng], []
        self.ops[eng].append((fn, waits, amount))
        if amount:
            self.cnt[eng] += amount
        if not quiet:
            c = self.cnt[eng]
            for b in reads:
                b.r[eng] = c
            for b in writes:
                b.w = (eng, c)
                b.r = {}

    def group(self, eng, fns, reads=(), writes=()):
        self.op(eng, fns[0], reads=reads, prewrites=writes, quiet=True)
        for f in fns[1:-1]:
            self.op(eng, f, quiet=True)
        self.op(eng, fns[-1], reads=reads, writes=writes)

    def barrier(self):
        for e in ENGS:
            for o in ENGS:
                if o != e:
                    self._wait(e, o, self.cnt[o])

    def emit(self, es):
        nc = self.nc
        sems = {e: es.enter_context(nc.semaphore("sem_" + e)) for e in ENGS}
        with nc.Block() as blk:
            for e in ENGS:
                if not self.ops[e]:
                    continue

                def body(h, e=e):
                    for fn, waits, amount in self.ops[e]:
                        for s, v in waits:
                            h.wait_ge(sems[s], v)
                        inst = fn(h)
                        if amount:
                            inst.then_inc(sems[e], amount)

                getattr(blk, BLKA[e])(body)


def build():
    nc = bass.Bass("TRN2", target_bir_lowering=False)
    x = nc.dram_tensor("x", [S, D], F32, kind="ExternalInput").ap()
    wq = nc.dram_tensor("wq", [D, D], F32, kind="ExternalInput").ap()
    wk = nc.dram_tensor("wk", [D, D], F32, kind="ExternalInput").ap()
    wv = nc.dram_tensor("wv", [D, D], F32, kind="ExternalInput").ap()
    wo = nc.dram_tensor("wo", [D, D], F32, kind="ExternalInput").ap()
    bqt = nc.dram_tensor("bqt", [128, 8], F32, kind="ExternalInput").ap()
    bkt = nc.dram_tensor("bkt", [128, 8], F32, kind="ExternalInput").ap()
    bv_v = nc.dram_tensor("bv_v", [D], F32, kind="ExternalInput").ap()
    bo_v = nc.dram_tensor("bo_v", [D], F32, kind="ExternalInput").ap()
    g2_v = nc.dram_tensor("g2_v", [D], F32, kind="ExternalInput").ap()
    be2_v = nc.dram_tensor("be2_v", [D], F32, kind="ExternalInput").ap()
    b2_v = nc.dram_tensor("b2_v", [D], F32, kind="ExternalInput").ap()
    b1t = nc.dram_tensor("b1t", [128, 32], F32, kind="ExternalInput").ap()
    w1 = nc.dram_tensor("w1", [DFF, D], F32, kind="ExternalInput").ap()
    w2 = nc.dram_tensor("w2", [D, DFF], F32, kind="ExternalInput").ap()
    y = nc.dram_tensor("y", [OWN, D], F32, kind="ExternalOutput").ap()
    DBG = "ExternalOutput" if int(os.environ.get("KDBG", "0")) else "Internal"
    q_h = nc.dram_tensor("q_h", [D, OWN], F32, kind=DBG).ap()
    k_h = nc.dram_tensor("k_h", [D, S], F32, kind=DBG).ap()
    v_h = nc.dram_tensor("v_h", [S, D], F32, kind=DBG).ap()
    ctx_h = nc.dram_tensor("ctx_h", [D, OWN], F32, kind=DBG).ap()
    x1_h = nc.dram_tensor("x1_h", [OWN, D], F32, kind=DBG).ap()
    w1qt_h = nc.dram_tensor("w1qt_h", [D, DFF], BF16, kind="Internal").ap()
    w2qt_h = nc.dram_tensor("w2qt_h", [DFF, D], BF16, kind="Internal").ap()
    bc_h = nc.dram_tensor("bc_h", [8, 512], F32, kind="Internal").ap()

    es = contextlib.ExitStack()
    pr = Prog(nc)
    PH = int(os.environ.get("KPHASES", "5"))
    P3L = int(os.environ.get("KP3LVL", "8"))
    P3P = int(os.environ.get("KP3PAIRS", "8"))
    P3I = int(os.environ.get("KP3ICS", "4"))
    P3J = int(os.environ.get("KP3JS", "32"))
    NOCB = int(os.environ.get("KP3NOCB", "0"))
    NOVP = int(os.environ.get("KP3NOVP", "0"))

    def sb(n, shape, dt, stack=None):
        return (stack or es).enter_context(nc.sbuf_tensor(n, shape, dt))

    def ps(n, shape, dt, stack=None):
        return (stack or es).enter_context(nc.psum_tensor(n, shape, dt))

    B = {}

    def nb(*names):
        for n in names:
            B[n] = Buf()

    iot = sb("iot", [128, 128], F32)
    pidx = sb("pidx", [128, 1], F32)
    identf = sb("identf", [128, 128], F32)
    identr = sb("identr", [128, 128], BF16)
    ones1 = sb("ones1", [128, 1], F32)
    onesr = sb("onesr", [1, 128], F32)
    bqs = sb("bqs", [128, 8], F32)
    bks = sb("bks", [128, 8], F32)
    b1s = sb("b1s", [128, 32], F32)
    mu1a = sb("mu1a", [128, 16], F32)
    r1a = sb("r1a", [128, 16], F32)
    sqdump = sb("sqdump", [128, 1024], F32)
    thrb = sb("thrb", [128, 4], F32)
    asum = sb("asum", [128, 64], F32)
    gsc = sb("gsc", [1, 8], F32)
    st0 = [sb(f"st0_{i}", [128, 1], F32) for i in range(5)]
    st1 = [sb(f"st1_{i}", [128, 1], F32) for i in range(5)]
    epsb = sb("epsb", [128, 2], F32)
    nb("iot", "pidx", "identf", "identr", "ones1", "bvb", "bob", "g2b", "be2b",
       "b2b", "bqs", "bks", "b1s", "mu1a", "r1a", "sqdump", "thrb", "asum",
       "gsc", "st0", "st1", "epsb", "onesr", "q_h", "k_h", "v_h", "ctx_h", "x1_h", "w1qt_h",
       "w2qt_h", "y")
    for i in range(8):
        nb(f"bch{i}")

    pr.op("gp", lambda g: g.iota(iot[:, :], [[1, 128]], base=0, channel_multiplier=0,
                                 allow_small_or_imprecise_dtypes=True), writes=[B["iot"]])
    pr.op("gp", lambda g: g.iota(pidx[:, :], [[0, 1]], base=0, channel_multiplier=1,
                                 allow_small_or_imprecise_dtypes=True), writes=[B["pidx"]])
    pr.op("ve", lambda v: v.tensor_scalar(identf[:, :], iot[:, :], pidx[:, 0:1], None, OP.is_equal),
          reads=[B["iot"], B["pidx"]], writes=[B["identf"]])
    pr.op("ve", lambda v: v.tensor_scalar(identr[:, :], iot[:, :], pidx[:, 0:1], None, OP.is_equal),
          reads=[B["iot"], B["pidx"]], writes=[B["identr"]])
    pr.op("ve", lambda v: v.memset(ones1[:, :], 1.0), writes=[B["ones1"]])
    pr.op("ve", lambda v: v.memset(onesr[0:1, :], 1.0), writes=[B["onesr"]])
    pr.op("ve", lambda v: v.memset(epsb[:, 0:1], EPS), writes=[B["epsb"]])
    pr.op("ve", lambda v: v.memset(epsb[:, 1:2], EPS2), writes=[B["epsb"]])
    def bcast_load(nm, buf, vec):
        pr.op("sy", lambda s, buf=buf, vec=vec: s.dma_start(
            out=buf[:, :], in_=bass.AP(vec.tensor, 0, [[0, 128], [1, D]])),
            writes=[B[nm]], dma=True)

    pr.op("sy", lambda s: s.dma_start(out=bqs[:, :], in_=bqt[:, :]), writes=[B["bqs"]], dma=True)
    pr.op("sy", lambda s: s.dma_start(out=bks[:, :], in_=bkt[:, :]), writes=[B["bks"]], dma=True)
    pr.op("sy", lambda s: s.dma_start(out=b1s[:, :], in_=b1t[:, :]), writes=[B["b1s"]], dma=True)

    w1b = [sb(f"p1w{i}", [128, 1024], F32) for i in range(2)]
    nb("p1w0", "p1w1")

    def p1_batch(idx):
        for t in range(4 * idx, 4 * idx + 4):
            par = t % 2
            wb, wbB = w1b[par], B[f"p1w{par}"]
            if t < 32:
                src = w1[t * 128:(t + 1) * 128, :]
            else:
                t2 = t - 32
                r, cc = t2 // 4, t2 % 4
                src = w2[r * 128:(r + 1) * 128, cc * 1024:(cc + 1) * 1024]
            pr.op("gp", lambda g, wb=wb, src=src: g.dma_start(out=wb[:, :], in_=src),
                  writes=[wbB], dma=True)
            pr.op("sc", lambda s, wb=wb, t=t: s.activation(
                sqdump[:, :], wb[:, :], AF.Abs, accum_out=asum[:, t:t + 1]),
                reads=[wbB], writes=[B["sqdump"], B["asum"]])

    def ln_stats(xb, xbB, stt, sttB, eps, save=None):
        ssq, ssum, mu, var, rstd = stt
        pr.op("sc", lambda s: s.activation(sqdump[:, :], xb[:, :], AF.Square,
                                           accum_out=ssq[:, 0:1]),
              reads=[xbB], writes=[B["sqdump"], sttB])
        pr.op("ve", lambda v: v.reduce_sum(ssum[:, 0:1], xb[:, :], axis=AX),
              reads=[xbB], writes=[sttB])
        pr.op("ve", lambda v: v.tensor_scalar_mul(mu[:, 0:1], ssum[:, 0:1], 1.0 / D),
              reads=[sttB], writes=[sttB])
        pr.op("ve", lambda v: v.tensor_scalar_mul(var[:, 0:1], ssq[:, 0:1], 1.0 / D),
              reads=[sttB], writes=[sttB])
        pr.op("ve", lambda v: v.tensor_mul(ssum[:, 0:1], mu[:, 0:1], mu[:, 0:1]),
              reads=[sttB], writes=[sttB])
        pr.op("ve", lambda v: v.tensor_sub(var[:, 0:1], var[:, 0:1], ssum[:, 0:1]),
              reads=[sttB], writes=[sttB])
        ecol = 0 if eps == EPS else 1
        pr.op("sc", lambda s, ecol=ecol: s.activation(ssum[:, 0:1], var[:, 0:1], AF.Sqrt,
                                                      bias=epsb[:, ecol:ecol + 1]),
              reads=[sttB, B["epsb"]], writes=[sttB])
        pr.op("ve", lambda v: v.reciprocal(rstd[:, 0:1], ssum[:, 0:1]),
              reads=[sttB], writes=[sttB])
        if save is not None:
            sv, svB, col = save
            pr.op("ve", lambda v: v.tensor_copy(sv[0][:, col:col + 1], mu[:, 0:1]),
                  reads=[sttB], writes=[svB])
            pr.op("ve", lambda v: v.tensor_copy(sv[1][:, col:col + 1], rstd[:, 0:1]),
                  reads=[sttB], writes=[svB])

    p2 = contextlib.ExitStack()
    bvb = sb("bvb", [128, D], F32, p2)
    bcast_load("bvb", bvb, bv_v)
    xt2 = [sb(f"p2x{i}", [128, 1024], F32, p2) for i in range(2)]
    ut2 = sb("p2u", [128, 1024], F32, p2)
    uTh = [sb(f"p2uth{c}", [128, MEG], BF16, p2) for c in range(DC)]
    uTl = [sb(f"p2utl{c}", [128, MEG], BF16, p2) for c in range(DC)]
    wraw = [sb(f"p2wr{i}", [128, 1024], F32, p2) for i in range(2)]
    wh = [sb(f"p2wh{c}", [128, 1024], BF16, p2) for c in range(DC)]
    wl = [sb(f"p2wl{c}", [128, 1024], BF16, p2) for c in range(DC)]
    qev = [sb(f"p2qe{i}", [128, 512], F32, p2) for i in range(2)]
    ps_t = [ps(f"p2pt{i}", [128, 128], F32, p2) for i in range(4)]
    ps_qk = [ps(f"p2pq{i}", [128, 512], F32, p2) for i in range(2)]
    ps_v = [ps(f"p2pv{i}", [128, 512], F32, p2) for i in range(2)]
    nb("p2x0", "p2x1", "p2u", "p2qe0", "p2qe1", "p2pt0", "p2pt1", "p2pt2", "p2pt3",
       "p2pq0", "p2pq1", "p2pv0", "p2pv1", "p2wr0", "p2wr1")
    for c in range(DC):
        nb(f"p2uth{c}", f"p2utl{c}", f"p2wh{c}", f"p2wl{c}")

    for M in range(NMEG):
        p1_batch(M * 4 + 0)
        for ts in range(8):
            t = M * 8 + ts
            par = ts % 2
            xb, xbB = xt2[par], B[f"p2x{par}"]
            pr.op("sy", lambda s, xb=xb, t=t: s.dma_start(
                out=xb[:, :], in_=x[t * 128:(t + 1) * 128, :]), writes=[xbB], dma=True)
            stt, sttB = (st0, B["st0"]) if par == 0 else (st1, B["st1"])
            save = ((mu1a, r1a), B["mu1a"], t) if t < 16 else None
            ln_stats(xb, xbB, stt, sttB, EPS, save=save)
            pr.op("ve", lambda v, xb=xb, stt=stt: v.tensor_scalar(
                ut2[:, :], xb[:, :], stt[2][:, 0:1], stt[4][:, 0:1], OP.subtract, OP.mult),
                reads=[xbB, sttB], writes=[B["p2u"]])
            for c in range(DC):
                pp = (ts * DC + c) % 4
                pr.op("te", lambda te, c=c, pp=pp: te.transpose(
                    ps_t[pp][:, :], ut2[:, c * 128:(c + 1) * 128], identf[:, :]),
                    reads=[B["p2u"], B["identf"]], writes=[B[f"p2pt{pp}"]])
                pr.op("sc", lambda s, c=c, ts=ts, pp=pp: s.activation(
                    uTh[c][:, ts * 128:(ts + 1) * 128], ps_t[pp][:, :], AF.Copy),
                    reads=[B[f"p2pt{pp}"]], writes=[B[f"p2uth{c}"]])
                pr.op("ve", lambda v, c=c, ts=ts, pp=pp: v.tensor_tensor(
                    uTl[c][:, ts * 128:(ts + 1) * 128], ps_t[pp][:, :],
                    uTh[c][:, ts * 128:(ts + 1) * 128], OP.subtract),
                    reads=[B[f"p2pt{pp}"], B[f"p2uth{c}"]], writes=[B[f"p2utl{c}"]])
        p1_batch(M * 4 + 1)
        wlist = [("q", wq, q_h, bqs, B["bqs"]), ("k", wk, k_h, bks, B["bks"]),
                 ("v", wv, None, None, None)]
        if M >= 2:
            wlist = wlist[1:]
        for wi, (kind, wsrc, dsth, bias, biasB) in enumerate(wlist):
            for c in range(DC):
                par = c % 2
                wb, wbB = wraw[par], B[f"p2wr{par}"]
                pr.op("sy", lambda s, wb=wb, wsrc=wsrc, c=c: s.dma_start(
                    out=wb[:, :], in_=wsrc[c * 128:(c + 1) * 128, :]), writes=[wbB], dma=True)
                pr.op("ve", lambda v, c=c, wb=wb: v.tensor_copy(wh[c][:, :], wb[:, :]),
                      reads=[wbB], writes=[B[f"p2wh{c}"]])
                pr.op("ve", lambda v, c=c, wb=wb: v.tensor_tensor(
                    wl[c][:, :], wb[:, :], wh[c][:, :], OP.subtract),
                    reads=[wbB, B[f"p2wh{c}"]], writes=[B[f"p2wl{c}"]])
            wrb = [B[f"p2wh{c}"] for c in range(DC)] + [B[f"p2wl{c}"] for c in range(DC)]
            uTb = [B[f"p2uth{c}"] for c in range(DC)] + [B[f"p2utl{c}"] for c in range(DC)]
            if kind in ("q", "k"):
                for m in range(8):
                    for n in range(2):
                        par = (m * 2 + n) % 2
                        pb = B[f"p2pq{par}"]
                        fns = []
                        for c in range(DC):
                            for (a_, b_) in ((wh, uTh), (wh, uTl), (wl, uTh)):
                                st_ = (c == 0 and a_ is wh and b_ is uTh)
                                sp_ = (c == DC - 1 and a_ is wl)
                                fns.append(lambda te, c=c, a_=a_, b_=b_, m=m, n=n, par=par, st_=st_, sp_=sp_:
                                           te.matmul(ps_qk[par][:, :],
                                                     a_[c][:, m * 128:(m + 1) * 128],
                                                     b_[c][:, n * 512:(n + 1) * 512],
                                                     start=st_, stop=sp_))
                        pr.group("te", fns, reads=wrb + uTb, writes=[pb])
                        qpar = (m * 2 + n) % 2
                        qb, qbB = qev[qpar], B[f"p2qe{qpar}"]
                        pr.op("ve", lambda v, par=par, qb=qb, bias=bias, m=m: v.tensor_scalar(
                            qb[:, :], ps_qk[par][:, :], bias[:, m:m + 1], None, OP.add),
                            reads=[pb, biasB], writes=[qbB])
                        tok0 = M * MEG + n * 512
                        pr.op("gp", lambda g, qb=qb, dsth=dsth, m=m, tok0=tok0: g.dma_start(
                            out=dsth[m * 128:(m + 1) * 128, tok0:tok0 + 512], in_=qb[:, :]),
                            reads=[qbB], writes=[B["q_h" if dsth is q_h else "k_h"]], dma=True)
            else:
                for ts in range(8):
                    for n in range(2):
                        par = (ts * 2 + n) % 2
                        pb = B[f"p2pv{par}"]
                        fns = []
                        for c in range(DC):
                            for (a_, b_) in ((uTh, wh), (uTh, wl), (uTl, wh)):
                                st_ = (c == 0 and a_ is uTh and b_ is wh)
                                sp_ = (c == DC - 1 and a_ is uTl)
                                fns.append(lambda te, c=c, a_=a_, b_=b_, ts=ts, n=n, par=par, st_=st_, sp_=sp_:
                                           te.matmul(ps_v[par][:, :],
                                                     a_[c][:, ts * 128:(ts + 1) * 128],
                                                     b_[c][:, n * 512:(n + 1) * 512],
                                                     start=st_, stop=sp_))
                        pr.group("te", fns, reads=wrb + uTb, writes=[pb])
                        qpar = (ts * 2 + n) % 2
                        qb, qbB = qev[qpar], B[f"p2qe{qpar}"]
                        pr.op("ve", lambda v, par=par, qb=qb, n=n: v.tensor_tensor(
                            qb[:, :], ps_v[par][:, :], bvb[:, n * 512:(n + 1) * 512], OP.add),
                            reads=[pb, B["bvb"]], writes=[qbB])
                        tok0 = M * MEG + ts * 128
                        pr.op("gp", lambda g, qb=qb, ts=ts, n=n, tok0=tok0: g.dma_start(
                            out=v_h[tok0:tok0 + 128, n * 512:(n + 1) * 512], in_=qb[:, :]),
                            reads=[qbB], writes=[B["v_h"]], dma=True)
        p1_batch(M * 4 + 2)
        p1_batch(M * 4 + 3)

    pr.op("ve", lambda v: v.reduce_sum(st0[0][:, 0:1], asum[:, 0:32], axis=AX),
          reads=[B["asum"]], writes=[B["st0"]])
    pr.op("ve", lambda v: v.reduce_sum(st0[1][:, 0:1], asum[:, 32:64], axis=AX),
          reads=[B["asum"]], writes=[B["st0"]])
    pr.op("te", lambda te: te.matmul(ps_t[0][0:1, 0:1], ones1[:, 0:1], st0[0][:, 0:1],
                                     start=True, stop=True),
          reads=[B["st0"], B["ones1"]], writes=[B["p2pt0"]])
    pr.op("te", lambda te: te.matmul(ps_t[1][0:1, 0:1], ones1[:, 0:1], st0[1][:, 0:1],
                                     start=True, stop=True),
          reads=[B["st0"], B["ones1"]], writes=[B["p2pt1"]])
    pr.op("ve", lambda v: v.tensor_scalar(gsc[0:1, 0:1], ps_t[0][0:1, 0:1],
                                          1.0 / (DFF * D), EPSQ, OP.mult, OP.add),
          reads=[B["p2pt0"]], writes=[B["gsc"]])
    pr.op("ve", lambda v: v.tensor_scalar(gsc[0:1, 2:3], ps_t[1][0:1, 0:1],
                                          1.0 / (DFF * D), EPSQ, OP.mult, OP.add),
          reads=[B["p2pt1"]], writes=[B["gsc"]])
    pr.op("ve", lambda v: v.tensor_scalar_mul(gsc[0:1, 0:1], gsc[0:1, 0:1], 0.5),
          reads=[B["gsc"]], writes=[B["gsc"]])
    pr.op("ve", lambda v: v.tensor_scalar_mul(gsc[0:1, 2:3], gsc[0:1, 2:3], 0.5),
          reads=[B["gsc"]], writes=[B["gsc"]])
    pr.op("ve", lambda v: v.tensor_scalar_mul(gsc[0:1, 1:2], gsc[0:1, 0:1], -1.0),
          reads=[B["gsc"]], writes=[B["gsc"]])
    pr.op("ve", lambda v: v.tensor_scalar_mul(gsc[0:1, 3:4], gsc[0:1, 2:3], -1.0),
          reads=[B["gsc"]], writes=[B["gsc"]])
    pr.op("te", lambda te: te.matmul(ps_t[0][:, 0:4], onesr[0:1, :], gsc[0:1, 0:4],
                                     start=True, stop=True),
          reads=[B["gsc"], B["onesr"]], writes=[B["p2pt0"]])
    pr.op("ve", lambda v: v.tensor_copy(thrb[:, :], ps_t[0][:, 0:4]),
          reads=[B["p2pt0"]], writes=[B["thrb"]])

    pr.barrier()
    p2.close()

    if PH < 3:
        pr._wait("gp", "gp", pr.cnt["gp"])
        pr._wait("gp", "sy", pr.cnt["sy"])
        pr.op("gp", lambda g: g.memset(gsc[0:1, 4:5], 0.0), writes=[B["gsc"]])
        pr.emit(es)
        return nc, es
    p3 = contextlib.ExitStack()
    kraw = sb("p3kr", [128, S], F32, p3)
    khh = sb("p3kh", [128, S], BF16, p3)
    kll = sb("p3kl", [128, S], BF16, p3)
    qraw = sb("p3qr", [128, OWN], F32, p3)
    qhh = sb("p3qh", [128, OWN], BF16, p3)
    qll = sb("p3ql", [128, OWN], BF16, p3)
    vr3 = sb("p3vr", [128, 32, 128], F32, p3)
    vph = sb("p3vph", [128, 32, 130], BF16, p3)
    vpl = sb("p3vpl", [128, 32, 130], BF16, p3)
    pfb = [sb(f"p3pf{i}", [128, 1024], F32, p3) for i in range(3)]
    phb = [sb(f"p3ph{i}", [128, 1024], BF16, p3) for i in range(3)]
    plb = [sb(f"p3pl{i}", [128, 1024], BF16, p3) for i in range(3)]
    maxk = sb("p3mk", [128, 1], F32, p3)
    cst = sb("p3cs", [1, 16], F32, p3)
    biasA = sb("p3ba", [128, 1], F32, p3)
    biasB = sb("p3bb", [128, 1], F32, p3)
    rs_s = sb("p3rs", [1, 512], F32, p3)
    rs_t = sb("p3rt", [1, 512], F32, p3)
    rsb = [sb(f"p3rb{i}", [64, 512], F32, p3) for i in range(4)]
    cxe = [sb(f"p3cx{i}", [64, 512], F32, p3) for i in range(4)]
    psSA = [ps(f"p3ssa{i}", [128, 512], F32, p3) for i in range(2)]
    psSB = [ps(f"p3ssb{i}", [128, 512], F32, p3) for i in range(2)]
    psCA = [ps(f"p3ca{i}", [65, 512], F32, p3) for i in range(2)]
    psCB = [ps(f"p3cb{i}", [65, 512], F32, p3) for i in range(2)]
    nb("p3kr", "p3kh", "p3kl", "p3qr", "p3qh", "p3ql", "p3vr", "p3vph", "p3vpl",
       "p3mk", "p3cs", "p3ba", "p3bb", "p3rs", "p3rt",
       "p3pf0", "p3pf1", "p3pf2", "p3ph0", "p3ph1", "p3ph2", "p3pl0", "p3pl1", "p3pl2",
       "p3ssa0", "p3ssa1", "p3ssb0", "p3ssb1", "p3ca0", "p3ca1", "p3cb0", "p3cb1",
       "p3rb0", "p3rb1", "p3rb2", "p3rb3", "p3cx0", "p3cx1", "p3cx2", "p3cx3")

    for p in range(P3P):
        pr.op("sy", lambda s, p=p: s.dma_start(out=kraw[:, :],
                                               in_=k_h[p * 128:(p + 1) * 128, :]),
              reads=[B["k_h"]], writes=[B["p3kr"]], dma=True)
        pr.op("sy", lambda s, p=p: s.dma_start(out=qraw[:, :],
                                               in_=q_h[p * 128:(p + 1) * 128, :]),
              reads=[B["q_h"]], writes=[B["p3qr"]], dma=True)
        pr.op("sy", lambda s, p=p: s.dma_start(
            out=vr3[:, :, :], in_=bass.AP(v_h.tensor, p * 128,
                                          [[1024, 128], [131072, 32], [1, 128]])),
            reads=[B["v_h"]], writes=[B["p3vr"]], dma=True)
        pr.op("ve", lambda v: v.tensor_copy(khh[:, :], kraw[:, :]),
              reads=[B["p3kr"]], writes=[B["p3kh"]])
        pr.op("ve", lambda v: v.tensor_tensor(kll[:, :], kraw[:, :], khh[:, :],
                                              OP.subtract),
              reads=[B["p3kr"], B["p3kh"]], writes=[B["p3kl"]])
        pr.op("ve", lambda v: v.tensor_copy(qhh[:, :], qraw[:, :]),
              reads=[B["p3qr"]], writes=[B["p3qh"]])
        pr.op("ve", lambda v: v.tensor_tensor(qll[:, :], qraw[:, :], qhh[:, :],
                                              OP.subtract),
              reads=[B["p3qr"], B["p3qh"]], writes=[B["p3ql"]])
        pr.op("ve", lambda v: v.reduce_max(maxk[:, 0:1], kraw[:, :], axis=AX,
                                           apply_absolute_value=True),
              reads=[B["p3kr"]], writes=[B["p3mk"]])
        pr.op("sc", lambda s: s.activation(pfb[0][:, :], qraw[:, 0:1024], AF.Abs),
              reads=[B["p3qr"]], writes=[B["p3pf0"]])
        pr.op("sc", lambda s: s.activation(pfb[1][:, :], qraw[:, 1024:2048], AF.Abs),
              reads=[B["p3qr"]], writes=[B["p3pf1"]])
        if P3L < 2:
            continue
        if p == 0 and not NOVP:
            pr.op("ve", lambda v: v.memset(vph[:, :, :], 1.0), writes=[B["p3vph"]])
            pr.op("ve", lambda v: v.memset(vpl[:, :, :], 0.0), writes=[B["p3vpl"]])
        if NOVP:
            pr.op("ve", lambda v: v.tensor_copy(vph[:, 0, 0:64], vr3[:, 0, 0:64]),
                  reads=[B["p3vr"]], writes=[B["p3vph"]])
        if not NOVP:
            pr.op("ve", lambda v: v.tensor_copy(vph[:, :, 0:64], vr3[:, :, 0:64]),
              reads=[B["p3vr"]], writes=[B["p3vph"]])
            pr.op("ve", lambda v: v.tensor_copy(vph[:, :, 65:129], vr3[:, :, 64:128]),
                  reads=[B["p3vr"]], writes=[B["p3vph"]])
            pr.op("ve", lambda v: v.tensor_tensor(vpl[:, :, 0:64], vr3[:, :, 0:64],
                                                  vph[:, :, 0:64], OP.subtract),
                  reads=[B["p3vr"], B["p3vph"]], writes=[B["p3vpl"]])
            pr.op("ve", lambda v: v.tensor_tensor(vpl[:, :, 65:129], vr3[:, :, 64:128],
                                                  vph[:, :, 65:129], OP.subtract),
                  reads=[B["p3vr"], B["p3vph"]], writes=[B["p3vpl"]])
        if P3L < 3:
            continue
        if NOCB:
            pr.op("ve", lambda v: v.memset(biasA[:, 0:1], -3.0), writes=[B["p3ba"]])
            pr.op("ve", lambda v: v.memset(biasB[:, 0:1], -3.0), writes=[B["p3bb"]])
        for hh_ in range(0 if NOCB else 2):
            for ic in range(4):
                pf, pfB = (pfb[0], B["p3pf0"]) if ic < 2 else (pfb[1], B["p3pf1"])
                cslice = pf[hh_ * 64:hh_ * 64 + 64, (ic % 2) * 512:(ic % 2) * 512 + 512]
                pr.op("te", lambda te, hh_=hh_, cs_=cslice: te.matmul(
                    psSA[0][0:1, :], maxk[hh_ * 64:hh_ * 64 + 64, 0:1], cs_,
                    start=True, stop=True),
                    reads=[B["p3mk"], pfB], writes=[B["p3ssa0"]])
                pr.op("ve", lambda v, hh_=hh_, ic=ic: v.reduce_max(
                    cst[0:1, hh_ * 4 + ic:hh_ * 4 + ic + 1], psSA[0][0:1, :], axis=AX),
                    reads=[B["p3ssa0"]], writes=[B["p3cs"]])
        if not NOCB:
            pr.op("ve", lambda v: v.reduce_max(cst[0:1, 8:9], cst[0:1, 0:4], axis=AX),
                  reads=[B["p3cs"]], writes=[B["p3cs"]])
            pr.op("ve", lambda v: v.reduce_max(cst[0:1, 9:10], cst[0:1, 4:8], axis=AX),
                  reads=[B["p3cs"]], writes=[B["p3cs"]])
            pr.op("ve", lambda v: v.tensor_scalar(cst[0:1, 10:12], cst[0:1, 8:10],
                                                  -0.125 * 1.001, -0.01, OP.mult, OP.add),
                  reads=[B["p3cs"]], writes=[B["p3cs"]])
            pr.op("te", lambda te: te.matmul(psSA[0][:, 0:2], onesr[0:1, :], cst[0:1, 10:12],
                                             start=True, stop=True),
                  reads=[B["p3cs"], B["onesr"]], writes=[B["p3ssa0"]])
            pr.op("ve", lambda v: v.tensor_copy(biasA[:, 0:1], psSA[0][:, 0:1]),
                  reads=[B["p3ssa0"]], writes=[B["p3ba"]])
            pr.op("ve", lambda v: v.tensor_copy(biasB[:, 0:1], psSA[0][:, 1:2]),
                  reads=[B["p3ssa0"]], writes=[B["p3bb"]])

        if P3L < 4:
            continue
        kb = [B["p3kh"], B["p3kl"]]
        qb = [B["p3qh"], B["p3ql"]]
        vb = [B["p3vph"], B["p3vpl"]]
        evac_q = []
        for ic in range(P3I):
            cp = ic % 2
            pCA, pCB = B[f"p3ca{cp}"], B[f"p3cb{cp}"]
            def emit_pv(j):
                sp = j % 3
                pvb = [B[f"p3ph{sp}"], B[f"p3pl{sp}"]]
                fns = []
                for hh_, pC in ((0, psCA[cp]), (1, psCB[cp])):
                    c0 = hh_ * 65
                    for pi, (va_, pa_) in enumerate(((vph, phb[sp]), (vpl, phb[sp]), (vph, plb[sp]))):
                        st_ = (j == 0 and pi == 0)
                        sp_ = (j == P3J - 1 and pi == 2)
                        fns.append(lambda te, va_=va_, pa_=pa_, hh_=hh_, j=j, c0=c0, pC=pC, st_=st_, sp_=sp_:
                                   te.matmul(pC[:, :], va_[:, j, c0:c0 + 65],
                                             pa_[:, hh_ * 512:hh_ * 512 + 512],
                                             start=st_, stop=sp_))
                pr.group("te", fns, reads=vb + pvb, writes=[pCA, pCB])

            for j in range(P3J + 1):
                if j < P3J:
                    sp = j % 2
                    pSA, pSB = B[f"p3ssa{sp}"], B[f"p3ssb{sp}"]
                    i0 = ic * 512
                    fns = []
                    for (ka_, qa_) in ((khh, qhh), (khh, qll), (kll, qhh)):
                        for hh_ in range(2):
                            st_ = (ka_ is khh and qa_ is qhh)
                            sp_ = (ka_ is kll)
                            pst = (psSA if hh_ == 0 else psSB)[sp]
                            fns.append(lambda te, ka_=ka_, qa_=qa_, hh_=hh_, j=j, i0=i0, pst=pst, st_=st_, sp_=sp_:
                                       te.matmul(pst[:, :],
                                                 ka_[hh_ * 64:hh_ * 64 + 64, j * 128:(j + 1) * 128],
                                                 qa_[hh_ * 64:hh_ * 64 + 64, i0:i0 + 512],
                                                 start=st_, stop=sp_,
                                                 tile_position=(hh_ * 64, 0)))
                    pr.group("te", fns, reads=kb + qb, writes=[pSA, pSB])
                    pp3 = j % 3
                    if P3L >= 5:
                        pr.op("sc", lambda s, pp3=pp3, sp=sp: s.activation(
                            pfb[pp3][:, 0:512], psSA[sp][:, :], AF.Exp, scale=0.125,
                            bias=biasA[:, 0:1]),
                            reads=[pSA, B["p3ba"]], writes=[B[f"p3pf{pp3}"]])
                        pr.op("sc", lambda s, pp3=pp3, sp=sp: s.activation(
                            pfb[pp3][:, 512:1024], psSB[sp][:, :], AF.Exp, scale=0.125,
                            bias=biasB[:, 0:1]),
                            reads=[pSB, B["p3bb"]], writes=[B[f"p3pf{pp3}"]])
                    if P3L >= 6:
                        pr.op("ve", lambda v, pp3=pp3: v.tensor_copy(
                            phb[pp3][:, :], pfb[pp3][:, :]),
                            reads=[B[f"p3pf{pp3}"]], writes=[B[f"p3ph{pp3}"]])
                        pr.op("ve", lambda v, pp3=pp3: v.tensor_tensor(
                            plb[pp3][:, :], pfb[pp3][:, :], phb[pp3][:, :], OP.subtract),
                            reads=[B[f"p3pf{pp3}"], B[f"p3ph{pp3}"]], writes=[B[f"p3pl{pp3}"]])
                if P3L >= 7 and j >= 2:
                    emit_pv(j - 2)
                if j == P3J:
                    emit_pv(P3J - 1)
                if j == 6 and evac_q:
                    evac_q.pop(0)()
            if P3L < 8:
                continue
            def emit_evac(cp=cp, pCA=pCA, pCB=pCB, ic=ic):
              for hh_, pC, pCB_ in ((0, psCA[cp], pCA), (1, psCB[cp], pCB)):
                  rp = cp * 2 + hh_
                  rb, rbB = rsb[rp], B[f"p3rb{rp}"]
                  cx, cxB = cxe[rp], B[f"p3cx{rp}"]
                  pr.op("ve", lambda v, pC=pC: v.tensor_scalar_add(rs_s[0:1, :], pC[64:65, :], 1e-30),
                        reads=[pCB_], writes=[B["p3rs"]])
                  pr.op("ve", lambda v: v.reciprocal(rs_t[0:1, :], rs_s[0:1, :]),
                        reads=[B["p3rs"]], writes=[B["p3rt"]])
                  pr.op("ve", lambda v: v.tensor_mul(rs_s[0:1, :], rs_s[0:1, :], rs_t[0:1, :]),
                        reads=[B["p3rs"], B["p3rt"]], writes=[B["p3rs"]])
                  pr.op("ve", lambda v: v.tensor_scalar(rs_s[0:1, :], rs_s[0:1, :], -1.0, 2.0,
                                                        OP.mult, OP.add),
                        reads=[B["p3rs"]], writes=[B["p3rs"]])
                  pr.op("ve", lambda v: v.tensor_mul(rs_t[0:1, :], rs_t[0:1, :], rs_s[0:1, :]),
                        reads=[B["p3rs"], B["p3rt"]], writes=[B["p3rt"]])
                  psr, psrB = (psSA, "p3ssa") if hh_ == 0 else (psSB, "p3ssb")
                  pr.op("te", lambda te, psr=psr: te.matmul(
                      psr[0][0:64, :], onesr[0:1, 0:64], rs_t[0:1, :], start=True, stop=True),
                      reads=[B["p3rt"], B["onesr"]], writes=[B[psrB + "0"]])
                  pr.op("ve", lambda v, rb=rb, psr=psr: v.tensor_copy(rb[:, :], psr[0][0:64, :]),
                        reads=[B[psrB + "0"]], writes=[rbB])
                  pr.op("ve", lambda v, cx=cx, pC=pC, rb=rb: v.tensor_tensor(
                      cx[:, :], pC[0:64, :], rb[:, :], OP.mult),
                      reads=[pCB_, rbB], writes=[cxB])
                  r0 = p * 128 + hh_ * 64
                  pr.op("gp", lambda g, cx=cx, r0=r0, ic=ic: g.dma_start(
                      out=ctx_h[r0:r0 + 64, ic * 512:(ic + 1) * 512], in_=cx[:, :]),
                      reads=[cxB], writes=[B["ctx_h"]], dma=True)
            evac_q.append(emit_evac)

        for fn_ in evac_q:
            fn_()
        evac_q = []
    pr.barrier()
    p3.close()

    if PH < 4:
        pr._wait("gp", "gp", pr.cnt["gp"])
        pr._wait("gp", "sy", pr.cnt["sy"])
        pr.op("gp", lambda g: g.memset(gsc[0:1, 4:5], 0.0), writes=[B["gsc"]])
        pr.emit(es)
        return nc, es
    p4 = contextlib.ExitStack()
    bob = sb("bob", [128, D], F32, p4)
    g2b = sb("g2b", [128, D], F32, p4)
    be2b = sb("be2b", [128, D], F32, p4)
    bcast_load("bob", bob, bo_v)
    bcast_load("g2b", g2b, g2_v)
    bcast_load("be2b", be2b, be2_v)
    ctxs = [sb(f"p4c{pp}", [128, OWN], F32, p4) for pp in range(PAIRS)]
    woh = [sb(f"p4wh{c}", [128, 1024], BF16, p4) for c in range(DC)]
    wol = [sb(f"p4wl{c}", [128, 1024], BF16, p4) for c in range(DC)]
    wraw4 = [sb(f"p4wr{i}", [128, 1024], F32, p4) for i in range(2)]
    chs = [sb(f"p4ch{i}", [128, 128], BF16, p4) for i in range(2)]
    cls = [sb(f"p4cl{i}", [128, 128], BF16, p4) for i in range(2)]
    xb4 = [sb(f"p4x{i}", [128, 1024], F32, p4) for i in range(2)]
    yb4 = sb("p4y", [128, 1024], F32, p4)
    zb4 = sb("p4z", [128, 1024], F32, p4)
    ps_ao = [ps(f"p4ao{i}", [128, 512], F32, p4) for i in range(4)]
    nb("p4y", "p4z", "p4wr0", "p4wr1", "p4ao0", "p4ao1", "p4ao2", "p4ao3", "p4x0", "p4x1",
       "p4ch0", "p4ch1", "p4cl0", "p4cl1")
    for pp in range(PAIRS):
        nb(f"p4c{pp}", f"p4wh{pp}", f"p4wl{pp}")

    p5a = contextlib.ExitStack()
    wr5 = [sb(f"p5wr{i}", [128, 1024], F32, p5a) for i in range(2)]
    tq5 = [sb(f"p5tq{i}", [128, 1024], BF16, p5a) for i in range(2)]
    sc5 = sb("p5sc", [128, 1024], F32, p5a)
    tevac = [sb(f"p5te{i}", [128, 128], BF16, p5a) for i in range(2)]
    ps_tr = [ps(f"p5tr{i}", [128, 128], BF16, p5a) for i in range(2)]
    nb("p5wr0", "p5wr1", "p5tq0", "p5tq1", "p5sc", "p5te0", "p5te1", "p5tr0", "p5tr1")

    def terntile(widx, fc):
        par = fc % 2
        wb, wbB = wr5[par], B[f"p5wr{par}"]
        tq, tqB = tq5[par], B[f"p5tq{par}"]
        tcol = 0 if widx == 0 else 2
        if widx == 0:
            pr.op("sy", lambda s, fc=fc: s.dma_start(out=wb[:, :],
                                                     in_=w1[fc * 128:(fc + 1) * 128, :]),
                  writes=[wbB], dma=True)
        else:
            for c in range(DC):
                pr.op("sy", lambda s, fc=fc, c=c: s.dma_start(
                    out=wb[:, c * 128:(c + 1) * 128],
                    in_=w2[c * 128:(c + 1) * 128, fc * 128:(fc + 1) * 128]),
                    writes=[wbB], dma=True)
        pr.op("ve", lambda v, wb=wb, tcol=tcol: v.tensor_scalar(
            sc5[:, :], wb[:, :], thrb[:, tcol:tcol + 1], None, OP.is_gt),
            reads=[wbB, B["thrb"]], writes=[B["p5sc"]])
        pr.op("ve", lambda v, wb=wb, tcol=tcol: v.tensor_scalar(
            wb[:, :], wb[:, :], thrb[:, tcol + 1:tcol + 2], None, OP.is_lt),
            reads=[wbB, B["thrb"]], writes=[wbB])
        pr.op("ve", lambda v, wb=wb, tq=tq: v.tensor_tensor(tq[:, :], sc5[:, :], wb[:, :],
                                                            OP.subtract),
            reads=[wbB, B["p5sc"]], writes=[tqB])
        for c in range(DC):
            tp = c % 2
            pr.op("te", lambda te, tq=tq, c=c, tp=tp: te.transpose(
                ps_tr[tp][:, :], tq[:, c * 128:(c + 1) * 128], identr[:, :]),
                reads=[tqB, B["identr"]], writes=[B[f"p5tr{tp}"]])
            pr.op("ve", lambda v, tp=tp: v.tensor_copy(tevac[tp][:, :], ps_tr[tp][:, :]),
                  reads=[B[f"p5tr{tp}"]], writes=[B[f"p5te{tp}"]])
            if widx == 0:
                dst = w1qt_h[c * 128:(c + 1) * 128, fc * 128:(fc + 1) * 128]
                dB = "w1qt_h"
            else:
                dst = w2qt_h[fc * 128:(fc + 1) * 128, c * 128:(c + 1) * 128]
                dB = "w2qt_h"
            pr.op("gp", lambda g, dst=dst, tp=tp: g.dma_start(out=dst, in_=tevac[tp][:, :]),
                  reads=[B[f"p5te{tp}"]], writes=[B[dB]], dma=True)


    for pp in range(PAIRS):
        pr.op("sy", lambda s, pp=pp: s.dma_start(out=ctxs[pp][:, :],
                                                 in_=ctx_h[pp * 128:(pp + 1) * 128, :]),
              reads=[B["ctx_h"]], writes=[B[f"p4c{pp}"]], dma=True)
        par = pp % 2
        wb, wbB = wraw4[par], B[f"p4wr{par}"]
        pr.op("sy", lambda s, wb=wb, pp=pp: s.dma_start(
            out=wb[:, :], in_=wo[pp * 128:(pp + 1) * 128, :]), writes=[wbB], dma=True)
        pr.op("ve", lambda v, pp=pp, wb=wb: v.tensor_copy(woh[pp][:, :], wb[:, :]),
              reads=[wbB], writes=[B[f"p4wh{pp}"]])
        pr.op("ve", lambda v, pp=pp, wb=wb: v.tensor_tensor(
            wol[pp][:, :], wb[:, :], woh[pp][:, :], OP.subtract),
            reads=[wbB, B[f"p4wh{pp}"]], writes=[B[f"p4wl{pp}"]])

    for t in range(16):
        ap_ = t % 2
        pA0, pA1 = B[f"p4ao{ap_ * 2}"], B[f"p4ao{ap_ * 2 + 1}"]
        for pp in range(PAIRS):
            cpar = pp % 2
            ch, chB = chs[cpar], B[f"p4ch{cpar}"]
            cl, clB = cls[cpar], B[f"p4cl{cpar}"]
            pr.op("ve", lambda v, ch=ch, pp=pp, t=t: v.tensor_copy(
                ch[:, :], ctxs[pp][:, t * 128:(t + 1) * 128]),
                reads=[B[f"p4c{pp}"]], writes=[chB])
            pr.op("ve", lambda v, cl=cl, ch=ch, pp=pp, t=t: v.tensor_tensor(
                cl[:, :], ctxs[pp][:, t * 128:(t + 1) * 128], ch[:, :],
                OP.subtract),
                reads=[B[f"p4c{pp}"], chB], writes=[clB])
            fns = []
            for n in range(2):
                for (ca_, wa_, wB_) in ((ch, woh[pp], B[f"p4wh{pp}"]),
                                        (ch, wol[pp], B[f"p4wl{pp}"]),
                                        (cl, woh[pp], B[f"p4wh{pp}"])):
                    st_ = (pp == 0 and ca_ is ch and wa_ is woh[pp])
                    sp_ = (pp == PAIRS - 1 and ca_ is cl)
                    fns.append(lambda te, ca_=ca_, wa_=wa_, n=n, ap_=ap_, st_=st_, sp_=sp_:
                               te.matmul(ps_ao[ap_ * 2 + n][:, :], ca_[:, :],
                                         wa_[:, n * 512:(n + 1) * 512], start=st_, stop=sp_))
            rds = [chB, clB, B[f"p4wh{pp}"], B[f"p4wl{pp}"]]
            pr.group("te", fns, reads=rds, writes=[pA0, pA1])
        xb, xbB = xb4[ap_], B[f"p4x{ap_}"]
        pr.op("sy", lambda s, xb=xb, t=t: s.dma_start(out=xb[:, :],
                                                      in_=x[t * 128:(t + 1) * 128, :]),
              writes=[xbB], dma=True)
        pr.op("ve", lambda v, ap_=ap_: v.tensor_tensor(
            yb4[:, 0:512], ps_ao[ap_ * 2][:, :], bob[:, 0:512], OP.add),
            reads=[pA0, B["bob"]], writes=[B["p4y"]])
        pr.op("ve", lambda v, ap_=ap_: v.tensor_tensor(
            yb4[:, 512:1024], ps_ao[ap_ * 2 + 1][:, :], bob[:, 512:1024], OP.add),
            reads=[pA1, B["bob"]], writes=[B["p4y"]])
        pr.op("ve", lambda v, xb=xb, t=t: v.tensor_scalar(
            zb4[:, :], xb[:, :], mu1a[:, t:t + 1], r1a[:, t:t + 1], OP.subtract, OP.mult),
            reads=[xbB, B["mu1a"]], writes=[B["p4z"]])
        pr.op("ve", lambda v: v.tensor_tensor(yb4[:, :], yb4[:, :], zb4[:, :], OP.add),
              reads=[B["p4z"], B["p4y"]], writes=[B["p4y"]])
        stt, sttB = (st0, B["st0"]) if ap_ == 0 else (st1, B["st1"])
        ln_stats(yb4, B["p4y"], stt, sttB, EPS2)
        pr.op("ve", lambda v, stt=stt: v.tensor_scalar(
            zb4[:, :], yb4[:, :], stt[2][:, 0:1], stt[4][:, 0:1], OP.subtract, OP.mult),
            reads=[B["p4y"], sttB], writes=[B["p4z"]])
        pr.op("ve", lambda v: v.tensor_mul(zb4[:, :], zb4[:, :], g2b[:, :]),
              reads=[B["g2b"], B["p4z"]], writes=[B["p4z"]])
        pr.op("ve", lambda v: v.tensor_tensor(zb4[:, :], zb4[:, :], be2b[:, :], OP.add),
              reads=[B["be2b"], B["p4z"]], writes=[B["p4z"]])
        ln_stats(zb4, B["p4z"], stt, sttB, EPS)
        pr.op("ve", lambda v, stt=stt: v.tensor_scalar(
            yb4[:, :], zb4[:, :], stt[2][:, 0:1], stt[4][:, 0:1], OP.subtract, OP.mult),
            reads=[B["p4z"], sttB], writes=[B["p4y"]])
        pr.op("ve", lambda v, xb=xb: v.tensor_tensor(yb4[:, :], yb4[:, :], xb[:, :], OP.add),
              reads=[xbB, B["p4y"]], writes=[B["p4y"]])
        pr.op("gp", lambda g, t=t: g.dma_start(out=x1_h[t * 128:(t + 1) * 128, :],
                                               in_=yb4[:, :]),
              reads=[B["p4y"]], writes=[B["x1_h"]], dma=True)
        for _fc in range(4 * t, 4 * t + 4):
            if _fc < 32:
                terntile(0, _fc)
            else:
                terntile(1, _fc - 32)

    pr.barrier()
    p5a.close()
    p4.close()



    p5 = contextlib.ExitStack()
    b2b = sb("b2b", [128, D], F32, p5)
    bcast_load("b2b", b2b, b2_v)
    x5 = [sb(f"p5x{i}", [128, 1024], F32, p5) for i in range(2)]
    u5 = sb("p5u", [128, 1024], F32, p5)
    u2h = [sb(f"p5uh{c}", [128, MEG], BF16, p5) for c in range(DC)]
    u2l = [sb(f"p5ul{c}", [128, MEG], BF16, p5) for c in range(DC)]
    w1qS = [sb(f"p5w1{c}", [128, 1024], BF16, p5) for c in range(DC)]
    w2qS = [sb(f"p5w2{c}", [128, 1024], BF16, p5) for c in range(DC)]
    hf5 = [sb(f"p5hf{i}", [128, 256], F32, p5) for i in range(2)]
    hh5 = [sb(f"p5hh{i}", [128, 256], BF16, p5) for i in range(2)]
    hl5 = [sb(f"p5hl{i}", [128, 256], BF16, p5) for i in range(2)]
    o2sb = [sb(f"p5o{i}", [128, 1024], F32, p5) for i in range(8)]
    ps_t5 = [ps(f"p5pt{i}", [128, 128], F32, p5) for i in range(2)]
    psH = [ps(f"p5ph{i}", [128, 256], F32, p5) for i in range(2)]
    psO = [ps(f"p5po{i}", [128, 512], F32, p5) for i in range(4)]
    nb("p5x0", "p5x1", "p5u", "p5pt0", "p5pt1", "p5ph0", "p5ph1",
       "p5po0", "p5po1", "p5po2", "p5po3",
       "p5hf0", "p5hf1", "p5hh0", "p5hh1", "p5hl0", "p5hl1")
    for c in range(DC):
        nb(f"p5uh{c}", f"p5ul{c}", f"p5w1{c}", f"p5w2{c}")
    for i in range(8):
        nb(f"p5o{i}")

    u2b = [B[f"p5uh{c}"] for c in range(DC)] + [B[f"p5ul{c}"] for c in range(DC)]
    w1b_ = [B[f"p5w1{c}"] for c in range(DC)]
    w2b_ = [B[f"p5w2{c}"] for c in range(DC)]

    for M in range(2):
        for ts in range(8):
            t = M * 8 + ts
            par = ts % 2
            xb, xbB = x5[par], B[f"p5x{par}"]
            pr.op("sy", lambda s, xb=xb, t=t: s.dma_start(
                out=xb[:, :], in_=x1_h[t * 128:(t + 1) * 128, :]),
                reads=[B["x1_h"]], writes=[xbB], dma=True)
            stt, sttB = (st0, B["st0"]) if par == 0 else (st1, B["st1"])
            ln_stats(xb, xbB, stt, sttB, EPS)
            pr.op("ve", lambda v, xb=xb, stt=stt: v.tensor_scalar(
                u5[:, :], xb[:, :], stt[2][:, 0:1], stt[4][:, 0:1], OP.subtract, OP.mult),
                reads=[xbB, sttB], writes=[B["p5u"]])
            for c in range(DC):
                tp = (ts * DC + c) % 2
                pr.op("te", lambda te, c=c, tp=tp: te.transpose(
                    ps_t5[tp][:, :], u5[:, c * 128:(c + 1) * 128], identf[:, :]),
                    reads=[B["p5u"], B["identf"]], writes=[B[f"p5pt{tp}"]])
                pr.op("sc", lambda s, c=c, ts=ts, tp=tp: s.activation(
                    u2h[c][:, ts * 128:(ts + 1) * 128], ps_t5[tp][:, :], AF.Copy),
                    reads=[B[f"p5pt{tp}"]], writes=[B[f"p5uh{c}"]])
                pr.op("ve", lambda v, c=c, ts=ts, tp=tp: v.tensor_tensor(
                    u2l[c][:, ts * 128:(ts + 1) * 128], ps_t5[tp][:, :],
                    u2h[c][:, ts * 128:(ts + 1) * 128], OP.subtract),
                    reads=[B[f"p5pt{tp}"], B[f"p5uh{c}"]], writes=[B[f"p5ul{c}"]])
        for sup in range(4):
            for c in range(DC):
                pr.op("sy", lambda s, c=c, sup=sup: s.dma_start(
                    out=w1qS[c][:, :],
                    in_=w1qt_h[c * 128:(c + 1) * 128, sup * 1024:(sup + 1) * 1024]),
                    reads=[B["w1qt_h"]], writes=[B[f"p5w1{c}"]], dma=True)
                pr.op("sy", lambda s, c=c, sup=sup: s.dma_start(
                    out=w2qS[c][:, :],
                    in_=w2qt_h[sup * 1024 + c * 128:sup * 1024 + (c + 1) * 128, :]),
                    reads=[B["w2qt_h"]], writes=[B[f"p5w2{c}"]], dma=True)
            for ts in range(4):
                def emit_ffn2(fc, ts=ts):
                    hp = fc % 2
                    hb = [B[f"p5hh{hp}"], B[f"p5hl{hp}"]]
                    fns2 = []
                    for tsl in range(2):
                        for ha_ in (hh5[hp], hl5[hp]):
                            for n in range(2):
                                st_ = (fc == 0 and ha_ is hh5[hp])
                                sp_ = (fc == 7 and ha_ is hl5[hp])
                                fns2.append(lambda te, tsl=tsl, ha_=ha_, n=n, hp=hp, fc=fc, st_=st_, sp_=sp_:
                                            te.matmul(psO[tsl * 2 + n][:, :],
                                                      ha_[:, tsl * 128:(tsl + 1) * 128],
                                                      w2qS[fc][:, n * 512:(n + 1) * 512],
                                                      start=st_, stop=sp_))
                    pr.group("te", fns2, reads=hb + w2b_,
                             writes=[B["p5po0"], B["p5po1"], B["p5po2"], B["p5po3"]])

                for fc in range(9):
                    if fc < 8:
                        hp = fc % 2
                        pH = B[f"p5ph{hp}"]
                        fns = []
                        for c in range(DC):
                            for ua_ in (u2h, u2l):
                                st_ = (c == 0 and ua_ is u2h)
                                sp_ = (c == DC - 1 and ua_ is u2l)
                                fns.append(lambda te, c=c, ua_=ua_, fc=fc, ts=ts, hp=hp, st_=st_, sp_=sp_:
                                           te.matmul(psH[hp][:, :],
                                                     w1qS[c][:, fc * 128:(fc + 1) * 128],
                                                     ua_[c][:, ts * 256:(ts + 1) * 256],
                                                     start=st_, stop=sp_))
                        pr.group("te", fns, reads=w1b_ + u2b, writes=[pH])
                        fc_g = sup * 8 + fc
                        pr.op("sc", lambda s, hp=hp, fc_g=fc_g: s.activation(
                            hf5[hp][:, :], psH[hp][:, :], AF.Relu, bias=b1s[:, fc_g:fc_g + 1]),
                            reads=[pH, B["b1s"]], writes=[B[f"p5hf{hp}"]])
                        pr.op("sc", lambda s, hp=hp: s.activation(
                            hh5[hp][:, :], hf5[hp][:, :], AF.Copy),
                            reads=[B[f"p5hf{hp}"]], writes=[B[f"p5hh{hp}"]])
                        pr.op("ve", lambda v, hp=hp: v.tensor_tensor(
                            hl5[hp][:, :], hf5[hp][:, :], hh5[hp][:, :], OP.subtract),
                            reads=[B[f"p5hf{hp}"], B[f"p5hh{hp}"]], writes=[B[f"p5hl{hp}"]])
                    if fc >= 1:
                        emit_ffn2(fc - 1)
                for tsl in range(2):
                    oi = ts * 2 + tsl
                    for n in range(2):
                        if sup == 0:
                            pr.op("ve", lambda v, oi=oi, tsl=tsl, n=n: v.tensor_copy(
                                o2sb[oi][:, n * 512:(n + 1) * 512], psO[tsl * 2 + n][:, :]),
                                reads=[B[f"p5po{tsl * 2 + n}"]], writes=[B[f"p5o{oi}"]])
                        else:
                            pr.op("ve", lambda v, oi=oi, tsl=tsl, n=n: v.tensor_tensor(
                                o2sb[oi][:, n * 512:(n + 1) * 512],
                                o2sb[oi][:, n * 512:(n + 1) * 512],
                                psO[tsl * 2 + n][:, :], OP.add),
                                reads=[B[f"p5po{tsl * 2 + n}"]], writes=[B[f"p5o{oi}"]])
        for ti in range(8):
            t = M * 8 + ti
            ob, obB = o2sb[ti], B[f"p5o{ti}"]
            pr.op("ve", lambda v, ob=ob: v.tensor_tensor(ob[:, :], ob[:, :], b2b[:, :], OP.add),
                  reads=[B["b2b"]], writes=[obB])
            stt, sttB = (st0, B["st0"]) if ti % 2 == 0 else (st1, B["st1"])
            ln_stats(ob, obB, stt, sttB, EPS)
            par = ti % 2
            xb, xbB = x5[par], B[f"p5x{par}"]
            pr.op("sy", lambda s, xb=xb, t=t: s.dma_start(
                out=xb[:, :], in_=x1_h[t * 128:(t + 1) * 128, :]),
                reads=[B["x1_h"]], writes=[xbB], dma=True)
            pr.op("ve", lambda v, ob=ob, stt=stt: v.tensor_scalar(
                u5[:, :], ob[:, :], stt[2][:, 0:1], stt[4][:, 0:1], OP.subtract, OP.mult),
                reads=[obB, sttB], writes=[B["p5u"]])
            pr.op("ve", lambda v, xb=xb: v.tensor_tensor(u5[:, :], u5[:, :], xb[:, :], OP.add),
                  reads=[xbB, B["p5u"]], writes=[B["p5u"]])
            pr.op("gp", lambda g, t=t: g.dma_start(out=y[t * 128:(t + 1) * 128, :],
                                                   in_=u5[:, :]),
                  reads=[B["p5u"]], writes=[B["y"]], dma=True)

    pr._wait("gp", "gp", pr.cnt["gp"])
    pr.op("gp", lambda g: g.memset(gsc[0:1, 4:5], 0.0), writes=[B["gsc"]])

    pr.emit(es)
    return nc, es


_CACHE = {}


def kernel(**inputs):
    x_full = np.asarray(inputs["L0_attn_x"], np.float32)
    wq = np.ascontiguousarray(np.asarray(inputs["L0_attn_Wq"], np.float32).reshape(D, D))
    wk = np.ascontiguousarray(np.asarray(inputs["L0_attn_Wk"], np.float32).reshape(D, D))
    wv = np.ascontiguousarray(np.asarray(inputs["L0_attn_Wv"], np.float32).reshape(D, D))
    wo = np.ascontiguousarray(np.asarray(inputs["L0_attn_Wo"], np.float32).reshape(D, D))
    bqt = np.ascontiguousarray(np.asarray(inputs["L0_attn_bq"], np.float32).reshape(8, 128).T)
    bkt = np.ascontiguousarray(np.asarray(inputs["L0_attn_bk"], np.float32).reshape(8, 128).T)
    bv_flat = np.asarray(inputs["L0_attn_bv"], np.float32).reshape(D)
    bo = np.asarray(inputs["L0_attn_bo"], np.float32).reshape(D)
    g2 = np.asarray(inputs["L0_attn_gamma"], np.float32).reshape(D)
    be2 = np.asarray(inputs["L0_attn_beta"], np.float32).reshape(D)
    w1 = np.asarray(inputs["L0_ffn_W1"], np.float32)
    b1t = np.ascontiguousarray(np.asarray(inputs["L0_ffn_b1"], np.float32).reshape(32, 128).T)
    w2 = np.asarray(inputs["L0_ffn_W2"], np.float32)
    b2 = np.asarray(inputs["L0_ffn_b2"], np.float32).reshape(D)

    if "nc" not in _CACHE:
        _CACHE["nc"] = build()
    nc, _ = _CACHE["nc"]

    in_maps = []
    for c in range(8):
        b, h = c // 2, c % 2
        own = x_full[b, h * 2048:(h + 1) * 2048]
        oth = x_full[b, (1 - h) * 2048:(2 - h) * 2048]
        xr = np.ascontiguousarray(np.concatenate([own, oth], axis=0))
        in_maps.append({
            "x": xr, "wq": wq, "wk": wk, "wv": wv, "wo": wo,
            "bqt": bqt, "bkt": bkt, "bv_v": bv_flat, "bo_v": bo,
            "g2_v": g2, "be2_v": be2, "b2_v": b2, "b1t": b1t,
            "w1": w1, "w2": w2,
        })
    res = run_bass_kernel_spmd(nc, in_maps, core_ids=list(range(8)))
    out = np.empty((4, 4096, 1024), np.float32)
    for c in range(8):
        b, h = c // 2, c % 2
        out[b, h * 2048:(h + 1) * 2048] = res.results[c]["y"]
    return out


# version 17
# speedup vs baseline: 1.1229x; 1.0290x over previous
import os
import contextlib
import numpy as np
import concourse.bass as bass
import concourse.mybir as mybir
from concourse.bass_utils import run_bass_kernel_spmd

F32 = mybir.dt.float32
F32R = mybir.dt.float32r
BF16 = mybir.dt.bfloat16
AF = mybir.ActivationFunctionType
OP = mybir.AluOpType
AX = mybir.AxisListType.X

D = 1024
DC = 8
S = 4096
OWN = 2048
PAIRS = 8
DFF = 4096
MEG = 1024
NMEG = 4
EPS = 1e-5
EPS2 = 1e-3
EPSQ = 1e-5

ENGS = ("sy", "sc", "ve", "te", "gp")
BLKA = {"sy": "sync", "sc": "scalar", "ve": "vector", "te": "tensor", "gp": "gpsimd"}


class Buf:
    __slots__ = ("w", "r")

    def __init__(self):
        self.w = None
        self.r = {}


class Prog:
    def __init__(self, nc):
        self.nc = nc
        self.ops = {e: [] for e in ENGS}
        self.cnt = {e: 0 for e in ENGS}
        self.pend = {e: [] for e in ENGS}
        self.waited = {e: {} for e in ENGS}

    def _wait(self, eng, sem, val):
        if val <= 0:
            return
        w = self.waited[eng]
        if w.get(sem, 0) >= val:
            return
        w[sem] = val
        self.pend[eng].append((sem, val))

    def op(self, eng, fn, reads=(), writes=(), prewrites=(), dma=False, quiet=False):
        for b in reads:
            if b.w is not None and b.w[0] != eng:
                self._wait(eng, b.w[0], b.w[1])
            elif b.w is not None:
                self._wait(eng, b.w[0], b.w[1])
        for b in prewrites:
            if b.w is not None:
                self._wait(eng, b.w[0], b.w[1])
            for re_, rv in b.r.items():
                self._wait(eng, re_, rv)
        for b in writes:
            if b.w is not None:
                self._wait(eng, b.w[0], b.w[1])
            for re_, rv in b.r.items():
                if re_ != eng or True:
                    self._wait(eng, re_, rv)
        amount = 0 if quiet else (16 if dma else 1)
        waits, self.pend[eng] = self.pend[eng], []
        self.ops[eng].append((fn, waits, amount))
        if amount:
            self.cnt[eng] += amount
        if not quiet:
            c = self.cnt[eng]
            for b in reads:
                b.r[eng] = c
            for b in writes:
                b.w = (eng, c)
                b.r = {}

    def group(self, eng, fns, reads=(), writes=()):
        self.op(eng, fns[0], reads=reads, prewrites=writes, quiet=True)
        for f in fns[1:-1]:
            self.op(eng, f, quiet=True)
        self.op(eng, fns[-1], reads=reads, writes=writes)

    def barrier(self):
        for e in ENGS:
            for o in ENGS:
                if o != e:
                    self._wait(e, o, self.cnt[o])

    def emit(self, es):
        nc = self.nc
        sems = {e: es.enter_context(nc.semaphore("sem_" + e)) for e in ENGS}
        with nc.Block() as blk:
            for e in ENGS:
                if not self.ops[e]:
                    continue

                def body(h, e=e):
                    for fn, waits, amount in self.ops[e]:
                        for s, v in waits:
                            h.wait_ge(sems[s], v)
                        inst = fn(h)
                        if amount:
                            inst.then_inc(sems[e], amount)

                getattr(blk, BLKA[e])(body)


def build():
    nc = bass.Bass("TRN2", target_bir_lowering=False)
    x = nc.dram_tensor("x", [S, D], F32, kind="ExternalInput").ap()
    wq = nc.dram_tensor("wq", [D, D], F32, kind="ExternalInput").ap()
    wk = nc.dram_tensor("wk", [D, D], F32, kind="ExternalInput").ap()
    wv = nc.dram_tensor("wv", [D, D], F32, kind="ExternalInput").ap()
    wo = nc.dram_tensor("wo", [D, D], F32, kind="ExternalInput").ap()
    bqt = nc.dram_tensor("bqt", [128, 8], F32, kind="ExternalInput").ap()
    bkt = nc.dram_tensor("bkt", [128, 8], F32, kind="ExternalInput").ap()
    bv_v = nc.dram_tensor("bv_v", [D], F32, kind="ExternalInput").ap()
    bo_v = nc.dram_tensor("bo_v", [D], F32, kind="ExternalInput").ap()
    g2_v = nc.dram_tensor("g2_v", [D], F32, kind="ExternalInput").ap()
    be2_v = nc.dram_tensor("be2_v", [D], F32, kind="ExternalInput").ap()
    b2_v = nc.dram_tensor("b2_v", [D], F32, kind="ExternalInput").ap()
    b1t = nc.dram_tensor("b1t", [128, 32], F32, kind="ExternalInput").ap()
    w1 = nc.dram_tensor("w1", [DFF, D], F32, kind="ExternalInput").ap()
    w2 = nc.dram_tensor("w2", [D, DFF], F32, kind="ExternalInput").ap()
    y = nc.dram_tensor("y", [OWN, D], F32, kind="ExternalOutput").ap()
    DBG = "ExternalOutput" if int(os.environ.get("KDBG", "0")) else "Internal"
    q_h = nc.dram_tensor("q_h", [D, OWN], F32, kind=DBG).ap()
    k_h = nc.dram_tensor("k_h", [D, S], F32, kind=DBG).ap()
    v_h = nc.dram_tensor("v_h", [S, D], F32, kind=DBG).ap()
    ctx_h = nc.dram_tensor("ctx_h", [D, OWN], F32, kind=DBG).ap()
    x1_h = nc.dram_tensor("x1_h", [OWN, D], F32, kind=DBG).ap()
    w1qt_h = nc.dram_tensor("w1qt_h", [D, DFF], BF16, kind="Internal").ap()
    w2qt_h = nc.dram_tensor("w2qt_h", [DFF, D], BF16, kind="Internal").ap()
    bc_h = nc.dram_tensor("bc_h", [8, 512], F32, kind="Internal").ap()

    es = contextlib.ExitStack()
    pr = Prog(nc)
    PH = int(os.environ.get("KPHASES", "5"))
    P3L = int(os.environ.get("KP3LVL", "8"))
    P3P = int(os.environ.get("KP3PAIRS", "8"))
    P3I = int(os.environ.get("KP3ICS", "4"))
    P3J = int(os.environ.get("KP3JS", "32"))
    NOCB = int(os.environ.get("KP3NOCB", "0"))
    NOVP = int(os.environ.get("KP3NOVP", "0"))

    def sb(n, shape, dt, stack=None):
        return (stack or es).enter_context(nc.sbuf_tensor(n, shape, dt))

    def ps(n, shape, dt, stack=None):
        return (stack or es).enter_context(nc.psum_tensor(n, shape, dt))

    B = {}

    def nb(*names):
        for n in names:
            B[n] = Buf()

    iot = sb("iot", [128, 128], F32)
    pidx = sb("pidx", [128, 1], F32)
    identf = sb("identf", [128, 128], F32)
    identr = sb("identr", [128, 128], BF16)
    ones1 = sb("ones1", [128, 1], F32)
    onesr = sb("onesr", [1, 128], F32)
    bqs = sb("bqs", [128, 8], F32)
    bks = sb("bks", [128, 8], F32)
    b1s = sb("b1s", [128, 32], F32)
    mu1a = sb("mu1a", [128, 16], F32)
    r1a = sb("r1a", [128, 16], F32)
    sqdump = sb("sqdump", [128, 1024], F32)
    thrb = sb("thrb", [128, 4], F32)
    asum = sb("asum", [128, 64], F32)
    gsc = sb("gsc", [1, 8], F32)
    st0 = [sb(f"st0_{i}", [128, 1], F32) for i in range(5)]
    st1 = [sb(f"st1_{i}", [128, 1], F32) for i in range(5)]
    epsb = sb("epsb", [128, 2], F32)
    nb("iot", "pidx", "identf", "identr", "ones1", "bvb", "bob", "g2b", "be2b",
       "b2b", "bqs", "bks", "b1s", "mu1a", "r1a", "sqdump", "thrb", "asum",
       "gsc", "st0", "st1", "epsb", "onesr", "q_h", "k_h", "v_h", "ctx_h", "x1_h", "w1qt_h",
       "w2qt_h", "y")
    for i in range(8):
        nb(f"bch{i}")

    pr.op("gp", lambda g: g.iota(iot[:, :], [[1, 128]], base=0, channel_multiplier=0,
                                 allow_small_or_imprecise_dtypes=True), writes=[B["iot"]])
    pr.op("gp", lambda g: g.iota(pidx[:, :], [[0, 1]], base=0, channel_multiplier=1,
                                 allow_small_or_imprecise_dtypes=True), writes=[B["pidx"]])
    pr.op("ve", lambda v: v.tensor_scalar(identf[:, :], iot[:, :], pidx[:, 0:1], None, OP.is_equal),
          reads=[B["iot"], B["pidx"]], writes=[B["identf"]])
    pr.op("ve", lambda v: v.tensor_scalar(identr[:, :], iot[:, :], pidx[:, 0:1], None, OP.is_equal),
          reads=[B["iot"], B["pidx"]], writes=[B["identr"]])
    pr.op("ve", lambda v: v.memset(ones1[:, :], 1.0), writes=[B["ones1"]])
    pr.op("ve", lambda v: v.memset(onesr[0:1, :], 1.0), writes=[B["onesr"]])
    pr.op("ve", lambda v: v.memset(epsb[:, 0:1], EPS), writes=[B["epsb"]])
    pr.op("ve", lambda v: v.memset(epsb[:, 1:2], EPS2), writes=[B["epsb"]])
    def bcast_load(nm, buf, vec):
        pr.op("sy", lambda s, buf=buf, vec=vec: s.dma_start(
            out=buf[:, :], in_=bass.AP(vec.tensor, 0, [[0, 128], [1, D]])),
            writes=[B[nm]], dma=True)

    pr.op("sy", lambda s: s.dma_start(out=bqs[:, :], in_=bqt[:, :]), writes=[B["bqs"]], dma=True)
    pr.op("sy", lambda s: s.dma_start(out=bks[:, :], in_=bkt[:, :]), writes=[B["bks"]], dma=True)
    pr.op("sy", lambda s: s.dma_start(out=b1s[:, :], in_=b1t[:, :]), writes=[B["b1s"]], dma=True)

    w1b = [sb(f"p1w{i}", [128, 1024], F32) for i in range(2)]
    nb("p1w0", "p1w1")

    def p1_batch(idx):
        for t in range(4 * idx, 4 * idx + 4):
            par = t % 2
            wb, wbB = w1b[par], B[f"p1w{par}"]
            if t < 32:
                src = w1[t * 128:(t + 1) * 128, :]
            else:
                t2 = t - 32
                r, cc = t2 // 4, t2 % 4
                src = w2[r * 128:(r + 1) * 128, cc * 1024:(cc + 1) * 1024]
            pr.op("gp", lambda g, wb=wb, src=src: g.dma_start(out=wb[:, :], in_=src),
                  writes=[wbB], dma=True)
            pr.op("sc", lambda s, wb=wb, t=t: s.activation(
                sqdump[:, :], wb[:, :], AF.Abs, accum_out=asum[:, t:t + 1]),
                reads=[wbB], writes=[B["sqdump"], B["asum"]])

    def ln_stats(xb, xbB, stt, sttB, eps, save=None):
        ssq, ssum, mu, var, rstd = stt
        pr.op("sc", lambda s: s.activation(sqdump[:, :], xb[:, :], AF.Square,
                                           accum_out=ssq[:, 0:1]),
              reads=[xbB], writes=[B["sqdump"], sttB])
        pr.op("ve", lambda v: v.reduce_sum(ssum[:, 0:1], xb[:, :], axis=AX),
              reads=[xbB], writes=[sttB])
        pr.op("ve", lambda v: v.tensor_scalar_mul(mu[:, 0:1], ssum[:, 0:1], 1.0 / D),
              reads=[sttB], writes=[sttB])
        pr.op("ve", lambda v: v.tensor_scalar_mul(var[:, 0:1], ssq[:, 0:1], 1.0 / D),
              reads=[sttB], writes=[sttB])
        pr.op("ve", lambda v: v.tensor_mul(ssum[:, 0:1], mu[:, 0:1], mu[:, 0:1]),
              reads=[sttB], writes=[sttB])
        pr.op("ve", lambda v: v.tensor_sub(var[:, 0:1], var[:, 0:1], ssum[:, 0:1]),
              reads=[sttB], writes=[sttB])
        ecol = 0 if eps == EPS else 1
        pr.op("sc", lambda s, ecol=ecol: s.activation(ssum[:, 0:1], var[:, 0:1], AF.Sqrt,
                                                      bias=epsb[:, ecol:ecol + 1]),
              reads=[sttB, B["epsb"]], writes=[sttB])
        pr.op("ve", lambda v: v.reciprocal(rstd[:, 0:1], ssum[:, 0:1]),
              reads=[sttB], writes=[sttB])
        if save is not None:
            sv, svB, col = save
            pr.op("ve", lambda v: v.tensor_copy(sv[0][:, col:col + 1], mu[:, 0:1]),
                  reads=[sttB], writes=[svB])
            pr.op("ve", lambda v: v.tensor_copy(sv[1][:, col:col + 1], rstd[:, 0:1]),
                  reads=[sttB], writes=[svB])

    p2 = contextlib.ExitStack()
    bvb = sb("bvb", [128, D], F32, p2)
    bcast_load("bvb", bvb, bv_v)
    xt2 = [sb(f"p2x{i}", [128, 1024], F32, p2) for i in range(2)]
    ut2 = sb("p2u", [128, 1024], F32, p2)
    uTh = [sb(f"p2uth{c}", [128, MEG], BF16, p2) for c in range(DC)]
    uTl = [sb(f"p2utl{c}", [128, MEG], BF16, p2) for c in range(DC)]
    wraw = [sb(f"p2wr{i}", [128, 1024], F32, p2) for i in range(2)]
    wh = [sb(f"p2wh{c}", [128, 1024], BF16, p2) for c in range(DC)]
    wl = [sb(f"p2wl{c}", [128, 1024], BF16, p2) for c in range(DC)]
    qev = [sb(f"p2qe{i}", [128, 512], F32, p2) for i in range(2)]
    ps_t = [ps(f"p2pt{i}", [128, 128], F32, p2) for i in range(4)]
    ps_qk = [ps(f"p2pq{i}", [128, 512], F32, p2) for i in range(2)]
    ps_v = [ps(f"p2pv{i}", [128, 512], F32, p2) for i in range(2)]
    nb("p2x0", "p2x1", "p2u", "p2qe0", "p2qe1", "p2pt0", "p2pt1", "p2pt2", "p2pt3",
       "p2pq0", "p2pq1", "p2pv0", "p2pv1", "p2wr0", "p2wr1")
    for c in range(DC):
        nb(f"p2uth{c}", f"p2utl{c}", f"p2wh{c}", f"p2wl{c}")

    for M in range(NMEG):
        p1_batch(M * 4 + 0)
        for ts in range(8):
            t = M * 8 + ts
            par = ts % 2
            xb, xbB = xt2[par], B[f"p2x{par}"]
            pr.op("sy", lambda s, xb=xb, t=t: s.dma_start(
                out=xb[:, :], in_=x[t * 128:(t + 1) * 128, :]), writes=[xbB], dma=True)
            stt, sttB = (st0, B["st0"]) if par == 0 else (st1, B["st1"])
            save = ((mu1a, r1a), B["mu1a"], t) if t < 16 else None
            ln_stats(xb, xbB, stt, sttB, EPS, save=save)
            pr.op("ve", lambda v, xb=xb, stt=stt: v.tensor_scalar(
                ut2[:, :], xb[:, :], stt[2][:, 0:1], stt[4][:, 0:1], OP.subtract, OP.mult),
                reads=[xbB, sttB], writes=[B["p2u"]])
            for c in range(DC):
                pp = (ts * DC + c) % 4
                pr.op("te", lambda te, c=c, pp=pp: te.transpose(
                    ps_t[pp][:, :], ut2[:, c * 128:(c + 1) * 128], identf[:, :]),
                    reads=[B["p2u"], B["identf"]], writes=[B[f"p2pt{pp}"]])
                pr.op("sc", lambda s, c=c, ts=ts, pp=pp: s.activation(
                    uTh[c][:, ts * 128:(ts + 1) * 128], ps_t[pp][:, :], AF.Copy),
                    reads=[B[f"p2pt{pp}"]], writes=[B[f"p2uth{c}"]])
                pr.op("ve", lambda v, c=c, ts=ts, pp=pp: v.tensor_tensor(
                    uTl[c][:, ts * 128:(ts + 1) * 128], ps_t[pp][:, :],
                    uTh[c][:, ts * 128:(ts + 1) * 128], OP.subtract),
                    reads=[B[f"p2pt{pp}"], B[f"p2uth{c}"]], writes=[B[f"p2utl{c}"]])
        p1_batch(M * 4 + 1)
        wlist = [("q", wq, q_h, bqs, B["bqs"]), ("k", wk, k_h, bks, B["bks"]),
                 ("v", wv, None, None, None)]
        if M >= 2:
            wlist = wlist[1:]
        for wi, (kind, wsrc, dsth, bias, biasB) in enumerate(wlist):
            for c in range(DC):
                par = c % 2
                wb, wbB = wraw[par], B[f"p2wr{par}"]
                pr.op("sy", lambda s, wb=wb, wsrc=wsrc, c=c: s.dma_start(
                    out=wb[:, :], in_=wsrc[c * 128:(c + 1) * 128, :]), writes=[wbB], dma=True)
                pr.op("ve", lambda v, c=c, wb=wb: v.tensor_copy(wh[c][:, :], wb[:, :]),
                      reads=[wbB], writes=[B[f"p2wh{c}"]])
                pr.op("ve", lambda v, c=c, wb=wb: v.tensor_tensor(
                    wl[c][:, :], wb[:, :], wh[c][:, :], OP.subtract),
                    reads=[wbB, B[f"p2wh{c}"]], writes=[B[f"p2wl{c}"]])
            wrb = [B[f"p2wh{c}"] for c in range(DC)] + [B[f"p2wl{c}"] for c in range(DC)]
            uTb = [B[f"p2uth{c}"] for c in range(DC)] + [B[f"p2utl{c}"] for c in range(DC)]
            if kind in ("q", "k"):
                for m in range(8):
                    for n in range(2):
                        par = (m * 2 + n) % 2
                        pb = B[f"p2pq{par}"]
                        fns = []
                        for c in range(DC):
                            for (a_, b_) in ((wh, uTh), (wh, uTl), (wl, uTh)):
                                st_ = (c == 0 and a_ is wh and b_ is uTh)
                                sp_ = (c == DC - 1 and a_ is wl)
                                fns.append(lambda te, c=c, a_=a_, b_=b_, m=m, n=n, par=par, st_=st_, sp_=sp_:
                                           te.matmul(ps_qk[par][:, :],
                                                     a_[c][:, m * 128:(m + 1) * 128],
                                                     b_[c][:, n * 512:(n + 1) * 512],
                                                     start=st_, stop=sp_))
                        pr.group("te", fns, reads=wrb + uTb, writes=[pb])
                        qpar = (m * 2 + n) % 2
                        qb, qbB = qev[qpar], B[f"p2qe{qpar}"]
                        pr.op("ve", lambda v, par=par, qb=qb, bias=bias, m=m: v.tensor_scalar(
                            qb[:, :], ps_qk[par][:, :], bias[:, m:m + 1], None, OP.add),
                            reads=[pb, biasB], writes=[qbB])
                        tok0 = M * MEG + n * 512
                        pr.op("gp", lambda g, qb=qb, dsth=dsth, m=m, tok0=tok0: g.dma_start(
                            out=dsth[m * 128:(m + 1) * 128, tok0:tok0 + 512], in_=qb[:, :]),
                            reads=[qbB], writes=[B["q_h" if dsth is q_h else "k_h"]], dma=True)
            else:
                for ts in range(8):
                    for n in range(2):
                        par = (ts * 2 + n) % 2
                        pb = B[f"p2pv{par}"]
                        fns = []
                        for c in range(DC):
                            for (a_, b_) in ((uTh, wh), (uTh, wl), (uTl, wh)):
                                st_ = (c == 0 and a_ is uTh and b_ is wh)
                                sp_ = (c == DC - 1 and a_ is uTl)
                                fns.append(lambda te, c=c, a_=a_, b_=b_, ts=ts, n=n, par=par, st_=st_, sp_=sp_:
                                           te.matmul(ps_v[par][:, :],
                                                     a_[c][:, ts * 128:(ts + 1) * 128],
                                                     b_[c][:, n * 512:(n + 1) * 512],
                                                     start=st_, stop=sp_))
                        pr.group("te", fns, reads=wrb + uTb, writes=[pb])
                        qpar = (ts * 2 + n) % 2
                        qb, qbB = qev[qpar], B[f"p2qe{qpar}"]
                        pr.op("ve", lambda v, par=par, qb=qb, n=n: v.tensor_tensor(
                            qb[:, :], ps_v[par][:, :], bvb[:, n * 512:(n + 1) * 512], OP.add),
                            reads=[pb, B["bvb"]], writes=[qbB])
                        tok0 = M * MEG + ts * 128
                        pr.op("gp", lambda g, qb=qb, ts=ts, n=n, tok0=tok0: g.dma_start(
                            out=v_h[tok0:tok0 + 128, n * 512:(n + 1) * 512], in_=qb[:, :]),
                            reads=[qbB], writes=[B["v_h"]], dma=True)
        p1_batch(M * 4 + 2)
        p1_batch(M * 4 + 3)

    pr.op("ve", lambda v: v.reduce_sum(st0[0][:, 0:1], asum[:, 0:32], axis=AX),
          reads=[B["asum"]], writes=[B["st0"]])
    pr.op("ve", lambda v: v.reduce_sum(st0[1][:, 0:1], asum[:, 32:64], axis=AX),
          reads=[B["asum"]], writes=[B["st0"]])
    pr.op("te", lambda te: te.matmul(ps_t[0][0:1, 0:1], ones1[:, 0:1], st0[0][:, 0:1],
                                     start=True, stop=True),
          reads=[B["st0"], B["ones1"]], writes=[B["p2pt0"]])
    pr.op("te", lambda te: te.matmul(ps_t[1][0:1, 0:1], ones1[:, 0:1], st0[1][:, 0:1],
                                     start=True, stop=True),
          reads=[B["st0"], B["ones1"]], writes=[B["p2pt1"]])
    pr.op("ve", lambda v: v.tensor_scalar(gsc[0:1, 0:1], ps_t[0][0:1, 0:1],
                                          1.0 / (DFF * D), EPSQ, OP.mult, OP.add),
          reads=[B["p2pt0"]], writes=[B["gsc"]])
    pr.op("ve", lambda v: v.tensor_scalar(gsc[0:1, 2:3], ps_t[1][0:1, 0:1],
                                          1.0 / (DFF * D), EPSQ, OP.mult, OP.add),
          reads=[B["p2pt1"]], writes=[B["gsc"]])
    pr.op("ve", lambda v: v.tensor_scalar_mul(gsc[0:1, 0:1], gsc[0:1, 0:1], 0.5),
          reads=[B["gsc"]], writes=[B["gsc"]])
    pr.op("ve", lambda v: v.tensor_scalar_mul(gsc[0:1, 2:3], gsc[0:1, 2:3], 0.5),
          reads=[B["gsc"]], writes=[B["gsc"]])
    pr.op("ve", lambda v: v.tensor_scalar_mul(gsc[0:1, 1:2], gsc[0:1, 0:1], -1.0),
          reads=[B["gsc"]], writes=[B["gsc"]])
    pr.op("ve", lambda v: v.tensor_scalar_mul(gsc[0:1, 3:4], gsc[0:1, 2:3], -1.0),
          reads=[B["gsc"]], writes=[B["gsc"]])
    pr.op("te", lambda te: te.matmul(ps_t[0][:, 0:4], onesr[0:1, :], gsc[0:1, 0:4],
                                     start=True, stop=True),
          reads=[B["gsc"], B["onesr"]], writes=[B["p2pt0"]])
    pr.op("ve", lambda v: v.tensor_copy(thrb[:, :], ps_t[0][:, 0:4]),
          reads=[B["p2pt0"]], writes=[B["thrb"]])

    pr.barrier()
    p2.close()

    if PH < 3:
        pr._wait("gp", "gp", pr.cnt["gp"])
        pr._wait("gp", "sy", pr.cnt["sy"])
        pr.op("gp", lambda g: g.memset(gsc[0:1, 4:5], 0.0), writes=[B["gsc"]])
        pr.emit(es)
        return nc, es
    p3 = contextlib.ExitStack()
    kraw = sb("p3kr", [128, S], F32, p3)
    khh = sb("p3kh", [128, S], BF16, p3)
    kll = sb("p3kl", [128, S], BF16, p3)
    qraw = sb("p3qr", [128, OWN], F32, p3)
    qhh = sb("p3qh", [128, OWN], BF16, p3)
    qll = sb("p3ql", [128, OWN], BF16, p3)
    vr3 = sb("p3vr", [128, 32, 128], F32, p3)
    vph = sb("p3vph", [128, 32, 130], BF16, p3)
    vpl = sb("p3vpl", [128, 32, 130], BF16, p3)
    pfb = [sb(f"p3pf{i}", [128, 1024], F32, p3) for i in range(4)]
    phb = [sb(f"p3ph{i}", [128, 1024], BF16, p3) for i in range(4)]
    plb = [sb(f"p3pl{i}", [128, 1024], BF16, p3) for i in range(4)]
    maxk = sb("p3mk", [128, 1], F32, p3)
    cst = sb("p3cs", [1, 16], F32, p3)
    biasA = sb("p3ba", [128, 1], F32, p3)
    biasB = sb("p3bb", [128, 1], F32, p3)
    rs_s = sb("p3rs", [1, 512], F32, p3)
    rs_t = sb("p3rt", [1, 512], F32, p3)
    rsb = [sb(f"p3rb{i}", [64, 512], F32, p3) for i in range(4)]
    cxe = [sb(f"p3cx{i}", [64, 512], F32, p3) for i in range(4)]
    psSA = [ps(f"p3ssa{i}", [128, 512], F32, p3) for i in range(2)]
    psSB = [ps(f"p3ssb{i}", [128, 512], F32, p3) for i in range(2)]
    psCA = [ps(f"p3ca{i}", [65, 512], F32, p3) for i in range(2)]
    psCB = [ps(f"p3cb{i}", [65, 512], F32, p3) for i in range(2)]
    nb("p3kr", "p3kh", "p3kl", "p3qr", "p3qh", "p3ql", "p3vr", "p3vph", "p3vpl",
       "p3mk", "p3cs", "p3ba", "p3bb", "p3rs", "p3rt",
       "p3pf0", "p3pf1", "p3pf2", "p3pf3", "p3ph0", "p3ph1", "p3ph2", "p3ph3",
       "p3pl0", "p3pl1", "p3pl2", "p3pl3",
       "p3ssa0", "p3ssa1", "p3ssb0", "p3ssb1", "p3ca0", "p3ca1", "p3cb0", "p3cb1",
       "p3rb0", "p3rb1", "p3rb2", "p3rb3", "p3cx0", "p3cx1", "p3cx2", "p3cx3")

    for p in range(P3P):
        pr.op("sy", lambda s, p=p: s.dma_start(out=kraw[:, :],
                                               in_=k_h[p * 128:(p + 1) * 128, :]),
              reads=[B["k_h"]], writes=[B["p3kr"]], dma=True)
        pr.op("sy", lambda s, p=p: s.dma_start(out=qraw[:, :],
                                               in_=q_h[p * 128:(p + 1) * 128, :]),
              reads=[B["q_h"]], writes=[B["p3qr"]], dma=True)
        pr.op("sy", lambda s, p=p: s.dma_start(
            out=vr3[:, :, :], in_=bass.AP(v_h.tensor, p * 128,
                                          [[1024, 128], [131072, 32], [1, 128]])),
            reads=[B["v_h"]], writes=[B["p3vr"]], dma=True)
        pr.op("ve", lambda v: v.tensor_copy(khh[:, :], kraw[:, :]),
              reads=[B["p3kr"]], writes=[B["p3kh"]])
        pr.op("ve", lambda v: v.tensor_tensor(kll[:, :], kraw[:, :], khh[:, :],
                                              OP.subtract),
              reads=[B["p3kr"], B["p3kh"]], writes=[B["p3kl"]])
        pr.op("ve", lambda v: v.tensor_copy(qhh[:, :], qraw[:, :]),
              reads=[B["p3qr"]], writes=[B["p3qh"]])
        pr.op("ve", lambda v: v.tensor_tensor(qll[:, :], qraw[:, :], qhh[:, :],
                                              OP.subtract),
              reads=[B["p3qr"], B["p3qh"]], writes=[B["p3ql"]])
        pr.op("ve", lambda v: v.reduce_max(maxk[:, 0:1], kraw[:, :], axis=AX,
                                           apply_absolute_value=True),
              reads=[B["p3kr"]], writes=[B["p3mk"]])
        pr.op("sc", lambda s: s.activation(pfb[0][:, :], qraw[:, 0:1024], AF.Abs),
              reads=[B["p3qr"]], writes=[B["p3pf0"]])
        pr.op("sc", lambda s: s.activation(pfb[1][:, :], qraw[:, 1024:2048], AF.Abs),
              reads=[B["p3qr"]], writes=[B["p3pf1"]])
        if P3L < 2:
            continue
        if p == 0 and not NOVP:
            pr.op("ve", lambda v: v.memset(vph[:, :, :], 1.0), writes=[B["p3vph"]])
            pr.op("ve", lambda v: v.memset(vpl[:, :, :], 0.0), writes=[B["p3vpl"]])
        if NOVP:
            pr.op("ve", lambda v: v.tensor_copy(vph[:, 0, 0:64], vr3[:, 0, 0:64]),
                  reads=[B["p3vr"]], writes=[B["p3vph"]])
        if not NOVP:
            pr.op("ve", lambda v: v.tensor_copy(vph[:, :, 0:64], vr3[:, :, 0:64]),
              reads=[B["p3vr"]], writes=[B["p3vph"]])
            pr.op("ve", lambda v: v.tensor_copy(vph[:, :, 65:129], vr3[:, :, 64:128]),
                  reads=[B["p3vr"]], writes=[B["p3vph"]])
            pr.op("ve", lambda v: v.tensor_tensor(vpl[:, :, 0:64], vr3[:, :, 0:64],
                                                  vph[:, :, 0:64], OP.subtract),
                  reads=[B["p3vr"], B["p3vph"]], writes=[B["p3vpl"]])
            pr.op("ve", lambda v: v.tensor_tensor(vpl[:, :, 65:129], vr3[:, :, 64:128],
                                                  vph[:, :, 65:129], OP.subtract),
                  reads=[B["p3vr"], B["p3vph"]], writes=[B["p3vpl"]])
        if P3L < 3:
            continue
        if NOCB:
            pr.op("ve", lambda v: v.memset(biasA[:, 0:1], -3.0), writes=[B["p3ba"]])
            pr.op("ve", lambda v: v.memset(biasB[:, 0:1], -3.0), writes=[B["p3bb"]])
        for hh_ in range(0 if NOCB else 2):
            for ic in range(4):
                pf, pfB = (pfb[0], B["p3pf0"]) if ic < 2 else (pfb[1], B["p3pf1"])
                cslice = pf[hh_ * 64:hh_ * 64 + 64, (ic % 2) * 512:(ic % 2) * 512 + 512]
                pr.op("te", lambda te, hh_=hh_, cs_=cslice: te.matmul(
                    psSA[0][0:1, :], maxk[hh_ * 64:hh_ * 64 + 64, 0:1], cs_,
                    start=True, stop=True),
                    reads=[B["p3mk"], pfB], writes=[B["p3ssa0"]])
                pr.op("ve", lambda v, hh_=hh_, ic=ic: v.reduce_max(
                    cst[0:1, hh_ * 4 + ic:hh_ * 4 + ic + 1], psSA[0][0:1, :], axis=AX),
                    reads=[B["p3ssa0"]], writes=[B["p3cs"]])
        if not NOCB:
            pr.op("ve", lambda v: v.reduce_max(cst[0:1, 8:9], cst[0:1, 0:4], axis=AX),
                  reads=[B["p3cs"]], writes=[B["p3cs"]])
            pr.op("ve", lambda v: v.reduce_max(cst[0:1, 9:10], cst[0:1, 4:8], axis=AX),
                  reads=[B["p3cs"]], writes=[B["p3cs"]])
            pr.op("ve", lambda v: v.tensor_scalar(cst[0:1, 10:12], cst[0:1, 8:10],
                                                  -0.125 * 1.001, -0.01, OP.mult, OP.add),
                  reads=[B["p3cs"]], writes=[B["p3cs"]])
            pr.op("te", lambda te: te.matmul(psSA[0][:, 0:2], onesr[0:1, :], cst[0:1, 10:12],
                                             start=True, stop=True),
                  reads=[B["p3cs"], B["onesr"]], writes=[B["p3ssa0"]])
            pr.op("ve", lambda v: v.tensor_copy(biasA[:, 0:1], psSA[0][:, 0:1]),
                  reads=[B["p3ssa0"]], writes=[B["p3ba"]])
            pr.op("ve", lambda v: v.tensor_copy(biasB[:, 0:1], psSA[0][:, 1:2]),
                  reads=[B["p3ssa0"]], writes=[B["p3bb"]])

        if P3L < 4:
            continue
        kb = [B["p3kh"], B["p3kl"]]
        qb = [B["p3qh"], B["p3ql"]]
        vb = [B["p3vph"], B["p3vpl"]]
        evac_q = []
        for ic in range(P3I):
            cp = ic % 2
            pCA, pCB = B[f"p3ca{cp}"], B[f"p3cb{cp}"]
            def emit_pv(j):
                sp = j % 4
                pvb = [B[f"p3ph{sp}"], B[f"p3pl{sp}"]]
                fns = []
                for hh_, pC in ((0, psCA[cp]), (1, psCB[cp])):
                    c0 = hh_ * 65
                    for pi, (va_, pa_) in enumerate(((vph, phb[sp]), (vpl, phb[sp]), (vph, plb[sp]))):
                        st_ = (j == 0 and pi == 0)
                        sp_ = (j == P3J - 1 and pi == 2)
                        fns.append(lambda te, va_=va_, pa_=pa_, hh_=hh_, j=j, c0=c0, pC=pC, st_=st_, sp_=sp_:
                                   te.matmul(pC[:, :], va_[:, j, c0:c0 + 65],
                                             pa_[:, hh_ * 512:hh_ * 512 + 512],
                                             start=st_, stop=sp_))
                pr.group("te", fns, reads=vb + pvb, writes=[pCA, pCB])

            for j in range(P3J + 1):
                if j < P3J:
                    sp = j % 2
                    pSA, pSB = B[f"p3ssa{sp}"], B[f"p3ssb{sp}"]
                    i0 = ic * 512
                    fns = []
                    for (ka_, qa_) in ((khh, qhh), (khh, qll), (kll, qhh)):
                        for hh_ in range(2):
                            st_ = (ka_ is khh and qa_ is qhh)
                            sp_ = (ka_ is kll)
                            pst = (psSA if hh_ == 0 else psSB)[sp]
                            fns.append(lambda te, ka_=ka_, qa_=qa_, hh_=hh_, j=j, i0=i0, pst=pst, st_=st_, sp_=sp_:
                                       te.matmul(pst[:, :],
                                                 ka_[hh_ * 64:hh_ * 64 + 64, j * 128:(j + 1) * 128],
                                                 qa_[hh_ * 64:hh_ * 64 + 64, i0:i0 + 512],
                                                 start=st_, stop=sp_,
                                                 tile_position=(hh_ * 64, 0)))
                    pr.group("te", fns, reads=kb + qb, writes=[pSA, pSB])
                    pp3 = j % 4
                    if P3L >= 5:
                        pr.op("sc", lambda s, pp3=pp3, sp=sp: s.activation(
                            pfb[pp3][:, 0:512], psSA[sp][:, :], AF.Exp, scale=0.125,
                            bias=biasA[:, 0:1]),
                            reads=[pSA, B["p3ba"]], writes=[B[f"p3pf{pp3}"]])
                        pr.op("sc", lambda s, pp3=pp3, sp=sp: s.activation(
                            pfb[pp3][:, 512:1024], psSB[sp][:, :], AF.Exp, scale=0.125,
                            bias=biasB[:, 0:1]),
                            reads=[pSB, B["p3bb"]], writes=[B[f"p3pf{pp3}"]])
                    if P3L >= 6:
                        pr.op("ve", lambda v, pp3=pp3: v.tensor_copy(
                            phb[pp3][:, :], pfb[pp3][:, :]),
                            reads=[B[f"p3pf{pp3}"]], writes=[B[f"p3ph{pp3}"]])
                        pr.op("ve", lambda v, pp3=pp3: v.tensor_tensor(
                            plb[pp3][:, :], pfb[pp3][:, :], phb[pp3][:, :], OP.subtract),
                            reads=[B[f"p3pf{pp3}"], B[f"p3ph{pp3}"]], writes=[B[f"p3pl{pp3}"]])
                if P3L >= 7 and j >= 3:
                    emit_pv(j - 3)
                if j == P3J:
                    emit_pv(P3J - 2)
                    emit_pv(P3J - 1)
                if j == 6 and evac_q:
                    evac_q.pop(0)()
            if P3L < 8:
                continue
            def emit_evac(cp=cp, pCA=pCA, pCB=pCB, ic=ic):
              for hh_, pC, pCB_ in ((0, psCA[cp], pCA), (1, psCB[cp], pCB)):
                  rp = cp * 2 + hh_
                  rb, rbB = rsb[rp], B[f"p3rb{rp}"]
                  cx, cxB = cxe[rp], B[f"p3cx{rp}"]
                  pr.op("ve", lambda v, pC=pC: v.tensor_scalar_add(rs_s[0:1, :], pC[64:65, :], 1e-30),
                        reads=[pCB_], writes=[B["p3rs"]])
                  pr.op("ve", lambda v: v.reciprocal(rs_t[0:1, :], rs_s[0:1, :]),
                        reads=[B["p3rs"]], writes=[B["p3rt"]])
                  pr.op("ve", lambda v: v.tensor_mul(rs_s[0:1, :], rs_s[0:1, :], rs_t[0:1, :]),
                        reads=[B["p3rs"], B["p3rt"]], writes=[B["p3rs"]])
                  pr.op("ve", lambda v: v.tensor_scalar(rs_s[0:1, :], rs_s[0:1, :], -1.0, 2.0,
                                                        OP.mult, OP.add),
                        reads=[B["p3rs"]], writes=[B["p3rs"]])
                  pr.op("ve", lambda v: v.tensor_mul(rs_t[0:1, :], rs_t[0:1, :], rs_s[0:1, :]),
                        reads=[B["p3rs"], B["p3rt"]], writes=[B["p3rt"]])
                  psr, psrB = (psSA, "p3ssa") if hh_ == 0 else (psSB, "p3ssb")
                  pr.op("te", lambda te, psr=psr: te.matmul(
                      psr[0][0:64, :], onesr[0:1, 0:64], rs_t[0:1, :], start=True, stop=True),
                      reads=[B["p3rt"], B["onesr"]], writes=[B[psrB + "0"]])
                  pr.op("ve", lambda v, rb=rb, psr=psr: v.tensor_copy(rb[:, :], psr[0][0:64, :]),
                        reads=[B[psrB + "0"]], writes=[rbB])
                  pr.op("ve", lambda v, cx=cx, pC=pC, rb=rb: v.tensor_tensor(
                      cx[:, :], pC[0:64, :], rb[:, :], OP.mult),
                      reads=[pCB_, rbB], writes=[cxB])
                  r0 = p * 128 + hh_ * 64
                  pr.op("gp", lambda g, cx=cx, r0=r0, ic=ic: g.dma_start(
                      out=ctx_h[r0:r0 + 64, ic * 512:(ic + 1) * 512], in_=cx[:, :]),
                      reads=[cxB], writes=[B["ctx_h"]], dma=True)
            evac_q.append(emit_evac)

        for fn_ in evac_q:
            fn_()
        evac_q = []
    pr.barrier()
    p3.close()

    if PH < 4:
        pr._wait("gp", "gp", pr.cnt["gp"])
        pr._wait("gp", "sy", pr.cnt["sy"])
        pr.op("gp", lambda g: g.memset(gsc[0:1, 4:5], 0.0), writes=[B["gsc"]])
        pr.emit(es)
        return nc, es
    p4 = contextlib.ExitStack()
    bob = sb("bob", [128, D], F32, p4)
    g2b = sb("g2b", [128, D], F32, p4)
    be2b = sb("be2b", [128, D], F32, p4)
    bcast_load("bob", bob, bo_v)
    bcast_load("g2b", g2b, g2_v)
    bcast_load("be2b", be2b, be2_v)
    ctxs = [sb(f"p4c{pp}", [128, OWN], F32, p4) for pp in range(PAIRS)]
    woh = [sb(f"p4wh{c}", [128, 1024], BF16, p4) for c in range(DC)]
    wol = [sb(f"p4wl{c}", [128, 1024], BF16, p4) for c in range(DC)]
    wraw4 = [sb(f"p4wr{i}", [128, 1024], F32, p4) for i in range(2)]
    chs = [sb(f"p4ch{i}", [128, 128], BF16, p4) for i in range(2)]
    cls = [sb(f"p4cl{i}", [128, 128], BF16, p4) for i in range(2)]
    xb4 = [sb(f"p4x{i}", [128, 1024], F32, p4) for i in range(2)]
    yb4 = sb("p4y", [128, 1024], F32, p4)
    zb4 = sb("p4z", [128, 1024], F32, p4)
    ps_ao = [ps(f"p4ao{i}", [128, 512], F32, p4) for i in range(4)]
    nb("p4y", "p4z", "p4wr0", "p4wr1", "p4ao0", "p4ao1", "p4ao2", "p4ao3", "p4x0", "p4x1",
       "p4ch0", "p4ch1", "p4cl0", "p4cl1")
    for pp in range(PAIRS):
        nb(f"p4c{pp}", f"p4wh{pp}", f"p4wl{pp}")

    p5a = contextlib.ExitStack()
    wr5 = [sb(f"p5wr{i}", [128, 1024], F32, p5a) for i in range(2)]
    tq5 = [sb(f"p5tq{i}", [128, 1024], BF16, p5a) for i in range(2)]
    sc5 = sb("p5sc", [128, 1024], F32, p5a)
    tevac = [sb(f"p5te{i}", [128, 128], BF16, p5a) for i in range(2)]
    ps_tr = [ps(f"p5tr{i}", [128, 128], BF16, p5a) for i in range(2)]
    nb("p5wr0", "p5wr1", "p5tq0", "p5tq1", "p5sc", "p5te0", "p5te1", "p5tr0", "p5tr1")

    def terntile(widx, fc):
        par = fc % 2
        wb, wbB = wr5[par], B[f"p5wr{par}"]
        tq, tqB = tq5[par], B[f"p5tq{par}"]
        tcol = 0 if widx == 0 else 2
        if widx == 0:
            pr.op("sy", lambda s, fc=fc: s.dma_start(out=wb[:, :],
                                                     in_=w1[fc * 128:(fc + 1) * 128, :]),
                  writes=[wbB], dma=True)
        else:
            for c in range(DC):
                pr.op("sy", lambda s, fc=fc, c=c: s.dma_start(
                    out=wb[:, c * 128:(c + 1) * 128],
                    in_=w2[c * 128:(c + 1) * 128, fc * 128:(fc + 1) * 128]),
                    writes=[wbB], dma=True)
        pr.op("ve", lambda v, wb=wb, tcol=tcol: v.tensor_scalar(
            sc5[:, :], wb[:, :], thrb[:, tcol:tcol + 1], None, OP.is_gt),
            reads=[wbB, B["thrb"]], writes=[B["p5sc"]])
        pr.op("ve", lambda v, wb=wb, tcol=tcol: v.tensor_scalar(
            wb[:, :], wb[:, :], thrb[:, tcol + 1:tcol + 2], None, OP.is_lt),
            reads=[wbB, B["thrb"]], writes=[wbB])
        pr.op("ve", lambda v, wb=wb, tq=tq: v.tensor_tensor(tq[:, :], sc5[:, :], wb[:, :],
                                                            OP.subtract),
            reads=[wbB, B["p5sc"]], writes=[tqB])
        for c in range(DC):
            tp = c % 2
            pr.op("te", lambda te, tq=tq, c=c, tp=tp: te.transpose(
                ps_tr[tp][:, :], tq[:, c * 128:(c + 1) * 128], identr[:, :]),
                reads=[tqB, B["identr"]], writes=[B[f"p5tr{tp}"]])
            pr.op("ve", lambda v, tp=tp: v.tensor_copy(tevac[tp][:, :], ps_tr[tp][:, :]),
                  reads=[B[f"p5tr{tp}"]], writes=[B[f"p5te{tp}"]])
            if widx == 0:
                dst = w1qt_h[c * 128:(c + 1) * 128, fc * 128:(fc + 1) * 128]
                dB = "w1qt_h"
            else:
                dst = w2qt_h[fc * 128:(fc + 1) * 128, c * 128:(c + 1) * 128]
                dB = "w2qt_h"
            pr.op("gp", lambda g, dst=dst, tp=tp: g.dma_start(out=dst, in_=tevac[tp][:, :]),
                  reads=[B[f"p5te{tp}"]], writes=[B[dB]], dma=True)


    for pp in range(PAIRS):
        pr.op("sy", lambda s, pp=pp: s.dma_start(out=ctxs[pp][:, :],
                                                 in_=ctx_h[pp * 128:(pp + 1) * 128, :]),
              reads=[B["ctx_h"]], writes=[B[f"p4c{pp}"]], dma=True)
        par = pp % 2
        wb, wbB = wraw4[par], B[f"p4wr{par}"]
        pr.op("sy", lambda s, wb=wb, pp=pp: s.dma_start(
            out=wb[:, :], in_=wo[pp * 128:(pp + 1) * 128, :]), writes=[wbB], dma=True)
        pr.op("ve", lambda v, pp=pp, wb=wb: v.tensor_copy(woh[pp][:, :], wb[:, :]),
              reads=[wbB], writes=[B[f"p4wh{pp}"]])
        pr.op("ve", lambda v, pp=pp, wb=wb: v.tensor_tensor(
            wol[pp][:, :], wb[:, :], woh[pp][:, :], OP.subtract),
            reads=[wbB, B[f"p4wh{pp}"]], writes=[B[f"p4wl{pp}"]])

    for t in range(16):
        ap_ = t % 2
        pA0, pA1 = B[f"p4ao{ap_ * 2}"], B[f"p4ao{ap_ * 2 + 1}"]
        for pp in range(PAIRS):
            cpar = pp % 2
            ch, chB = chs[cpar], B[f"p4ch{cpar}"]
            cl, clB = cls[cpar], B[f"p4cl{cpar}"]
            pr.op("ve", lambda v, ch=ch, pp=pp, t=t: v.tensor_copy(
                ch[:, :], ctxs[pp][:, t * 128:(t + 1) * 128]),
                reads=[B[f"p4c{pp}"]], writes=[chB])
            pr.op("ve", lambda v, cl=cl, ch=ch, pp=pp, t=t: v.tensor_tensor(
                cl[:, :], ctxs[pp][:, t * 128:(t + 1) * 128], ch[:, :],
                OP.subtract),
                reads=[B[f"p4c{pp}"], chB], writes=[clB])
            fns = []
            for n in range(2):
                for (ca_, wa_, wB_) in ((ch, woh[pp], B[f"p4wh{pp}"]),
                                        (ch, wol[pp], B[f"p4wl{pp}"]),
                                        (cl, woh[pp], B[f"p4wh{pp}"])):
                    st_ = (pp == 0 and ca_ is ch and wa_ is woh[pp])
                    sp_ = (pp == PAIRS - 1 and ca_ is cl)
                    fns.append(lambda te, ca_=ca_, wa_=wa_, n=n, ap_=ap_, st_=st_, sp_=sp_:
                               te.matmul(ps_ao[ap_ * 2 + n][:, :], ca_[:, :],
                                         wa_[:, n * 512:(n + 1) * 512], start=st_, stop=sp_))
            rds = [chB, clB, B[f"p4wh{pp}"], B[f"p4wl{pp}"]]
            pr.group("te", fns, reads=rds, writes=[pA0, pA1])
        xb, xbB = xb4[ap_], B[f"p4x{ap_}"]
        pr.op("sy", lambda s, xb=xb, t=t: s.dma_start(out=xb[:, :],
                                                      in_=x[t * 128:(t + 1) * 128, :]),
              writes=[xbB], dma=True)
        pr.op("ve", lambda v, ap_=ap_: v.tensor_tensor(
            yb4[:, 0:512], ps_ao[ap_ * 2][:, :], bob[:, 0:512], OP.add),
            reads=[pA0, B["bob"]], writes=[B["p4y"]])
        pr.op("ve", lambda v, ap_=ap_: v.tensor_tensor(
            yb4[:, 512:1024], ps_ao[ap_ * 2 + 1][:, :], bob[:, 512:1024], OP.add),
            reads=[pA1, B["bob"]], writes=[B["p4y"]])
        pr.op("ve", lambda v, xb=xb, t=t: v.tensor_scalar(
            zb4[:, :], xb[:, :], mu1a[:, t:t + 1], r1a[:, t:t + 1], OP.subtract, OP.mult),
            reads=[xbB, B["mu1a"]], writes=[B["p4z"]])
        pr.op("ve", lambda v: v.tensor_tensor(yb4[:, :], yb4[:, :], zb4[:, :], OP.add),
              reads=[B["p4z"], B["p4y"]], writes=[B["p4y"]])
        stt, sttB = (st0, B["st0"]) if ap_ == 0 else (st1, B["st1"])
        ln_stats(yb4, B["p4y"], stt, sttB, EPS2)
        pr.op("ve", lambda v, stt=stt: v.tensor_scalar(
            zb4[:, :], yb4[:, :], stt[2][:, 0:1], stt[4][:, 0:1], OP.subtract, OP.mult),
            reads=[B["p4y"], sttB], writes=[B["p4z"]])
        pr.op("ve", lambda v: v.tensor_mul(zb4[:, :], zb4[:, :], g2b[:, :]),
              reads=[B["g2b"], B["p4z"]], writes=[B["p4z"]])
        pr.op("ve", lambda v: v.tensor_tensor(zb4[:, :], zb4[:, :], be2b[:, :], OP.add),
              reads=[B["be2b"], B["p4z"]], writes=[B["p4z"]])
        ln_stats(zb4, B["p4z"], stt, sttB, EPS)
        pr.op("ve", lambda v, stt=stt: v.tensor_scalar(
            yb4[:, :], zb4[:, :], stt[2][:, 0:1], stt[4][:, 0:1], OP.subtract, OP.mult),
            reads=[B["p4z"], sttB], writes=[B["p4y"]])
        pr.op("ve", lambda v, xb=xb: v.tensor_tensor(yb4[:, :], yb4[:, :], xb[:, :], OP.add),
              reads=[xbB, B["p4y"]], writes=[B["p4y"]])
        pr.op("gp", lambda g, t=t: g.dma_start(out=x1_h[t * 128:(t + 1) * 128, :],
                                               in_=yb4[:, :]),
              reads=[B["p4y"]], writes=[B["x1_h"]], dma=True)
        for _fc in range(4 * t, 4 * t + 4):
            if _fc < 32:
                terntile(0, _fc)
            else:
                terntile(1, _fc - 32)

    pr.barrier()
    p5a.close()
    p4.close()



    p5 = contextlib.ExitStack()
    b2b = sb("b2b", [128, D], F32, p5)
    bcast_load("b2b", b2b, b2_v)
    x5 = [sb(f"p5x{i}", [128, 1024], F32, p5) for i in range(2)]
    u5 = sb("p5u", [128, 1024], F32, p5)
    u2h = [sb(f"p5uh{c}", [128, MEG], BF16, p5) for c in range(DC)]
    u2l = [sb(f"p5ul{c}", [128, MEG], BF16, p5) for c in range(DC)]
    w1qS = [sb(f"p5w1{c}", [128, 1024], BF16, p5) for c in range(DC)]
    w2qS = [sb(f"p5w2{c}", [128, 1024], BF16, p5) for c in range(DC)]
    hf5 = [sb(f"p5hf{i}", [128, 256], F32, p5) for i in range(3)]
    hh5 = [sb(f"p5hh{i}", [128, 256], BF16, p5) for i in range(3)]
    hl5 = [sb(f"p5hl{i}", [128, 256], BF16, p5) for i in range(3)]
    o2sb = [sb(f"p5o{i}", [128, 1024], F32, p5) for i in range(8)]
    ps_t5 = [ps(f"p5pt{i}", [128, 128], F32, p5) for i in range(2)]
    psH = [ps(f"p5ph{i}", [128, 256], F32, p5) for i in range(2)]
    psO = [ps(f"p5po{i}", [128, 512], F32, p5) for i in range(4)]
    nb("p5x0", "p5x1", "p5u", "p5pt0", "p5pt1", "p5ph0", "p5ph1",
       "p5po0", "p5po1", "p5po2", "p5po3",
       "p5hf0", "p5hf1", "p5hf2", "p5hh0", "p5hh1", "p5hh2", "p5hl0", "p5hl1", "p5hl2")
    for c in range(DC):
        nb(f"p5uh{c}", f"p5ul{c}", f"p5w1{c}", f"p5w2{c}")
    for i in range(8):
        nb(f"p5o{i}")

    u2b = [B[f"p5uh{c}"] for c in range(DC)] + [B[f"p5ul{c}"] for c in range(DC)]
    w1b_ = [B[f"p5w1{c}"] for c in range(DC)]
    w2b_ = [B[f"p5w2{c}"] for c in range(DC)]

    for M in range(2):
        for ts in range(8):
            t = M * 8 + ts
            par = ts % 2
            xb, xbB = x5[par], B[f"p5x{par}"]
            pr.op("sy", lambda s, xb=xb, t=t: s.dma_start(
                out=xb[:, :], in_=x1_h[t * 128:(t + 1) * 128, :]),
                reads=[B["x1_h"]], writes=[xbB], dma=True)
            stt, sttB = (st0, B["st0"]) if par == 0 else (st1, B["st1"])
            ln_stats(xb, xbB, stt, sttB, EPS)
            pr.op("ve", lambda v, xb=xb, stt=stt: v.tensor_scalar(
                u5[:, :], xb[:, :], stt[2][:, 0:1], stt[4][:, 0:1], OP.subtract, OP.mult),
                reads=[xbB, sttB], writes=[B["p5u"]])
            for c in range(DC):
                tp = (ts * DC + c) % 2
                pr.op("te", lambda te, c=c, tp=tp: te.transpose(
                    ps_t5[tp][:, :], u5[:, c * 128:(c + 1) * 128], identf[:, :]),
                    reads=[B["p5u"], B["identf"]], writes=[B[f"p5pt{tp}"]])
                pr.op("sc", lambda s, c=c, ts=ts, tp=tp: s.activation(
                    u2h[c][:, ts * 128:(ts + 1) * 128], ps_t5[tp][:, :], AF.Copy),
                    reads=[B[f"p5pt{tp}"]], writes=[B[f"p5uh{c}"]])
                pr.op("ve", lambda v, c=c, ts=ts, tp=tp: v.tensor_tensor(
                    u2l[c][:, ts * 128:(ts + 1) * 128], ps_t5[tp][:, :],
                    u2h[c][:, ts * 128:(ts + 1) * 128], OP.subtract),
                    reads=[B[f"p5pt{tp}"], B[f"p5uh{c}"]], writes=[B[f"p5ul{c}"]])
        for sup in range(4):
            for c in range(DC):
                pr.op("sy", lambda s, c=c, sup=sup: s.dma_start(
                    out=w1qS[c][:, :],
                    in_=w1qt_h[c * 128:(c + 1) * 128, sup * 1024:(sup + 1) * 1024]),
                    reads=[B["w1qt_h"]], writes=[B[f"p5w1{c}"]], dma=True)
                pr.op("sy", lambda s, c=c, sup=sup: s.dma_start(
                    out=w2qS[c][:, :],
                    in_=w2qt_h[sup * 1024 + c * 128:sup * 1024 + (c + 1) * 128, :]),
                    reads=[B["w2qt_h"]], writes=[B[f"p5w2{c}"]], dma=True)
            for ts in range(4):
                def emit_ffn2(fc, ts=ts):
                    hp = fc % 3
                    hb = [B[f"p5hh{hp}"], B[f"p5hl{hp}"]]
                    fns2 = []
                    for tsl in range(2):
                        for ha_ in (hh5[hp], hl5[hp]):
                            for n in range(2):
                                st_ = (fc == 0 and ha_ is hh5[hp])
                                sp_ = (fc == 7 and ha_ is hl5[hp])
                                fns2.append(lambda te, tsl=tsl, ha_=ha_, n=n, hp=hp, fc=fc, st_=st_, sp_=sp_:
                                            te.matmul(psO[tsl * 2 + n][:, :],
                                                      ha_[:, tsl * 128:(tsl + 1) * 128],
                                                      w2qS[fc][:, n * 512:(n + 1) * 512],
                                                      start=st_, stop=sp_))
                    pr.group("te", fns2, reads=hb + w2b_,
                             writes=[B["p5po0"], B["p5po1"], B["p5po2"], B["p5po3"]])

                for fc in range(10):
                    if fc < 8:
                        hp = fc % 3
                        hp2 = fc % 2
                        pH = B[f"p5ph{hp2}"]
                        fns = []
                        for c in range(DC):
                            for ua_ in (u2h, u2l):
                                st_ = (c == 0 and ua_ is u2h)
                                sp_ = (c == DC - 1 and ua_ is u2l)
                                fns.append(lambda te, c=c, ua_=ua_, fc=fc, ts=ts, hp=hp2, st_=st_, sp_=sp_:
                                           te.matmul(psH[hp][:, :],
                                                     w1qS[c][:, fc * 128:(fc + 1) * 128],
                                                     ua_[c][:, ts * 256:(ts + 1) * 256],
                                                     start=st_, stop=sp_))
                        pr.group("te", fns, reads=w1b_ + u2b, writes=[pH])
                        fc_g = sup * 8 + fc
                        pr.op("sc", lambda s, hp=hp, hp2=hp2, fc_g=fc_g: s.activation(
                            hf5[hp][:, :], psH[hp2][:, :], AF.Relu, bias=b1s[:, fc_g:fc_g + 1]),
                            reads=[pH, B["b1s"]], writes=[B[f"p5hf{hp}"]])
                        pr.op("sc", lambda s, hp=hp: s.activation(
                            hh5[hp][:, :], hf5[hp][:, :], AF.Copy),
                            reads=[B[f"p5hf{hp}"]], writes=[B[f"p5hh{hp}"]])
                        pr.op("ve", lambda v, hp=hp: v.tensor_tensor(
                            hl5[hp][:, :], hf5[hp][:, :], hh5[hp][:, :], OP.subtract),
                            reads=[B[f"p5hf{hp}"], B[f"p5hh{hp}"]], writes=[B[f"p5hl{hp}"]])
                    if fc >= 2 and fc < 10:
                        if fc < 10 and fc - 2 < 8:
                            emit_ffn2(fc - 2)
                for tsl in range(2):
                    oi = ts * 2 + tsl
                    for n in range(2):
                        if sup == 0:
                            pr.op("ve", lambda v, oi=oi, tsl=tsl, n=n: v.tensor_copy(
                                o2sb[oi][:, n * 512:(n + 1) * 512], psO[tsl * 2 + n][:, :]),
                                reads=[B[f"p5po{tsl * 2 + n}"]], writes=[B[f"p5o{oi}"]])
                        else:
                            pr.op("ve", lambda v, oi=oi, tsl=tsl, n=n: v.tensor_tensor(
                                o2sb[oi][:, n * 512:(n + 1) * 512],
                                o2sb[oi][:, n * 512:(n + 1) * 512],
                                psO[tsl * 2 + n][:, :], OP.add),
                                reads=[B[f"p5po{tsl * 2 + n}"]], writes=[B[f"p5o{oi}"]])
        for ti in range(8):
            t = M * 8 + ti
            ob, obB = o2sb[ti], B[f"p5o{ti}"]
            pr.op("ve", lambda v, ob=ob: v.tensor_tensor(ob[:, :], ob[:, :], b2b[:, :], OP.add),
                  reads=[B["b2b"]], writes=[obB])
            stt, sttB = (st0, B["st0"]) if ti % 2 == 0 else (st1, B["st1"])
            ln_stats(ob, obB, stt, sttB, EPS)
            par = ti % 2
            xb, xbB = x5[par], B[f"p5x{par}"]
            pr.op("sy", lambda s, xb=xb, t=t: s.dma_start(
                out=xb[:, :], in_=x1_h[t * 128:(t + 1) * 128, :]),
                reads=[B["x1_h"]], writes=[xbB], dma=True)
            pr.op("ve", lambda v, ob=ob, stt=stt: v.tensor_scalar(
                u5[:, :], ob[:, :], stt[2][:, 0:1], stt[4][:, 0:1], OP.subtract, OP.mult),
                reads=[obB, sttB], writes=[B["p5u"]])
            pr.op("ve", lambda v, xb=xb: v.tensor_tensor(u5[:, :], u5[:, :], xb[:, :], OP.add),
                  reads=[xbB, B["p5u"]], writes=[B["p5u"]])
            pr.op("gp", lambda g, t=t: g.dma_start(out=y[t * 128:(t + 1) * 128, :],
                                                   in_=u5[:, :]),
                  reads=[B["p5u"]], writes=[B["y"]], dma=True)

    pr._wait("gp", "gp", pr.cnt["gp"])
    pr.op("gp", lambda g: g.memset(gsc[0:1, 4:5], 0.0), writes=[B["gsc"]])

    pr.emit(es)
    return nc, es


_CACHE = {}


def kernel(**inputs):
    x_full = np.asarray(inputs["L0_attn_x"], np.float32)
    wq = np.ascontiguousarray(np.asarray(inputs["L0_attn_Wq"], np.float32).reshape(D, D))
    wk = np.ascontiguousarray(np.asarray(inputs["L0_attn_Wk"], np.float32).reshape(D, D))
    wv = np.ascontiguousarray(np.asarray(inputs["L0_attn_Wv"], np.float32).reshape(D, D))
    wo = np.ascontiguousarray(np.asarray(inputs["L0_attn_Wo"], np.float32).reshape(D, D))
    bqt = np.ascontiguousarray(np.asarray(inputs["L0_attn_bq"], np.float32).reshape(8, 128).T)
    bkt = np.ascontiguousarray(np.asarray(inputs["L0_attn_bk"], np.float32).reshape(8, 128).T)
    bv_flat = np.asarray(inputs["L0_attn_bv"], np.float32).reshape(D)
    bo = np.asarray(inputs["L0_attn_bo"], np.float32).reshape(D)
    g2 = np.asarray(inputs["L0_attn_gamma"], np.float32).reshape(D)
    be2 = np.asarray(inputs["L0_attn_beta"], np.float32).reshape(D)
    w1 = np.asarray(inputs["L0_ffn_W1"], np.float32)
    b1t = np.ascontiguousarray(np.asarray(inputs["L0_ffn_b1"], np.float32).reshape(32, 128).T)
    w2 = np.asarray(inputs["L0_ffn_W2"], np.float32)
    b2 = np.asarray(inputs["L0_ffn_b2"], np.float32).reshape(D)

    if "nc" not in _CACHE:
        _CACHE["nc"] = build()
    nc, _ = _CACHE["nc"]

    in_maps = []
    for c in range(8):
        b, h = c // 2, c % 2
        own = x_full[b, h * 2048:(h + 1) * 2048]
        oth = x_full[b, (1 - h) * 2048:(2 - h) * 2048]
        xr = np.ascontiguousarray(np.concatenate([own, oth], axis=0))
        in_maps.append({
            "x": xr, "wq": wq, "wk": wk, "wv": wv, "wo": wo,
            "bqt": bqt, "bkt": bkt, "bv_v": bv_flat, "bo_v": bo,
            "g2_v": g2, "be2_v": be2, "b2_v": b2, "b1t": b1t,
            "w1": w1, "w2": w2,
        })
    res = run_bass_kernel_spmd(nc, in_maps, core_ids=list(range(8)))
    out = np.empty((4, 4096, 1024), np.float32)
    for c in range(8):
        b, h = c // 2, c % 2
        out[b, h * 2048:(h + 1) * 2048] = res.results[c]["y"]
    return out


# version 18 (final)
# speedup vs baseline: 1.1417x; 1.0167x over previous
import os
import contextlib
import numpy as np
import concourse.bass as bass
import concourse.mybir as mybir
from concourse.bass_utils import run_bass_kernel_spmd

F32 = mybir.dt.float32
F32R = mybir.dt.float32r
BF16 = mybir.dt.bfloat16
AF = mybir.ActivationFunctionType
OP = mybir.AluOpType
AX = mybir.AxisListType.X

D = 1024
DC = 8
S = 4096
OWN = 2048
PAIRS = 8
DFF = 4096
MEG = 1024
NMEG = 4
EPS = 1e-5
EPS2 = 1e-3
EPSQ = 1e-5

ENGS = ("sy", "sc", "ve", "te", "gp")
BLKA = {"sy": "sync", "sc": "scalar", "ve": "vector", "te": "tensor", "gp": "gpsimd"}


class Buf:
    __slots__ = ("w", "r")

    def __init__(self):
        self.w = None
        self.r = {}


class Prog:
    def __init__(self, nc):
        self.nc = nc
        self.ops = {e: [] for e in ENGS}
        self.cnt = {e: 0 for e in ENGS}
        self.pend = {e: [] for e in ENGS}
        self.waited = {e: {} for e in ENGS}

    def _wait(self, eng, sem, val):
        if val <= 0:
            return
        w = self.waited[eng]
        if w.get(sem, 0) >= val:
            return
        w[sem] = val
        self.pend[eng].append((sem, val))

    def op(self, eng, fn, reads=(), writes=(), prewrites=(), dma=False, quiet=False):
        for b in reads:
            if b.w is not None and b.w[0] != eng:
                self._wait(eng, b.w[0], b.w[1])
            elif b.w is not None:
                self._wait(eng, b.w[0], b.w[1])
        for b in prewrites:
            if b.w is not None:
                self._wait(eng, b.w[0], b.w[1])
            for re_, rv in b.r.items():
                self._wait(eng, re_, rv)
        for b in writes:
            if b.w is not None:
                self._wait(eng, b.w[0], b.w[1])
            for re_, rv in b.r.items():
                if re_ != eng or True:
                    self._wait(eng, re_, rv)
        amount = 0 if quiet else (16 if dma else 1)
        waits, self.pend[eng] = self.pend[eng], []
        self.ops[eng].append((fn, waits, amount))
        if amount:
            self.cnt[eng] += amount
        if not quiet:
            c = self.cnt[eng]
            for b in reads:
                b.r[eng] = c
            for b in writes:
                b.w = (eng, c)
                b.r = {}

    def group(self, eng, fns, reads=(), writes=()):
        self.op(eng, fns[0], reads=reads, prewrites=writes, quiet=True)
        for f in fns[1:-1]:
            self.op(eng, f, quiet=True)
        self.op(eng, fns[-1], reads=reads, writes=writes)

    def barrier(self):
        for e in ENGS:
            for o in ENGS:
                if o != e:
                    self._wait(e, o, self.cnt[o])

    def emit(self, es):
        nc = self.nc
        sems = {e: es.enter_context(nc.semaphore("sem_" + e)) for e in ENGS}
        with nc.Block() as blk:
            for e in ENGS:
                if not self.ops[e]:
                    continue

                def body(h, e=e):
                    for fn, waits, amount in self.ops[e]:
                        for s, v in waits:
                            h.wait_ge(sems[s], v)
                        inst = fn(h)
                        if amount:
                            inst.then_inc(sems[e], amount)

                getattr(blk, BLKA[e])(body)


def build():
    nc = bass.Bass("TRN2", target_bir_lowering=False)
    x = nc.dram_tensor("x", [S, D], F32, kind="ExternalInput").ap()
    wq = nc.dram_tensor("wq", [D, D], F32, kind="ExternalInput").ap()
    wk = nc.dram_tensor("wk", [D, D], F32, kind="ExternalInput").ap()
    wv = nc.dram_tensor("wv", [D, D], F32, kind="ExternalInput").ap()
    wo = nc.dram_tensor("wo", [D, D], F32, kind="ExternalInput").ap()
    bqt = nc.dram_tensor("bqt", [128, 8], F32, kind="ExternalInput").ap()
    bkt = nc.dram_tensor("bkt", [128, 8], F32, kind="ExternalInput").ap()
    bv_v = nc.dram_tensor("bv_v", [D], F32, kind="ExternalInput").ap()
    bo_v = nc.dram_tensor("bo_v", [D], F32, kind="ExternalInput").ap()
    g2_v = nc.dram_tensor("g2_v", [D], F32, kind="ExternalInput").ap()
    be2_v = nc.dram_tensor("be2_v", [D], F32, kind="ExternalInput").ap()
    b2_v = nc.dram_tensor("b2_v", [D], F32, kind="ExternalInput").ap()
    b1t = nc.dram_tensor("b1t", [128, 32], F32, kind="ExternalInput").ap()
    w1 = nc.dram_tensor("w1", [DFF, D], F32, kind="ExternalInput").ap()
    w2 = nc.dram_tensor("w2", [D, DFF], F32, kind="ExternalInput").ap()
    y = nc.dram_tensor("y", [OWN, D], F32, kind="ExternalOutput").ap()
    DBG = "ExternalOutput" if int(os.environ.get("KDBG", "0")) else "Internal"
    q_h = nc.dram_tensor("q_h", [D, OWN], F32, kind=DBG).ap()
    k_h = nc.dram_tensor("k_h", [D, S], F32, kind=DBG).ap()
    v_h = nc.dram_tensor("v_h", [S, D], F32, kind=DBG).ap()
    ctx_h = nc.dram_tensor("ctx_h", [D, OWN], F32, kind=DBG).ap()
    x1_h = nc.dram_tensor("x1_h", [OWN, D], F32, kind=DBG).ap()
    w1qt_h = nc.dram_tensor("w1qt_h", [D, DFF], BF16, kind="Internal").ap()
    w2qt_h = nc.dram_tensor("w2qt_h", [DFF, D], BF16, kind="Internal").ap()
    bc_h = nc.dram_tensor("bc_h", [8, 512], F32, kind="Internal").ap()

    es = contextlib.ExitStack()
    pr = Prog(nc)
    PH = int(os.environ.get("KPHASES", "5"))
    P3L = int(os.environ.get("KP3LVL", "8"))
    P3P = int(os.environ.get("KP3PAIRS", "8"))
    P3I = int(os.environ.get("KP3ICS", "4"))
    P3J = int(os.environ.get("KP3JS", "32"))
    NOCB = int(os.environ.get("KP3NOCB", "0"))
    NOVP = int(os.environ.get("KP3NOVP", "0"))

    def sb(n, shape, dt, stack=None):
        return (stack or es).enter_context(nc.sbuf_tensor(n, shape, dt))

    def ps(n, shape, dt, stack=None):
        return (stack or es).enter_context(nc.psum_tensor(n, shape, dt))

    B = {}

    def nb(*names):
        for n in names:
            B[n] = Buf()

    iot = sb("iot", [128, 128], F32)
    pidx = sb("pidx", [128, 1], F32)
    identf = sb("identf", [128, 128], F32)
    identr = sb("identr", [128, 128], BF16)
    ones1 = sb("ones1", [128, 1], F32)
    onesr = sb("onesr", [1, 128], F32)
    bqs = sb("bqs", [128, 8], F32)
    bks = sb("bks", [128, 8], F32)
    b1s = sb("b1s", [128, 32], F32)
    mu1a = sb("mu1a", [128, 16], F32)
    r1a = sb("r1a", [128, 16], F32)
    sqdump = sb("sqdump", [128, 1024], F32)
    thrb = sb("thrb", [128, 4], F32)
    asum = sb("asum", [128, 64], F32)
    gsc = sb("gsc", [1, 8], F32)
    st0 = [sb(f"st0_{i}", [128, 1], F32) for i in range(5)]
    st1 = [sb(f"st1_{i}", [128, 1], F32) for i in range(5)]
    epsb = sb("epsb", [128, 2], F32)
    nb("iot", "pidx", "identf", "identr", "ones1", "bvb", "bob", "g2b", "be2b",
       "b2b", "bqs", "bks", "b1s", "mu1a", "r1a", "sqdump", "thrb", "asum",
       "gsc", "st0", "st1", "epsb", "onesr", "q_h", "k_h", "v_h", "ctx_h", "x1_h", "w1qt_h",
       "w2qt_h", "y")
    for i in range(8):
        nb(f"bch{i}")

    pr.op("gp", lambda g: g.iota(iot[:, :], [[1, 128]], base=0, channel_multiplier=0,
                                 allow_small_or_imprecise_dtypes=True), writes=[B["iot"]])
    pr.op("gp", lambda g: g.iota(pidx[:, :], [[0, 1]], base=0, channel_multiplier=1,
                                 allow_small_or_imprecise_dtypes=True), writes=[B["pidx"]])
    pr.op("ve", lambda v: v.tensor_scalar(identf[:, :], iot[:, :], pidx[:, 0:1], None, OP.is_equal),
          reads=[B["iot"], B["pidx"]], writes=[B["identf"]])
    pr.op("ve", lambda v: v.tensor_scalar(identr[:, :], iot[:, :], pidx[:, 0:1], None, OP.is_equal),
          reads=[B["iot"], B["pidx"]], writes=[B["identr"]])
    pr.op("ve", lambda v: v.memset(ones1[:, :], 1.0), writes=[B["ones1"]])
    pr.op("ve", lambda v: v.memset(onesr[0:1, :], 1.0), writes=[B["onesr"]])
    pr.op("ve", lambda v: v.memset(epsb[:, 0:1], EPS), writes=[B["epsb"]])
    pr.op("ve", lambda v: v.memset(epsb[:, 1:2], EPS2), writes=[B["epsb"]])
    def bcast_load(nm, buf, vec):
        pr.op("sy", lambda s, buf=buf, vec=vec: s.dma_start(
            out=buf[:, :], in_=bass.AP(vec.tensor, 0, [[0, 128], [1, D]])),
            writes=[B[nm]], dma=True)

    pr.op("sy", lambda s: s.dma_start(out=bqs[:, :], in_=bqt[:, :]), writes=[B["bqs"]], dma=True)
    pr.op("sy", lambda s: s.dma_start(out=bks[:, :], in_=bkt[:, :]), writes=[B["bks"]], dma=True)
    pr.op("sy", lambda s: s.dma_start(out=b1s[:, :], in_=b1t[:, :]), writes=[B["b1s"]], dma=True)

    w1b = [sb(f"p1w{i}", [128, 1024], F32) for i in range(2)]
    nb("p1w0", "p1w1")

    def p1_batch(idx):
        for t in range(4 * idx, 4 * idx + 4):
            par = t % 2
            wb, wbB = w1b[par], B[f"p1w{par}"]
            if t < 32:
                src = w1[t * 128:(t + 1) * 128, :]
            else:
                t2 = t - 32
                r, cc = t2 // 4, t2 % 4
                src = w2[r * 128:(r + 1) * 128, cc * 1024:(cc + 1) * 1024]
            pr.op("gp", lambda g, wb=wb, src=src: g.dma_start(out=wb[:, :], in_=src),
                  writes=[wbB], dma=True)
            pr.op("sc", lambda s, wb=wb, t=t: s.activation(
                sqdump[:, :], wb[:, :], AF.Abs, accum_out=asum[:, t:t + 1]),
                reads=[wbB], writes=[B["sqdump"], B["asum"]])

    def ln_stats(xb, xbB, stt, sttB, eps, save=None):
        ssq, ssum, mu, var, rstd = stt
        pr.op("sc", lambda s: s.activation(sqdump[:, :], xb[:, :], AF.Square,
                                           accum_out=ssq[:, 0:1]),
              reads=[xbB], writes=[B["sqdump"], sttB])
        pr.op("ve", lambda v: v.reduce_sum(ssum[:, 0:1], xb[:, :], axis=AX),
              reads=[xbB], writes=[sttB])
        pr.op("ve", lambda v: v.tensor_scalar_mul(mu[:, 0:1], ssum[:, 0:1], 1.0 / D),
              reads=[sttB], writes=[sttB])
        pr.op("ve", lambda v: v.tensor_scalar_mul(var[:, 0:1], ssq[:, 0:1], 1.0 / D),
              reads=[sttB], writes=[sttB])
        pr.op("ve", lambda v: v.tensor_mul(ssum[:, 0:1], mu[:, 0:1], mu[:, 0:1]),
              reads=[sttB], writes=[sttB])
        pr.op("ve", lambda v: v.tensor_sub(var[:, 0:1], var[:, 0:1], ssum[:, 0:1]),
              reads=[sttB], writes=[sttB])
        ecol = 0 if eps == EPS else 1
        pr.op("sc", lambda s, ecol=ecol: s.activation(ssum[:, 0:1], var[:, 0:1], AF.Sqrt,
                                                      bias=epsb[:, ecol:ecol + 1]),
              reads=[sttB, B["epsb"]], writes=[sttB])
        pr.op("ve", lambda v: v.reciprocal(rstd[:, 0:1], ssum[:, 0:1]),
              reads=[sttB], writes=[sttB])
        if save is not None:
            sv, svB, col = save
            pr.op("ve", lambda v: v.tensor_copy(sv[0][:, col:col + 1], mu[:, 0:1]),
                  reads=[sttB], writes=[svB])
            pr.op("ve", lambda v: v.tensor_copy(sv[1][:, col:col + 1], rstd[:, 0:1]),
                  reads=[sttB], writes=[svB])

    p2 = contextlib.ExitStack()
    bvb = sb("bvb", [128, D], F32, p2)
    bcast_load("bvb", bvb, bv_v)
    xt2 = [sb(f"p2x{i}", [128, 1024], F32, p2) for i in range(2)]
    ut2 = sb("p2u", [128, 1024], F32, p2)
    uTh = [sb(f"p2uth{c}", [128, MEG], BF16, p2) for c in range(DC)]
    uTl = [sb(f"p2utl{c}", [128, MEG], BF16, p2) for c in range(DC)]
    wraw = [sb(f"p2wr{i}", [128, 1024], F32, p2) for i in range(2)]
    wh = [sb(f"p2wh{c}", [128, 1024], BF16, p2) for c in range(DC)]
    wl = [sb(f"p2wl{c}", [128, 1024], BF16, p2) for c in range(DC)]
    qev = [sb(f"p2qe{i}", [128, 512], F32, p2) for i in range(2)]
    ps_t = [ps(f"p2pt{i}", [128, 128], F32, p2) for i in range(4)]
    ps_qk = [ps(f"p2pq{i}", [128, 512], F32, p2) for i in range(2)]
    ps_v = [ps(f"p2pv{i}", [128, 512], F32, p2) for i in range(2)]
    nb("p2x0", "p2x1", "p2u", "p2qe0", "p2qe1", "p2pt0", "p2pt1", "p2pt2", "p2pt3",
       "p2pq0", "p2pq1", "p2pv0", "p2pv1", "p2wr0", "p2wr1")
    for c in range(DC):
        nb(f"p2uth{c}", f"p2utl{c}", f"p2wh{c}", f"p2wl{c}")

    for M in range(NMEG):
        p1_batch(M * 4 + 0)
        for ts in range(8):
            t = M * 8 + ts
            par = ts % 2
            xb, xbB = xt2[par], B[f"p2x{par}"]
            pr.op("sy", lambda s, xb=xb, t=t: s.dma_start(
                out=xb[:, :], in_=x[t * 128:(t + 1) * 128, :]), writes=[xbB], dma=True)
            stt, sttB = (st0, B["st0"]) if par == 0 else (st1, B["st1"])
            save = ((mu1a, r1a), B["mu1a"], t) if t < 16 else None
            ln_stats(xb, xbB, stt, sttB, EPS, save=save)
            pr.op("ve", lambda v, xb=xb, stt=stt: v.tensor_scalar(
                ut2[:, :], xb[:, :], stt[2][:, 0:1], stt[4][:, 0:1], OP.subtract, OP.mult),
                reads=[xbB, sttB], writes=[B["p2u"]])
            for c in range(DC):
                pp = (ts * DC + c) % 4
                pr.op("te", lambda te, c=c, pp=pp: te.transpose(
                    ps_t[pp][:, :], ut2[:, c * 128:(c + 1) * 128], identf[:, :]),
                    reads=[B["p2u"], B["identf"]], writes=[B[f"p2pt{pp}"]])
                pr.op("sc", lambda s, c=c, ts=ts, pp=pp: s.activation(
                    uTh[c][:, ts * 128:(ts + 1) * 128], ps_t[pp][:, :], AF.Copy),
                    reads=[B[f"p2pt{pp}"]], writes=[B[f"p2uth{c}"]])
                pr.op("ve", lambda v, c=c, ts=ts, pp=pp: v.tensor_tensor(
                    uTl[c][:, ts * 128:(ts + 1) * 128], ps_t[pp][:, :],
                    uTh[c][:, ts * 128:(ts + 1) * 128], OP.subtract),
                    reads=[B[f"p2pt{pp}"], B[f"p2uth{c}"]], writes=[B[f"p2utl{c}"]])
        p1_batch(M * 4 + 1)
        wlist = [("q", wq, q_h, bqs, B["bqs"]), ("k", wk, k_h, bks, B["bks"]),
                 ("v", wv, None, None, None)]
        if M >= 2:
            wlist = wlist[1:]
        for wi, (kind, wsrc, dsth, bias, biasB) in enumerate(wlist):
            for c in range(DC):
                par = c % 2
                wb, wbB = wraw[par], B[f"p2wr{par}"]
                pr.op("sy", lambda s, wb=wb, wsrc=wsrc, c=c: s.dma_start(
                    out=wb[:, :], in_=wsrc[c * 128:(c + 1) * 128, :]), writes=[wbB], dma=True)
                pr.op("ve", lambda v, c=c, wb=wb: v.tensor_copy(wh[c][:, :], wb[:, :]),
                      reads=[wbB], writes=[B[f"p2wh{c}"]])
                pr.op("ve", lambda v, c=c, wb=wb: v.tensor_tensor(
                    wl[c][:, :], wb[:, :], wh[c][:, :], OP.subtract),
                    reads=[wbB, B[f"p2wh{c}"]], writes=[B[f"p2wl{c}"]])
            wrb = [B[f"p2wh{c}"] for c in range(DC)] + [B[f"p2wl{c}"] for c in range(DC)]
            uTb = [B[f"p2uth{c}"] for c in range(DC)] + [B[f"p2utl{c}"] for c in range(DC)]
            if kind in ("q", "k"):
                for m in range(8):
                    for n in range(2):
                        par = (m * 2 + n) % 2
                        pb = B[f"p2pq{par}"]
                        fns = []
                        for c in range(DC):
                            for (a_, b_) in ((wh, uTh), (wh, uTl), (wl, uTh)):
                                st_ = (c == 0 and a_ is wh and b_ is uTh)
                                sp_ = (c == DC - 1 and a_ is wl)
                                fns.append(lambda te, c=c, a_=a_, b_=b_, m=m, n=n, par=par, st_=st_, sp_=sp_:
                                           te.matmul(ps_qk[par][:, :],
                                                     a_[c][:, m * 128:(m + 1) * 128],
                                                     b_[c][:, n * 512:(n + 1) * 512],
                                                     start=st_, stop=sp_))
                        pr.group("te", fns, reads=wrb + uTb, writes=[pb])
                        qpar = (m * 2 + n) % 2
                        qb, qbB = qev[qpar], B[f"p2qe{qpar}"]
                        pr.op("ve", lambda v, par=par, qb=qb, bias=bias, m=m: v.tensor_scalar(
                            qb[:, :], ps_qk[par][:, :], bias[:, m:m + 1], None, OP.add),
                            reads=[pb, biasB], writes=[qbB])
                        tok0 = M * MEG + n * 512
                        pr.op("gp", lambda g, qb=qb, dsth=dsth, m=m, tok0=tok0: g.dma_start(
                            out=dsth[m * 128:(m + 1) * 128, tok0:tok0 + 512], in_=qb[:, :]),
                            reads=[qbB], writes=[B["q_h" if dsth is q_h else "k_h"]], dma=True)
            else:
                for ts in range(8):
                    for n in range(2):
                        par = (ts * 2 + n) % 2
                        pb = B[f"p2pv{par}"]
                        fns = []
                        for c in range(DC):
                            for (a_, b_) in ((uTh, wh), (uTh, wl), (uTl, wh)):
                                st_ = (c == 0 and a_ is uTh and b_ is wh)
                                sp_ = (c == DC - 1 and a_ is uTl)
                                fns.append(lambda te, c=c, a_=a_, b_=b_, ts=ts, n=n, par=par, st_=st_, sp_=sp_:
                                           te.matmul(ps_v[par][:, :],
                                                     a_[c][:, ts * 128:(ts + 1) * 128],
                                                     b_[c][:, n * 512:(n + 1) * 512],
                                                     start=st_, stop=sp_))
                        pr.group("te", fns, reads=wrb + uTb, writes=[pb])
                        qpar = (ts * 2 + n) % 2
                        qb, qbB = qev[qpar], B[f"p2qe{qpar}"]
                        pr.op("ve", lambda v, par=par, qb=qb, n=n: v.tensor_tensor(
                            qb[:, :], ps_v[par][:, :], bvb[:, n * 512:(n + 1) * 512], OP.add),
                            reads=[pb, B["bvb"]], writes=[qbB])
                        tok0 = M * MEG + ts * 128
                        pr.op("gp", lambda g, qb=qb, ts=ts, n=n, tok0=tok0: g.dma_start(
                            out=v_h[tok0:tok0 + 128, n * 512:(n + 1) * 512], in_=qb[:, :]),
                            reads=[qbB], writes=[B["v_h"]], dma=True)
        p1_batch(M * 4 + 2)
        p1_batch(M * 4 + 3)

    pr.op("ve", lambda v: v.reduce_sum(st0[0][:, 0:1], asum[:, 0:32], axis=AX),
          reads=[B["asum"]], writes=[B["st0"]])
    pr.op("ve", lambda v: v.reduce_sum(st0[1][:, 0:1], asum[:, 32:64], axis=AX),
          reads=[B["asum"]], writes=[B["st0"]])
    pr.op("te", lambda te: te.matmul(ps_t[0][0:1, 0:1], ones1[:, 0:1], st0[0][:, 0:1],
                                     start=True, stop=True),
          reads=[B["st0"], B["ones1"]], writes=[B["p2pt0"]])
    pr.op("te", lambda te: te.matmul(ps_t[1][0:1, 0:1], ones1[:, 0:1], st0[1][:, 0:1],
                                     start=True, stop=True),
          reads=[B["st0"], B["ones1"]], writes=[B["p2pt1"]])
    pr.op("ve", lambda v: v.tensor_scalar(gsc[0:1, 0:1], ps_t[0][0:1, 0:1],
                                          1.0 / (DFF * D), EPSQ, OP.mult, OP.add),
          reads=[B["p2pt0"]], writes=[B["gsc"]])
    pr.op("ve", lambda v: v.tensor_scalar(gsc[0:1, 2:3], ps_t[1][0:1, 0:1],
                                          1.0 / (DFF * D), EPSQ, OP.mult, OP.add),
          reads=[B["p2pt1"]], writes=[B["gsc"]])
    pr.op("ve", lambda v: v.tensor_scalar_mul(gsc[0:1, 0:1], gsc[0:1, 0:1], 0.5),
          reads=[B["gsc"]], writes=[B["gsc"]])
    pr.op("ve", lambda v: v.tensor_scalar_mul(gsc[0:1, 2:3], gsc[0:1, 2:3], 0.5),
          reads=[B["gsc"]], writes=[B["gsc"]])
    pr.op("ve", lambda v: v.tensor_scalar_mul(gsc[0:1, 1:2], gsc[0:1, 0:1], -1.0),
          reads=[B["gsc"]], writes=[B["gsc"]])
    pr.op("ve", lambda v: v.tensor_scalar_mul(gsc[0:1, 3:4], gsc[0:1, 2:3], -1.0),
          reads=[B["gsc"]], writes=[B["gsc"]])
    pr.op("te", lambda te: te.matmul(ps_t[0][:, 0:4], onesr[0:1, :], gsc[0:1, 0:4],
                                     start=True, stop=True),
          reads=[B["gsc"], B["onesr"]], writes=[B["p2pt0"]])
    pr.op("ve", lambda v: v.tensor_copy(thrb[:, :], ps_t[0][:, 0:4]),
          reads=[B["p2pt0"]], writes=[B["thrb"]])

    pr.barrier()
    p2.close()

    if PH < 3:
        pr._wait("gp", "gp", pr.cnt["gp"])
        pr._wait("gp", "sy", pr.cnt["sy"])
        pr.op("gp", lambda g: g.memset(gsc[0:1, 4:5], 0.0), writes=[B["gsc"]])
        pr.emit(es)
        return nc, es
    p3 = contextlib.ExitStack()
    krawl = [sb(f"p3kr{i}", [128, S], F32, p3) for i in range(2)]
    khh = sb("p3kh", [128, S], BF16, p3)
    kll = sb("p3kl", [128, S], BF16, p3)
    qrawl = [sb(f"p3qr{i}", [128, OWN], F32, p3) for i in range(2)]
    qhh = sb("p3qh", [128, OWN], BF16, p3)
    qll = sb("p3ql", [128, OWN], BF16, p3)
    vr3l = [sb(f"p3vr{i}", [128, 32, 128], F32, p3) for i in range(2)]
    vph = sb("p3vph", [128, 32, 130], BF16, p3)
    vpl = sb("p3vpl", [128, 32, 130], BF16, p3)
    pfb = [sb(f"p3pf{i}", [128, 1024], F32, p3) for i in range(4)]
    phb = [sb(f"p3ph{i}", [128, 1024], BF16, p3) for i in range(4)]
    plb = [sb(f"p3pl{i}", [128, 1024], BF16, p3) for i in range(4)]
    maxk = sb("p3mk", [128, 1], F32, p3)
    cst = sb("p3cs", [1, 16], F32, p3)
    biasA = sb("p3ba", [128, 1], F32, p3)
    biasB = sb("p3bb", [128, 1], F32, p3)
    rs_s = sb("p3rs", [1, 512], F32, p3)
    rs_t = sb("p3rt", [1, 512], F32, p3)
    rsb = [sb(f"p3rb{i}", [64, 512], F32, p3) for i in range(4)]
    cxe = [sb(f"p3cx{i}", [64, 512], F32, p3) for i in range(4)]
    psSA = [ps(f"p3ssa{i}", [128, 512], F32, p3) for i in range(2)]
    psSB = [ps(f"p3ssb{i}", [128, 512], F32, p3) for i in range(2)]
    psCA = [ps(f"p3ca{i}", [65, 512], F32, p3) for i in range(2)]
    psCB = [ps(f"p3cb{i}", [65, 512], F32, p3) for i in range(2)]
    nb("p3kr0", "p3kr1", "p3kh", "p3kl", "p3qr0", "p3qr1", "p3qh", "p3ql",
       "p3vr0", "p3vr1", "p3vph", "p3vpl",
       "p3mk", "p3cs", "p3ba", "p3bb", "p3rs", "p3rt",
       "p3pf0", "p3pf1", "p3pf2", "p3pf3", "p3ph0", "p3ph1", "p3ph2", "p3ph3",
       "p3pl0", "p3pl1", "p3pl2", "p3pl3",
       "p3ssa0", "p3ssa1", "p3ssb0", "p3ssb1", "p3ca0", "p3ca1", "p3cb0", "p3cb1",
       "p3rb0", "p3rb1", "p3rb2", "p3rb3", "p3cx0", "p3cx1", "p3cx2", "p3cx3")

    for p in range(P3P):
        pp_ = p % 2
        kraw, krB = krawl[pp_], B[f"p3kr{pp_}"]
        qraw, qrB = qrawl[pp_], B[f"p3qr{pp_}"]
        vr3, vrB = vr3l[pp_], B[f"p3vr{pp_}"]
        pr.op("sy", lambda s, kraw=kraw, p=p: s.dma_start(out=kraw[:, :],
                                               in_=k_h[p * 128:(p + 1) * 128, :]),
              reads=[B["k_h"]], writes=[krB], dma=True)
        pr.op("sy", lambda s, qraw=qraw, p=p: s.dma_start(out=qraw[:, :],
                                               in_=q_h[p * 128:(p + 1) * 128, :]),
              reads=[B["q_h"]], writes=[qrB], dma=True)
        pr.op("sy", lambda s, vr3=vr3, p=p: s.dma_start(
            out=vr3[:, :, :], in_=bass.AP(v_h.tensor, p * 128,
                                          [[1024, 128], [131072, 32], [1, 128]])),
            reads=[B["v_h"]], writes=[vrB], dma=True)
        pr.op("ve", lambda v, kraw=kraw: v.tensor_copy(khh[:, :], kraw[:, :]),
              reads=[krB], writes=[B["p3kh"]])
        pr.op("ve", lambda v, kraw=kraw: v.tensor_tensor(kll[:, :], kraw[:, :], khh[:, :],
                                              OP.subtract),
              reads=[krB, B["p3kh"]], writes=[B["p3kl"]])
        pr.op("ve", lambda v, qraw=qraw: v.tensor_copy(qhh[:, :], qraw[:, :]),
              reads=[qrB], writes=[B["p3qh"]])
        pr.op("ve", lambda v, qraw=qraw: v.tensor_tensor(qll[:, :], qraw[:, :], qhh[:, :],
                                              OP.subtract),
              reads=[qrB, B["p3qh"]], writes=[B["p3ql"]])
        pr.op("ve", lambda v, kraw=kraw: v.reduce_max(maxk[:, 0:1], kraw[:, :], axis=AX,
                                           apply_absolute_value=True),
              reads=[krB], writes=[B["p3mk"]])
        pr.op("sc", lambda s, qraw=qraw: s.activation(pfb[0][:, :], qraw[:, 0:1024], AF.Abs),
              reads=[qrB], writes=[B["p3pf0"]])
        pr.op("sc", lambda s, qraw=qraw: s.activation(pfb[1][:, :], qraw[:, 1024:2048], AF.Abs),
              reads=[qrB], writes=[B["p3pf1"]])
        if P3L < 2:
            continue
        if p == 0 and not NOVP:
            pr.op("ve", lambda v: v.memset(vph[:, :, :], 1.0), writes=[B["p3vph"]])
            pr.op("ve", lambda v: v.memset(vpl[:, :, :], 0.0), writes=[B["p3vpl"]])
        if NOVP:
            pr.op("ve", lambda v, vr3=vr3: v.tensor_copy(vph[:, 0, 0:64], vr3[:, 0, 0:64]),
                  reads=[vrB], writes=[B["p3vph"]])
        if not NOVP:
            pr.op("ve", lambda v, vr3=vr3: v.tensor_copy(vph[:, :, 0:64], vr3[:, :, 0:64]),
              reads=[vrB], writes=[B["p3vph"]])
            pr.op("ve", lambda v, vr3=vr3: v.tensor_copy(vph[:, :, 65:129], vr3[:, :, 64:128]),
                  reads=[vrB], writes=[B["p3vph"]])
            pr.op("ve", lambda v, vr3=vr3: v.tensor_tensor(vpl[:, :, 0:64], vr3[:, :, 0:64],
                                                  vph[:, :, 0:64], OP.subtract),
                  reads=[vrB, B["p3vph"]], writes=[B["p3vpl"]])
            pr.op("ve", lambda v, vr3=vr3: v.tensor_tensor(vpl[:, :, 65:129], vr3[:, :, 64:128],
                                                  vph[:, :, 65:129], OP.subtract),
                  reads=[vrB, B["p3vph"]], writes=[B["p3vpl"]])
        if P3L < 3:
            continue
        if NOCB:
            pr.op("ve", lambda v: v.memset(biasA[:, 0:1], -3.0), writes=[B["p3ba"]])
            pr.op("ve", lambda v: v.memset(biasB[:, 0:1], -3.0), writes=[B["p3bb"]])
        for hh_ in range(0 if NOCB else 2):
            for ic in range(4):
                pf, pfB = (pfb[0], B["p3pf0"]) if ic < 2 else (pfb[1], B["p3pf1"])
                cslice = pf[hh_ * 64:hh_ * 64 + 64, (ic % 2) * 512:(ic % 2) * 512 + 512]
                pr.op("te", lambda te, hh_=hh_, cs_=cslice: te.matmul(
                    psSA[0][0:1, :], maxk[hh_ * 64:hh_ * 64 + 64, 0:1], cs_,
                    start=True, stop=True),
                    reads=[B["p3mk"], pfB], writes=[B["p3ssa0"]])
                pr.op("ve", lambda v, hh_=hh_, ic=ic: v.reduce_max(
                    cst[0:1, hh_ * 4 + ic:hh_ * 4 + ic + 1], psSA[0][0:1, :], axis=AX),
                    reads=[B["p3ssa0"]], writes=[B["p3cs"]])
        if not NOCB:
            pr.op("ve", lambda v: v.reduce_max(cst[0:1, 8:9], cst[0:1, 0:4], axis=AX),
                  reads=[B["p3cs"]], writes=[B["p3cs"]])
            pr.op("ve", lambda v: v.reduce_max(cst[0:1, 9:10], cst[0:1, 4:8], axis=AX),
                  reads=[B["p3cs"]], writes=[B["p3cs"]])
            pr.op("ve", lambda v: v.tensor_scalar(cst[0:1, 10:12], cst[0:1, 8:10],
                                                  -0.125 * 1.001, -0.01, OP.mult, OP.add),
                  reads=[B["p3cs"]], writes=[B["p3cs"]])
            pr.op("te", lambda te: te.matmul(psSA[0][:, 0:2], onesr[0:1, :], cst[0:1, 10:12],
                                             start=True, stop=True),
                  reads=[B["p3cs"], B["onesr"]], writes=[B["p3ssa0"]])
            pr.op("ve", lambda v: v.tensor_copy(biasA[:, 0:1], psSA[0][:, 0:1]),
                  reads=[B["p3ssa0"]], writes=[B["p3ba"]])
            pr.op("ve", lambda v: v.tensor_copy(biasB[:, 0:1], psSA[0][:, 1:2]),
                  reads=[B["p3ssa0"]], writes=[B["p3bb"]])

        if P3L < 4:
            continue
        kb = [B["p3kh"], B["p3kl"]]
        qb = [B["p3qh"], B["p3ql"]]
        vb = [B["p3vph"], B["p3vpl"]]
        evac_q = []
        for ic in range(P3I):
            cp = ic % 2
            pCA, pCB = B[f"p3ca{cp}"], B[f"p3cb{cp}"]
            def emit_pv(j):
                sp = j % 4
                pvb = [B[f"p3ph{sp}"], B[f"p3pl{sp}"]]
                fns = []
                for hh_, pC in ((0, psCA[cp]), (1, psCB[cp])):
                    c0 = hh_ * 65
                    for pi, (va_, pa_) in enumerate(((vph, phb[sp]), (vpl, phb[sp]), (vph, plb[sp]))):
                        st_ = (j == 0 and pi == 0)
                        sp_ = (j == P3J - 1 and pi == 2)
                        fns.append(lambda te, va_=va_, pa_=pa_, hh_=hh_, j=j, c0=c0, pC=pC, st_=st_, sp_=sp_:
                                   te.matmul(pC[:, :], va_[:, j, c0:c0 + 65],
                                             pa_[:, hh_ * 512:hh_ * 512 + 512],
                                             start=st_, stop=sp_))
                pr.group("te", fns, reads=vb + pvb, writes=[pCA, pCB])

            for j in range(P3J + 1):
                if j < P3J:
                    sp = j % 2
                    pSA, pSB = B[f"p3ssa{sp}"], B[f"p3ssb{sp}"]
                    i0 = ic * 512
                    fns = []
                    for (ka_, qa_) in ((khh, qhh), (khh, qll), (kll, qhh)):
                        for hh_ in range(2):
                            st_ = (ka_ is khh and qa_ is qhh)
                            sp_ = (ka_ is kll)
                            pst = (psSA if hh_ == 0 else psSB)[sp]
                            fns.append(lambda te, ka_=ka_, qa_=qa_, hh_=hh_, j=j, i0=i0, pst=pst, st_=st_, sp_=sp_:
                                       te.matmul(pst[:, :],
                                                 ka_[hh_ * 64:hh_ * 64 + 64, j * 128:(j + 1) * 128],
                                                 qa_[hh_ * 64:hh_ * 64 + 64, i0:i0 + 512],
                                                 start=st_, stop=sp_,
                                                 tile_position=(hh_ * 64, 0)))
                    pr.group("te", fns, reads=kb + qb, writes=[pSA, pSB])
                    pp3 = j % 4
                    if P3L >= 5:
                        pr.op("sc", lambda s, pp3=pp3, sp=sp: s.activation(
                            pfb[pp3][:, 0:512], psSA[sp][:, :], AF.Exp, scale=0.125,
                            bias=biasA[:, 0:1]),
                            reads=[pSA, B["p3ba"]], writes=[B[f"p3pf{pp3}"]])
                        pr.op("sc", lambda s, pp3=pp3, sp=sp: s.activation(
                            pfb[pp3][:, 512:1024], psSB[sp][:, :], AF.Exp, scale=0.125,
                            bias=biasB[:, 0:1]),
                            reads=[pSB, B["p3bb"]], writes=[B[f"p3pf{pp3}"]])
                    if P3L >= 6:
                        pr.op("ve", lambda v, pp3=pp3: v.tensor_copy(
                            phb[pp3][:, :], pfb[pp3][:, :]),
                            reads=[B[f"p3pf{pp3}"]], writes=[B[f"p3ph{pp3}"]])
                        pr.op("ve", lambda v, pp3=pp3: v.tensor_tensor(
                            plb[pp3][:, :], pfb[pp3][:, :], phb[pp3][:, :], OP.subtract),
                            reads=[B[f"p3pf{pp3}"], B[f"p3ph{pp3}"]], writes=[B[f"p3pl{pp3}"]])
                if P3L >= 7 and j >= 3:
                    emit_pv(j - 3)
                if j == P3J:
                    emit_pv(P3J - 2)
                    emit_pv(P3J - 1)
                if j == 6 and evac_q:
                    evac_q.pop(0)()
            if P3L < 8:
                continue
            def emit_evac(cp=cp, pCA=pCA, pCB=pCB, ic=ic):
              for hh_, pC, pCB_ in ((0, psCA[cp], pCA), (1, psCB[cp], pCB)):
                  rp = cp * 2 + hh_
                  rb, rbB = rsb[rp], B[f"p3rb{rp}"]
                  cx, cxB = cxe[rp], B[f"p3cx{rp}"]
                  pr.op("ve", lambda v, pC=pC: v.tensor_scalar_add(rs_s[0:1, :], pC[64:65, :], 1e-30),
                        reads=[pCB_], writes=[B["p3rs"]])
                  pr.op("ve", lambda v: v.reciprocal(rs_t[0:1, :], rs_s[0:1, :]),
                        reads=[B["p3rs"]], writes=[B["p3rt"]])
                  pr.op("ve", lambda v: v.tensor_mul(rs_s[0:1, :], rs_s[0:1, :], rs_t[0:1, :]),
                        reads=[B["p3rs"], B["p3rt"]], writes=[B["p3rs"]])
                  pr.op("ve", lambda v: v.tensor_scalar(rs_s[0:1, :], rs_s[0:1, :], -1.0, 2.0,
                                                        OP.mult, OP.add),
                        reads=[B["p3rs"]], writes=[B["p3rs"]])
                  pr.op("ve", lambda v: v.tensor_mul(rs_t[0:1, :], rs_t[0:1, :], rs_s[0:1, :]),
                        reads=[B["p3rs"], B["p3rt"]], writes=[B["p3rt"]])
                  psr, psrB = (psSA, "p3ssa") if hh_ == 0 else (psSB, "p3ssb")
                  pr.op("te", lambda te, psr=psr: te.matmul(
                      psr[0][0:64, :], onesr[0:1, 0:64], rs_t[0:1, :], start=True, stop=True),
                      reads=[B["p3rt"], B["onesr"]], writes=[B[psrB + "0"]])
                  pr.op("ve", lambda v, rb=rb, psr=psr: v.tensor_copy(rb[:, :], psr[0][0:64, :]),
                        reads=[B[psrB + "0"]], writes=[rbB])
                  pr.op("ve", lambda v, cx=cx, pC=pC, rb=rb: v.tensor_tensor(
                      cx[:, :], pC[0:64, :], rb[:, :], OP.mult),
                      reads=[pCB_, rbB], writes=[cxB])
                  r0 = p * 128 + hh_ * 64
                  pr.op("gp", lambda g, cx=cx, r0=r0, ic=ic: g.dma_start(
                      out=ctx_h[r0:r0 + 64, ic * 512:(ic + 1) * 512], in_=cx[:, :]),
                      reads=[cxB], writes=[B["ctx_h"]], dma=True)
            evac_q.append(emit_evac)

        for fn_ in evac_q:
            fn_()
        evac_q = []
    pr.barrier()
    p3.close()

    if PH < 4:
        pr._wait("gp", "gp", pr.cnt["gp"])
        pr._wait("gp", "sy", pr.cnt["sy"])
        pr.op("gp", lambda g: g.memset(gsc[0:1, 4:5], 0.0), writes=[B["gsc"]])
        pr.emit(es)
        return nc, es
    p4 = contextlib.ExitStack()
    bob = sb("bob", [128, D], F32, p4)
    g2b = sb("g2b", [128, D], F32, p4)
    be2b = sb("be2b", [128, D], F32, p4)
    bcast_load("bob", bob, bo_v)
    bcast_load("g2b", g2b, g2_v)
    bcast_load("be2b", be2b, be2_v)
    ctxs = [sb(f"p4c{pp}", [128, OWN], F32, p4) for pp in range(PAIRS)]
    woh = [sb(f"p4wh{c}", [128, 1024], BF16, p4) for c in range(DC)]
    wol = [sb(f"p4wl{c}", [128, 1024], BF16, p4) for c in range(DC)]
    wraw4 = [sb(f"p4wr{i}", [128, 1024], F32, p4) for i in range(2)]
    chs = [sb(f"p4ch{i}", [128, 128], BF16, p4) for i in range(2)]
    cls = [sb(f"p4cl{i}", [128, 128], BF16, p4) for i in range(2)]
    xb4 = [sb(f"p4x{i}", [128, 1024], F32, p4) for i in range(2)]
    yb4 = sb("p4y", [128, 1024], F32, p4)
    zb4 = sb("p4z", [128, 1024], F32, p4)
    ps_ao = [ps(f"p4ao{i}", [128, 512], F32, p4) for i in range(4)]
    nb("p4y", "p4z", "p4wr0", "p4wr1", "p4ao0", "p4ao1", "p4ao2", "p4ao3", "p4x0", "p4x1",
       "p4ch0", "p4ch1", "p4cl0", "p4cl1")
    for pp in range(PAIRS):
        nb(f"p4c{pp}", f"p4wh{pp}", f"p4wl{pp}")

    p5a = contextlib.ExitStack()
    wr5 = [sb(f"p5wr{i}", [128, 1024], F32, p5a) for i in range(2)]
    tq5 = [sb(f"p5tq{i}", [128, 1024], BF16, p5a) for i in range(2)]
    sc5 = sb("p5sc", [128, 1024], F32, p5a)
    tevac = [sb(f"p5te{i}", [128, 128], BF16, p5a) for i in range(2)]
    ps_tr = [ps(f"p5tr{i}", [128, 128], BF16, p5a) for i in range(2)]
    nb("p5wr0", "p5wr1", "p5tq0", "p5tq1", "p5sc", "p5te0", "p5te1", "p5tr0", "p5tr1")

    def terntile(widx, fc):
        par = fc % 2
        wb, wbB = wr5[par], B[f"p5wr{par}"]
        tq, tqB = tq5[par], B[f"p5tq{par}"]
        tcol = 0 if widx == 0 else 2
        if widx == 0:
            pr.op("sy", lambda s, fc=fc: s.dma_start(out=wb[:, :],
                                                     in_=w1[fc * 128:(fc + 1) * 128, :]),
                  writes=[wbB], dma=True)
        else:
            for c in range(DC):
                pr.op("sy", lambda s, fc=fc, c=c: s.dma_start(
                    out=wb[:, c * 128:(c + 1) * 128],
                    in_=w2[c * 128:(c + 1) * 128, fc * 128:(fc + 1) * 128]),
                    writes=[wbB], dma=True)
        pr.op("ve", lambda v, wb=wb, tcol=tcol: v.tensor_scalar(
            sc5[:, :], wb[:, :], thrb[:, tcol:tcol + 1], None, OP.is_gt),
            reads=[wbB, B["thrb"]], writes=[B["p5sc"]])
        pr.op("ve", lambda v, wb=wb, tcol=tcol: v.tensor_scalar(
            wb[:, :], wb[:, :], thrb[:, tcol + 1:tcol + 2], None, OP.is_lt),
            reads=[wbB, B["thrb"]], writes=[wbB])
        pr.op("ve", lambda v, wb=wb, tq=tq: v.tensor_tensor(tq[:, :], sc5[:, :], wb[:, :],
                                                            OP.subtract),
            reads=[wbB, B["p5sc"]], writes=[tqB])
        for c in range(DC):
            tp = c % 2
            pr.op("te", lambda te, tq=tq, c=c, tp=tp: te.transpose(
                ps_tr[tp][:, :], tq[:, c * 128:(c + 1) * 128], identr[:, :]),
                reads=[tqB, B["identr"]], writes=[B[f"p5tr{tp}"]])
            pr.op("ve", lambda v, tp=tp: v.tensor_copy(tevac[tp][:, :], ps_tr[tp][:, :]),
                  reads=[B[f"p5tr{tp}"]], writes=[B[f"p5te{tp}"]])
            if widx == 0:
                dst = w1qt_h[c * 128:(c + 1) * 128, fc * 128:(fc + 1) * 128]
                dB = "w1qt_h"
            else:
                dst = w2qt_h[fc * 128:(fc + 1) * 128, c * 128:(c + 1) * 128]
                dB = "w2qt_h"
            pr.op("gp", lambda g, dst=dst, tp=tp: g.dma_start(out=dst, in_=tevac[tp][:, :]),
                  reads=[B[f"p5te{tp}"]], writes=[B[dB]], dma=True)


    for pp in range(PAIRS):
        pr.op("sy", lambda s, pp=pp: s.dma_start(out=ctxs[pp][:, :],
                                                 in_=ctx_h[pp * 128:(pp + 1) * 128, :]),
              reads=[B["ctx_h"]], writes=[B[f"p4c{pp}"]], dma=True)
        par = pp % 2
        wb, wbB = wraw4[par], B[f"p4wr{par}"]
        pr.op("sy", lambda s, wb=wb, pp=pp: s.dma_start(
            out=wb[:, :], in_=wo[pp * 128:(pp + 1) * 128, :]), writes=[wbB], dma=True)
        pr.op("ve", lambda v, pp=pp, wb=wb: v.tensor_copy(woh[pp][:, :], wb[:, :]),
              reads=[wbB], writes=[B[f"p4wh{pp}"]])
        pr.op("ve", lambda v, pp=pp, wb=wb: v.tensor_tensor(
            wol[pp][:, :], wb[:, :], woh[pp][:, :], OP.subtract),
            reads=[wbB, B[f"p4wh{pp}"]], writes=[B[f"p4wl{pp}"]])

    for t in range(16):
        ap_ = t % 2
        pA0, pA1 = B[f"p4ao{ap_ * 2}"], B[f"p4ao{ap_ * 2 + 1}"]
        for pp in range(PAIRS):
            cpar = pp % 2
            ch, chB = chs[cpar], B[f"p4ch{cpar}"]
            cl, clB = cls[cpar], B[f"p4cl{cpar}"]
            pr.op("ve", lambda v, ch=ch, pp=pp, t=t: v.tensor_copy(
                ch[:, :], ctxs[pp][:, t * 128:(t + 1) * 128]),
                reads=[B[f"p4c{pp}"]], writes=[chB])
            pr.op("ve", lambda v, cl=cl, ch=ch, pp=pp, t=t: v.tensor_tensor(
                cl[:, :], ctxs[pp][:, t * 128:(t + 1) * 128], ch[:, :],
                OP.subtract),
                reads=[B[f"p4c{pp}"], chB], writes=[clB])
            fns = []
            for n in range(2):
                for (ca_, wa_, wB_) in ((ch, woh[pp], B[f"p4wh{pp}"]),
                                        (ch, wol[pp], B[f"p4wl{pp}"]),
                                        (cl, woh[pp], B[f"p4wh{pp}"])):
                    st_ = (pp == 0 and ca_ is ch and wa_ is woh[pp])
                    sp_ = (pp == PAIRS - 1 and ca_ is cl)
                    fns.append(lambda te, ca_=ca_, wa_=wa_, n=n, ap_=ap_, st_=st_, sp_=sp_:
                               te.matmul(ps_ao[ap_ * 2 + n][:, :], ca_[:, :],
                                         wa_[:, n * 512:(n + 1) * 512], start=st_, stop=sp_))
            rds = [chB, clB, B[f"p4wh{pp}"], B[f"p4wl{pp}"]]
            pr.group("te", fns, reads=rds, writes=[pA0, pA1])
        xb, xbB = xb4[ap_], B[f"p4x{ap_}"]
        pr.op("sy", lambda s, xb=xb, t=t: s.dma_start(out=xb[:, :],
                                                      in_=x[t * 128:(t + 1) * 128, :]),
              writes=[xbB], dma=True)
        pr.op("ve", lambda v, ap_=ap_: v.tensor_tensor(
            yb4[:, 0:512], ps_ao[ap_ * 2][:, :], bob[:, 0:512], OP.add),
            reads=[pA0, B["bob"]], writes=[B["p4y"]])
        pr.op("ve", lambda v, ap_=ap_: v.tensor_tensor(
            yb4[:, 512:1024], ps_ao[ap_ * 2 + 1][:, :], bob[:, 512:1024], OP.add),
            reads=[pA1, B["bob"]], writes=[B["p4y"]])
        pr.op("ve", lambda v, xb=xb, t=t: v.tensor_scalar(
            zb4[:, :], xb[:, :], mu1a[:, t:t + 1], r1a[:, t:t + 1], OP.subtract, OP.mult),
            reads=[xbB, B["mu1a"]], writes=[B["p4z"]])
        pr.op("ve", lambda v: v.tensor_tensor(yb4[:, :], yb4[:, :], zb4[:, :], OP.add),
              reads=[B["p4z"], B["p4y"]], writes=[B["p4y"]])
        stt, sttB = (st0, B["st0"]) if ap_ == 0 else (st1, B["st1"])
        ln_stats(yb4, B["p4y"], stt, sttB, EPS2)
        pr.op("ve", lambda v, stt=stt: v.tensor_scalar(
            zb4[:, :], yb4[:, :], stt[2][:, 0:1], stt[4][:, 0:1], OP.subtract, OP.mult),
            reads=[B["p4y"], sttB], writes=[B["p4z"]])
        pr.op("ve", lambda v: v.tensor_mul(zb4[:, :], zb4[:, :], g2b[:, :]),
              reads=[B["g2b"], B["p4z"]], writes=[B["p4z"]])
        pr.op("ve", lambda v: v.tensor_tensor(zb4[:, :], zb4[:, :], be2b[:, :], OP.add),
              reads=[B["be2b"], B["p4z"]], writes=[B["p4z"]])
        ln_stats(zb4, B["p4z"], stt, sttB, EPS)
        pr.op("ve", lambda v, stt=stt: v.tensor_scalar(
            yb4[:, :], zb4[:, :], stt[2][:, 0:1], stt[4][:, 0:1], OP.subtract, OP.mult),
            reads=[B["p4z"], sttB], writes=[B["p4y"]])
        pr.op("ve", lambda v, xb=xb: v.tensor_tensor(yb4[:, :], yb4[:, :], xb[:, :], OP.add),
              reads=[xbB, B["p4y"]], writes=[B["p4y"]])
        pr.op("gp", lambda g, t=t: g.dma_start(out=x1_h[t * 128:(t + 1) * 128, :],
                                               in_=yb4[:, :]),
              reads=[B["p4y"]], writes=[B["x1_h"]], dma=True)
        for _fc in range(4 * t, 4 * t + 4):
            if _fc < 32:
                terntile(0, _fc)
            else:
                terntile(1, _fc - 32)

    pr.barrier()
    p5a.close()
    p4.close()



    p5 = contextlib.ExitStack()
    b2b = sb("b2b", [128, D], F32, p5)
    bcast_load("b2b", b2b, b2_v)
    x5 = [sb(f"p5x{i}", [128, 1024], F32, p5) for i in range(2)]
    u5 = sb("p5u", [128, 1024], F32, p5)
    u2h = [sb(f"p5uh{c}", [128, MEG], BF16, p5) for c in range(DC)]
    u2l = [sb(f"p5ul{c}", [128, MEG], BF16, p5) for c in range(DC)]
    w1qS2 = [[sb(f"p5w1{c}_{i}", [128, 1024], BF16, p5) for c in range(DC)] for i in range(2)]
    w2qS2 = [[sb(f"p5w2{c}_{i}", [128, 1024], BF16, p5) for c in range(DC)] for i in range(2)]
    hf5 = [sb(f"p5hf{i}", [128, 256], F32, p5) for i in range(3)]
    hh5 = [sb(f"p5hh{i}", [128, 256], BF16, p5) for i in range(3)]
    hl5 = [sb(f"p5hl{i}", [128, 256], BF16, p5) for i in range(3)]
    o2sb = [sb(f"p5o{i}", [128, 1024], F32, p5) for i in range(8)]
    ps_t5 = [ps(f"p5pt{i}", [128, 128], F32, p5) for i in range(2)]
    psH = [ps(f"p5ph{i}", [128, 256], F32, p5) for i in range(2)]
    psO = [ps(f"p5po{i}", [128, 512], F32, p5) for i in range(4)]
    nb("p5x0", "p5x1", "p5u", "p5pt0", "p5pt1", "p5ph0", "p5ph1",
       "p5po0", "p5po1", "p5po2", "p5po3",
       "p5hf0", "p5hf1", "p5hf2", "p5hh0", "p5hh1", "p5hh2", "p5hl0", "p5hl1", "p5hl2")
    for c in range(DC):
        nb(f"p5uh{c}", f"p5ul{c}", f"p5w1{c}_0", f"p5w2{c}_0", f"p5w1{c}_1", f"p5w2{c}_1")
    for i in range(8):
        nb(f"p5o{i}")

    u2b = [B[f"p5uh{c}"] for c in range(DC)] + [B[f"p5ul{c}"] for c in range(DC)]
    w1b2 = [[B[f"p5w1{c}_{i}"] for c in range(DC)] for i in range(2)]
    w2b2 = [[B[f"p5w2{c}_{i}"] for c in range(DC)] for i in range(2)]

    for M in range(2):
        for ts in range(8):
            t = M * 8 + ts
            par = ts % 2
            xb, xbB = x5[par], B[f"p5x{par}"]
            pr.op("sy", lambda s, xb=xb, t=t: s.dma_start(
                out=xb[:, :], in_=x1_h[t * 128:(t + 1) * 128, :]),
                reads=[B["x1_h"]], writes=[xbB], dma=True)
            stt, sttB = (st0, B["st0"]) if par == 0 else (st1, B["st1"])
            ln_stats(xb, xbB, stt, sttB, EPS)
            pr.op("ve", lambda v, xb=xb, stt=stt: v.tensor_scalar(
                u5[:, :], xb[:, :], stt[2][:, 0:1], stt[4][:, 0:1], OP.subtract, OP.mult),
                reads=[xbB, sttB], writes=[B["p5u"]])
            for c in range(DC):
                tp = (ts * DC + c) % 2
                pr.op("te", lambda te, c=c, tp=tp: te.transpose(
                    ps_t5[tp][:, :], u5[:, c * 128:(c + 1) * 128], identf[:, :]),
                    reads=[B["p5u"], B["identf"]], writes=[B[f"p5pt{tp}"]])
                pr.op("sc", lambda s, c=c, ts=ts, tp=tp: s.activation(
                    u2h[c][:, ts * 128:(ts + 1) * 128], ps_t5[tp][:, :], AF.Copy),
                    reads=[B[f"p5pt{tp}"]], writes=[B[f"p5uh{c}"]])
                pr.op("ve", lambda v, c=c, ts=ts, tp=tp: v.tensor_tensor(
                    u2l[c][:, ts * 128:(ts + 1) * 128], ps_t5[tp][:, :],
                    u2h[c][:, ts * 128:(ts + 1) * 128], OP.subtract),
                    reads=[B[f"p5pt{tp}"], B[f"p5uh{c}"]], writes=[B[f"p5ul{c}"]])
        for sup in range(4):
            sp_ = sup % 2
            w1qS, w2qS = w1qS2[sp_], w2qS2[sp_]
            w1b_, w2b_ = w1b2[sp_], w2b2[sp_]
            for c in range(DC):
                pr.op("sy", lambda s, w1qS=w1qS, c=c, sup=sup: s.dma_start(
                    out=w1qS[c][:, :],
                    in_=w1qt_h[c * 128:(c + 1) * 128, sup * 1024:(sup + 1) * 1024]),
                    reads=[B["w1qt_h"]], writes=[w1b_[c]], dma=True)
                pr.op("sy", lambda s, w2qS=w2qS, c=c, sup=sup: s.dma_start(
                    out=w2qS[c][:, :],
                    in_=w2qt_h[sup * 1024 + c * 128:sup * 1024 + (c + 1) * 128, :]),
                    reads=[B["w2qt_h"]], writes=[w2b_[c]], dma=True)
            for ts in range(4):
                def emit_ffn2(fc, ts=ts, w2qS=w2qS, w2b_=w2b_):
                    hp = fc % 3
                    hb = [B[f"p5hh{hp}"], B[f"p5hl{hp}"]]
                    fns2 = []
                    for tsl in range(2):
                        for ha_ in (hh5[hp], hl5[hp]):
                            for n in range(2):
                                st_ = (fc == 0 and ha_ is hh5[hp])
                                sp_ = (fc == 7 and ha_ is hl5[hp])
                                fns2.append(lambda te, tsl=tsl, ha_=ha_, n=n, hp=hp, fc=fc, w2qS=w2qS, st_=st_, sp_=sp_:
                                            te.matmul(psO[tsl * 2 + n][:, :],
                                                      ha_[:, tsl * 128:(tsl + 1) * 128],
                                                      w2qS[fc][:, n * 512:(n + 1) * 512],
                                                      start=st_, stop=sp_))
                    pr.group("te", fns2, reads=hb + w2b_,
                             writes=[B["p5po0"], B["p5po1"], B["p5po2"], B["p5po3"]])

                for fc in range(10):
                    if fc < 8:
                        hp = fc % 3
                        hp2 = fc % 2
                        pH = B[f"p5ph{hp2}"]
                        fns = []
                        for c in range(DC):
                            for ua_ in (u2h, u2l):
                                st_ = (c == 0 and ua_ is u2h)
                                sp_ = (c == DC - 1 and ua_ is u2l)
                                fns.append(lambda te, c=c, ua_=ua_, fc=fc, ts=ts, hp=hp2, w1qS=w1qS, st_=st_, sp_=sp_:
                                           te.matmul(psH[hp][:, :],
                                                     w1qS[c][:, fc * 128:(fc + 1) * 128],
                                                     ua_[c][:, ts * 256:(ts + 1) * 256],
                                                     start=st_, stop=sp_))
                        pr.group("te", fns, reads=w1b_ + u2b, writes=[pH])
                        fc_g = sup * 8 + fc
                        pr.op("sc", lambda s, hp=hp, hp2=hp2, fc_g=fc_g: s.activation(
                            hf5[hp][:, :], psH[hp2][:, :], AF.Relu, bias=b1s[:, fc_g:fc_g + 1]),
                            reads=[pH, B["b1s"]], writes=[B[f"p5hf{hp}"]])
                        pr.op("sc", lambda s, hp=hp: s.activation(
                            hh5[hp][:, :], hf5[hp][:, :], AF.Copy),
                            reads=[B[f"p5hf{hp}"]], writes=[B[f"p5hh{hp}"]])
                        pr.op("ve", lambda v, hp=hp: v.tensor_tensor(
                            hl5[hp][:, :], hf5[hp][:, :], hh5[hp][:, :], OP.subtract),
                            reads=[B[f"p5hf{hp}"], B[f"p5hh{hp}"]], writes=[B[f"p5hl{hp}"]])
                    if fc >= 2 and fc < 10:
                        if fc < 10 and fc - 2 < 8:
                            emit_ffn2(fc - 2)
                for tsl in range(2):
                    oi = ts * 2 + tsl
                    for n in range(2):
                        if sup == 0:
                            pr.op("ve", lambda v, oi=oi, tsl=tsl, n=n: v.tensor_copy(
                                o2sb[oi][:, n * 512:(n + 1) * 512], psO[tsl * 2 + n][:, :]),
                                reads=[B[f"p5po{tsl * 2 + n}"]], writes=[B[f"p5o{oi}"]])
                        else:
                            pr.op("ve", lambda v, oi=oi, tsl=tsl, n=n: v.tensor_tensor(
                                o2sb[oi][:, n * 512:(n + 1) * 512],
                                o2sb[oi][:, n * 512:(n + 1) * 512],
                                psO[tsl * 2 + n][:, :], OP.add),
                                reads=[B[f"p5po{tsl * 2 + n}"]], writes=[B[f"p5o{oi}"]])
        for ti in range(8):
            t = M * 8 + ti
            ob, obB = o2sb[ti], B[f"p5o{ti}"]
            pr.op("ve", lambda v, ob=ob: v.tensor_tensor(ob[:, :], ob[:, :], b2b[:, :], OP.add),
                  reads=[B["b2b"]], writes=[obB])
            stt, sttB = (st0, B["st0"]) if ti % 2 == 0 else (st1, B["st1"])
            ln_stats(ob, obB, stt, sttB, EPS)
            par = ti % 2
            xb, xbB = x5[par], B[f"p5x{par}"]
            pr.op("sy", lambda s, xb=xb, t=t: s.dma_start(
                out=xb[:, :], in_=x1_h[t * 128:(t + 1) * 128, :]),
                reads=[B["x1_h"]], writes=[xbB], dma=True)
            pr.op("ve", lambda v, ob=ob, stt=stt: v.tensor_scalar(
                u5[:, :], ob[:, :], stt[2][:, 0:1], stt[4][:, 0:1], OP.subtract, OP.mult),
                reads=[obB, sttB], writes=[B["p5u"]])
            pr.op("ve", lambda v, xb=xb: v.tensor_tensor(u5[:, :], u5[:, :], xb[:, :], OP.add),
                  reads=[xbB, B["p5u"]], writes=[B["p5u"]])
            pr.op("gp", lambda g, t=t: g.dma_start(out=y[t * 128:(t + 1) * 128, :],
                                                   in_=u5[:, :]),
                  reads=[B["p5u"]], writes=[B["y"]], dma=True)

    pr._wait("gp", "gp", pr.cnt["gp"])
    pr.op("gp", lambda g: g.memset(gsc[0:1, 4:5], 0.0), writes=[B["gsc"]])

    pr.emit(es)
    return nc, es


_CACHE = {}


def kernel(**inputs):
    x_full = np.asarray(inputs["L0_attn_x"], np.float32)
    wq = np.ascontiguousarray(np.asarray(inputs["L0_attn_Wq"], np.float32).reshape(D, D))
    wk = np.ascontiguousarray(np.asarray(inputs["L0_attn_Wk"], np.float32).reshape(D, D))
    wv = np.ascontiguousarray(np.asarray(inputs["L0_attn_Wv"], np.float32).reshape(D, D))
    wo = np.ascontiguousarray(np.asarray(inputs["L0_attn_Wo"], np.float32).reshape(D, D))
    bqt = np.ascontiguousarray(np.asarray(inputs["L0_attn_bq"], np.float32).reshape(8, 128).T)
    bkt = np.ascontiguousarray(np.asarray(inputs["L0_attn_bk"], np.float32).reshape(8, 128).T)
    bv_flat = np.asarray(inputs["L0_attn_bv"], np.float32).reshape(D)
    bo = np.asarray(inputs["L0_attn_bo"], np.float32).reshape(D)
    g2 = np.asarray(inputs["L0_attn_gamma"], np.float32).reshape(D)
    be2 = np.asarray(inputs["L0_attn_beta"], np.float32).reshape(D)
    w1 = np.asarray(inputs["L0_ffn_W1"], np.float32)
    b1t = np.ascontiguousarray(np.asarray(inputs["L0_ffn_b1"], np.float32).reshape(32, 128).T)
    w2 = np.asarray(inputs["L0_ffn_W2"], np.float32)
    b2 = np.asarray(inputs["L0_ffn_b2"], np.float32).reshape(D)

    if "nc" not in _CACHE:
        _CACHE["nc"] = build()
    nc, _ = _CACHE["nc"]

    in_maps = []
    for c in range(8):
        b, h = c // 2, c % 2
        own = x_full[b, h * 2048:(h + 1) * 2048]
        oth = x_full[b, (1 - h) * 2048:(2 - h) * 2048]
        xr = np.ascontiguousarray(np.concatenate([own, oth], axis=0))
        in_maps.append({
            "x": xr, "wq": wq, "wk": wk, "wv": wv, "wo": wo,
            "bqt": bqt, "bkt": bkt, "bv_v": bv_flat, "bo_v": bo,
            "g2_v": g2, "be2_v": be2, "b2_v": b2, "b1t": b1t,
            "w1": w1, "w2": w2,
        })
    res = run_bass_kernel_spmd(nc, in_maps, core_ids=list(range(8)))
    out = np.empty((4, 4096, 1024), np.float32)
    for c in range(8):
        b, h = c // 2, c % 2
        out[b, h * 2048:(h + 1) * 2048] = res.results[c]["y"]
    return out
